# version 03
# baseline (speedup 1.0000x reference)
import numpy as np
from contextlib import ExitStack

import concourse.bass as bass
import concourse.mybir as mybir
from concourse.bass_utils import run_bass_kernel_spmd

F32 = mybir.dt.float32
BF16 = mybir.dt.bfloat16
AF = mybir.ActivationFunctionType
ALU = mybir.AluOpType
AX = mybir.AxisListType

B, T, D, DI, S = 4, 8192, 1024, 2048, 16
NC = 8
KD = D // 128
KI = DI // 128
SH = S // 2
LNEPS = 1e-5

TCP = 512
AGC = 2

ENG_SCAN = "vector"

ENGS = ("sync", "gpsimd", "scalar", "vector", "tensor")
GROUPS = [[0, 1], [2, 3], [4, 5], [6, 7]]


def _derive():
    global TOK, NCH, GA, NAG, NTG, DTU_COLS
    TOK = T // 2
    NCH = TOK // TCP
    GA = AGC * TCP
    NAG = NCH // AGC
    NTG = T // GA
    DTU_COLS = 2 * KI * GA + GA


_derive()


class Sem:
    __slots__ = ("h", "n")


class Rec:
    def __init__(self, nc, ctx):
        self.nc = nc
        self.ctx = ctx
        self.q = {e: [] for e in ENGS}
        self.auto = {}

    def sem(self, name):
        s = Sem()
        s.h = self.ctx.enter_context(self.nc.semaphore(name))
        s.n = 0
        return s

    def w(self, eng, sem, val=None):
        v = sem.n if val is None else val
        assert v <= sem.n, f"wait on future semaphore value {v} > {sem.n}"
        if v <= 0:
            return
        self.q[eng].append(lambda e, h=sem.h, v=v: e.wait_ge(h, v))

    def do(self, eng, meth, *args, inc=None, inck=1, **kw):
        s = self.auto.get(eng) if meth not in ("dma_start",
                                               "collective_compute") else None
        if s is not None and s.n > 0:
            self.w(eng, s, s.n)
        if s is not None and inc is None:
            inc = s
        if inc is not None:
            inc.n += inck

        def f(e, meth=meth, args=args, kw=kw, inc=inc, inck=inck):
            ins = getattr(e, meth)(*args, **kw)
            if inc is not None:
                ins.then_inc(inc.h, inck)

        self.q[eng].append(f)

    def dma(self, eng, out, in_, inc, nc_ok=False):
        kw = {"allow_slow_non_contiguous": True} if nc_ok else {}
        self.do(eng, "dma_start", inc=inc, inck=16, out=out, in_=in_, **kw)


class Arena:
    def __init__(self, handle, nelem, gran):
        self.h = handle
        self.ne = nelem
        self.off = 0
        self.gran = gran
        self.marks = []

    def alloc(self, n):
        nr = -(-n // self.gran) * self.gran
        assert self.off + nr <= self.ne, f"arena overflow {self.off}+{nr}>{self.ne}"
        ap = self.h[:, self.off: self.off + n]
        self.off += nr
        return ap

    def tiles(self, k, w):
        return [self.alloc(w) for _ in range(k)]

    def push(self):
        self.marks.append(self.off)

    def pop(self):
        self.off = self.marks.pop()


def build():
    nc = bass.Bass("TRN2", target_bir_lowering=False)
    ctx = ExitStack()
    W = TCP

    din = {}
    for name, shape in [
        ("xT", [D, 1 + TOK]),
        ("wrT", [D, D]), ("wkT", [D, D]), ("wvT", [D, D]),
        ("win", [D, DI]), ("wgate", [D, DI]), ("wdt", [DI, DI]),
        ("wb16", [DI, S]), ("wc16", [DI, S]), ("wout", [DI, D]),
        ("bdt", [128, KI]), ("mur", [128, KD]), ("muk", [128, KD]),
        ("nw", [128, KD]), ("nb", [128, KD]),
        ("alog", [128, KI * SH]), ("hmask", [128, 1]),
    ]:
        din[name] = nc.dram_tensor(name, shape, F32, kind="ExternalInput")
    outT = nc.dram_tensor("outT", [D, TOK], F32, kind="ExternalOutput")

    wr_b = nc.dram_tensor("wr_b", [128, 2 * KD * 512], BF16)
    wk_b = nc.dram_tensor("wk_b", [128, 2 * KD * 512], BF16)
    wv_b = nc.dram_tensor("wv_b", [128, 2 * KD * 512], BF16)
    win_b = nc.dram_tensor("win_b", [128, 4 * KD * 512], BF16)
    wgate_b = nc.dram_tensor("wgate_b", [128, 4 * KD * 512], BF16)
    wdt_b = nc.dram_tensor("wdt_b", [128, 8 * KD * 512], BF16)
    wout_b = nc.dram_tensor("wout_b", [128, 4 * KD * 512], BF16)

    r12_sp = nc.dram_tensor("r12_sp", [128, KD * TOK], F32)
    gate_sp = nc.dram_tensor("gate_sp", [128, KI * TOK], BF16)

    dtu_in = [nc.dram_tensor(f"dtu_in{g}", [128, DTU_COLS], BF16) for g in range(NAG)]
    dtu_out = [nc.dram_tensor(f"dtu_out{g}", [2, 128, DTU_COLS], BF16)
               for g in range(NAG)]
    y_in = [nc.dram_tensor(f"y_in{g}", [2, 128, KI * GA], BF16) for g in range(NAG)]
    y_out = [nc.dram_tensor(f"y_out{g}", [128, KI * GA], BF16) for g in range(NAG)]
    yc_in = nc.dram_tensor("yc_in", [128, KI], F32)
    yc_out = nc.dram_tensor("yc_out", [2, 128, KI], F32)
    rh_in = nc.dram_tensor("rh_in", [128, KD], F32)
    bsel_d = nc.dram_tensor("bsel_d", [2, GA], BF16)
    rh_out = nc.dram_tensor("rh_out", [2, 128, KD], F32)

    sb = lambda nm, shape, dt: ctx.enter_context(nc.sbuf_tensor(nm, shape, dt))
    AF32 = Arena(sb("af32", [128, 10200], F32), 10200, 4)
    ABF = Arena(sb("abf", [128, 27200], BF16), 27200, 16)
    sA = sb("sA", [128, KD * 512], BF16)
    sB = sb("sB", [128, KD * 512], BF16)
    wb_sb = sb("wb_sb", [128, KI * S], BF16)
    wc_sb = sb("wc_sb", [128, KI * S], BF16)
    bdt_sb = sb("bdt_sb", [128, KI], F32)
    mur_sb = sb("mur_sb", [128, KD], F32)
    muk_sb = sb("muk_sb", [128, KD], F32)
    nw_sb = sb("nw_sb", [128, KD], F32)
    nb_sb = sb("nb_sb", [128, KD], F32)
    A_sb = sb("A_sb", [128, KI * SH], F32)
    hmask_sb = sb("hmask_sb", [128, 1], F32)
    ceps = sb("ceps", [128, 1], F32)
    cone = sb("cone", [128, 1], F32)
    ones_cb = sb("ones_cb", [128, 1], BF16)
    ones_r = sb("ones_r", [1, 128], F32)
    carry = sb("carry", [128, KI * SH], F32)
    row_mean = sb("row_mean", [1, 512], F32)
    row_msq = sb("row_msq", [1, 512], F32)
    row_tmp = sb("row_tmp", [1, 512], F32)
    row_rstd = sb("row_rstd", [1, 512], F32)
    row_sd = sb("row_sd", [1, 512], F32)
    xn0_halo = sb("xn0_halo", [128, KD], BF16)
    xn3_halo = sb("xn3_halo", [128, KD], F32)
    y4095 = sb("y4095", [128, KI], F32)
    dts = [sb(f"dts{i}", [128, GA], BF16) for i in range(2)]
    uts = [sb(f"uts{i}", [128, GA], BF16) for i in range(2)]
    Bbc = [sb(f"Bbc{i}", [128, GA], BF16) for i in range(SH)]
    Cbc = [sb(f"Cbc{i}", [128, GA], BF16) for i in range(SH)]
    dAr = [sb(f"dAr{i}", [128, GA], F32) for i in range(2)]
    hbr = [sb(f"hbr{i}", [128, GA], BF16) for i in range(2)]
    hca = sb("hca", [128, GA], BF16)
    hcb = sb("hcb", [128, GA], BF16)
    hp = [sb(f"hp{i}", [128, GA], BF16) for i in range(2)]
    ysc = [sb(f"ysc{i}", [128, GA], BF16) for i in range(2)]
    zero5 = sb("zero5", [128, 512], BF16)
    hm0_sb = sb("hm0_sb", [128, 1], F32)
    brow = [sb(f"brow{i}", [1, GA], BF16) for i in range(4)]
    bwork = [sb(f"bwork{i}", [1, GA], BF16) for i in range(2)]

    PS = [ctx.enter_context(nc.psum_tensor(f"ps{i}", [128, 512], F32))
          for i in range(8)]

    R = Rec(nc, ctx)
    st = R.sem("st")
    sv = R.sem("sv")
    pv = R.sem("pv")
    tt = R.sem("tt")
    sL = R.sem("sL")
    scf = [R.sem(f"scf{i}") for i in range(2)]
    scs = [R.sem(f"scs{i}") for i in range(2)]
    sx = [R.sem(f"sx{i}") for i in range(2)]
    sw = [R.sem(f"sw{i}") for i in range(2)]
    sr12 = R.sem("sr12")
    sg = [R.sem(f"sg{i}") for i in range(2)]
    sd = [R.sem(f"sd{i}") for i in range(2)]
    sbcst = R.sem("sbcst")
    sbc = R.sem("sbc")
    sdu = [R.sem(f"sdu{i}") for i in range(2)]
    ssy = [R.sem(f"ssy{i}") for i in range(NTG)]
    syc = R.sem("syc")
    srh = R.sem("srh")
    sey = [R.sem(f"sey{i}") for i in range(2)]
    ser = [R.sem(f"ser{i}") for i in range(2)]
    sout = [R.sem(f"sout{i}") for i in range(2)]
    ccs = [R.sem(f"cc{i}") for i in range(2 * NAG + 3)]
    cc_n = [0]

    def next_cc():
        s = ccs[cc_n[0]]
        cc_n[0] += 1
        return s

    R.auto = {"vector": sv, "scalar": st, "gpsimd": pv}
    bank_free = [(sv, 0)] * 8

    def mm_group(bank_ap, bank_idx, pairs, guard=True):
        if guard:
            s0, v0 = bank_free[bank_idx]
            R.w("tensor", s0, v0)
        for i, (lh, rh) in enumerate(pairs):
            R.do("tensor", "matmul", bank_ap, lh, rh,
                 start=(i == 0), stop=(i == len(pairs) - 1),
                 inc=tt if i == len(pairs) - 1 else None)

    def ln_stats(cast_fn, w, kd=KD):
        s0, v0 = bank_free[2]
        ttv = []
        for k in range(kd):
            if k >= 2:
                R.w("scalar", tt, ttv[k - 2])
            vb, qb = cast_fn(k)
            if k == 0:
                R.w("tensor", s0, v0)
            R.w("tensor", st, st.n)
            R.do("tensor", "matmul", PS[2][0:1, :w], ones_cb[:, :], vb,
                 start=(k == 0), stop=(k == kd - 1))
            R.do("tensor", "matmul", PS[2][32:33, :w], ones_cb[:, :], qb,
                 start=(k == 0), stop=(k == kd - 1), tile_position=(0, 32),
                 inc=tt)
            ttv.append(tt.n)
        R.w("vector", tt, tt.n)
        inv = 1.0 / (kd * 128)
        R.do("vector", "tensor_scalar_mul", row_mean[:, :w], PS[2][0:1, :w], inv)
        R.do("vector", "tensor_scalar_mul", row_msq[:, :w], PS[2][32:33, :w], inv)
        R.do("vector", "tensor_mul", row_tmp[:, :w], row_mean[:, :w],
             row_mean[:, :w])
        R.do("vector", "tensor_sub", row_tmp[:, :w], row_msq[:, :w],
             row_tmp[:, :w], inc=sv)
        bank_free[2] = (sv, sv.n)
        R.w("scalar", sv, sv.n)
        R.do("scalar", "activation", row_sd[:, :w], row_tmp[:, :w], AF.Sqrt,
             bias=ceps[0:1, 0:1], inc=st)
        R.w("vector", st, st.n)
        R.do("vector", "reciprocal", row_rstd[:, :w], row_sd[:, :w], inc=sv)
        for bank, row in ((3, row_mean), (4, row_rstd)):
            s0, v0 = bank_free[bank]
            R.w("tensor", s0, v0)
            R.w("tensor", sv, sv.n)
            R.do("tensor", "matmul", PS[bank][:, :w], ones_r[:, :], row[:, :w],
                 start=True, stop=True, inc=tt)
        R.w("vector", tt, tt.n)

    def mini_ln_cols(vals_f32, vb, qb, kd):
        R.do("scalar", "activation", vb[:, :], vals_f32[:, :], AF.Copy, inc=st)
        R.do("scalar", "activation", qb[:, :], vals_f32[:, :], AF.Square, inc=st)
        s0b, v0b = bank_free[5]
        R.w("tensor", s0b, v0b)
        R.w("tensor", st, st.n)
        R.do("tensor", "matmul", PS[5][0:1, 0:kd], ones_cb[:, :], vb[:, :],
             start=True, stop=True)
        R.do("tensor", "matmul", PS[5][32:33, 0:kd], ones_cb[:, :], qb[:, :],
             start=True, stop=True, tile_position=(0, 32), inc=tt)
        R.w("vector", tt, tt.n)
        R.do("vector", "tensor_reduce", row_tmp[0:1, 8:9], PS[5][0:1, 0:kd],
             AX.X, ALU.add)
        R.do("vector", "tensor_reduce", row_tmp[0:1, 9:10],
             PS[5][32:33, 0:kd], AX.X, ALU.add)
        R.do("vector", "tensor_scalar_mul", row_tmp[0:1, 8:9],
             row_tmp[0:1, 8:9], 1.0 / D)
        R.do("vector", "tensor_scalar_mul", row_tmp[0:1, 9:10],
             row_tmp[0:1, 9:10], 1.0 / D)
        R.do("vector", "tensor_mul", row_tmp[0:1, 10:11], row_tmp[0:1, 8:9],
             row_tmp[0:1, 8:9])
        R.do("vector", "tensor_sub", row_tmp[0:1, 9:10], row_tmp[0:1, 9:10],
             row_tmp[0:1, 10:11], inc=sv)
        bank_free[5] = (sv, sv.n)
        R.w("scalar", sv, sv.n)
        R.do("scalar", "activation", row_tmp[0:1, 12:13], row_tmp[0:1, 9:10],
             AF.Sqrt, bias=ceps[0:1, 0:1], inc=st)
        R.w("vector", st, st.n)
        R.do("vector", "reciprocal", row_tmp[0:1, 11:12], row_tmp[0:1, 12:13],
             inc=sv)
        s0b, v0b = bank_free[5]
        R.w("tensor", s0b, v0b)
        R.w("tensor", sv, sv.n)
        R.do("tensor", "matmul", PS[5][:, 0:1], ones_r[:, :],
             row_tmp[0:1, 8:9], start=True, stop=True)
        R.do("tensor", "matmul", PS[5][:, 1:2], ones_r[:, :],
             row_tmp[0:1, 11:12], start=True, stop=True, inc=tt)
        R.w("vector", tt, tt.n)

    for name, tile in [
        ("bdt", bdt_sb), ("mur", mur_sb), ("muk", muk_sb), ("nw", nw_sb),
        ("nb", nb_sb), ("hmask", hmask_sb),
    ]:
        R.dma("sync", tile[:, :], din[name][:, :], sL)
    R.do("vector", "memset", ceps[:, :], LNEPS)
    R.do("vector", "memset", cone[:, :], 1.0)
    R.do("vector", "memset", ones_cb[:, :], 1.0)
    R.do("vector", "memset", ones_r[:, :], 1.0)
    R.do("vector", "memset", carry[:, :], 0.0)
    R.do("vector", "memset", zero5[:, :], 0.0)
    R.w("vector", sL)
    R.do("vector", "tensor_scalar", hm0_sb[:, :], hmask_sb[:, :], -1.0, 1.0,
         ALU.mult, ALU.add)
    R.do("vector", "memset", xn3_halo[:, :], 0.0, inc=sv)
    R.w("sync", sv, sv.n)
    for _g in range(NAG):
        for _c2 in range(AGC):
            _bco = 2 * KI * GA + _c2 * TCP
            R.dma("sync", dtu_in[_g][32:128, _bco:_bco + TCP],
                  zero5[0:96, :TCP], sbcst)
    AF32.push()
    atmp = AF32.alloc(KI * SH)
    R.dma("sync", atmp[:, :], din["alog"][:, :], sL)
    R.w("scalar", sL)
    R.do("scalar", "activation", A_sb[:, :], atmp[:, :], AF.Exp, inc=st)
    R.w("vector", st, st.n)
    R.do("vector", "tensor_scalar", A_sb[:, :], A_sb[:, :], 1e-6, 1.0, ALU.max,
         ALU.min)
    R.do("vector", "tensor_scalar_mul", A_sb[:, :], A_sb[:, :], -1.0, inc=sv)
    AF32.pop()

    R.w("sync", st, st.n)
    AF32.push()
    ABF.push()
    cast_f = AF32.tiles(2, 512)
    cast_b = ABF.tiles(2, 512)
    cast_sv = []
    cast_n = [0]

    def cast_tile(src_ap, dst_ap):
        i = cast_n[0] % 2
        if cast_n[0] >= 2:
            R.w("sync", sv, cast_sv[cast_n[0] - 2])
        R.dma("sync", cast_f[i][:, :512], src_ap, scf[i])
        R.w("vector", scf[i])
        R.w("vector", scs[i])
        R.do("vector", "tensor_copy", cast_b[i][:, :512], cast_f[i][:, :512],
             inc=sv)
        cast_sv.append(sv.n)
        R.w("sync", sv, sv.n)
        R.dma("sync", dst_ap, cast_b[i][:, :512], scs[i])
        cast_n[0] += 1

    def cast_w(src, dst, n_k, n_pieces):
        for p in range(n_pieces):
            for k in range(n_k):
                cast_tile(src[k * 128:(k + 1) * 128, p * 512:(p + 1) * 512],
                          dst[:, (p * n_k + k) * 512:(p * n_k + k + 1) * 512])

    cast_w(din["wrT"], wr_b, KD, 2)
    cast_w(din["wkT"], wk_b, KD, 2)
    cast_w(din["wvT"], wv_b, KD, 2)
    cast_w(din["win"], win_b, KD, 4)
    cast_w(din["wgate"], wgate_b, KD, 4)
    cast_w(din["wout"], wout_b, KI, 2)
    for kh in range(2):
        for mq in range(4):
            for k in range(KD):
                p = kh * 4 + mq
                cast_tile(
                    din["wdt"][(kh * KD + k) * 128:(kh * KD + k + 1) * 128,
                               mq * 512:(mq + 1) * 512],
                    wdt_b[:, (p * KD + k) * 512:(p * KD + k + 1) * 512])
    for src, dst in [(din["wb16"], wb_sb), (din["wc16"], wc_sb)]:
        for k in range(KI):
            i = cast_n[0] % 2
            R.w("sync", sv, cast_sv[cast_n[0] - 2])
            R.dma("sync", cast_f[i][:, :S], src[k * 128:(k + 1) * 128, :],
                  scf[i])
            R.w("vector", scf[i])
            R.do("vector", "tensor_copy", dst[:, k * S:(k + 1) * S],
                 cast_f[i][:, :S], inc=sv)
            cast_sv.append(sv.n)
            cast_n[0] += 1
    wbc_ready = sv.n
    AF32.pop()
    ABF.pop()
    R.w("sync", sv, wbc_ready)

    AF32.push()
    ABF.push()
    xf = AF32.tiles(2, W)
    tmpf = AF32.tiles(2, W)
    y1f = AF32.tiles(KD, W)
    dtt = AF32.tiles(2, W)
    rvf = AF32.tiles(2, W)
    hx = AF32.alloc(KD)
    xb = ABF.tiles(KD, W)
    xq = ABF.tiles(2, W)
    xn0b = [ABF.alloc(1 + W) for _ in range(KD)]
    blockA = ABF.tiles(KD, W)
    blockB = ABF.tiles(KD, W)
    rbt = ABF.tiles(KD, W)
    dtf = ABF.tiles(2, W)
    ub = ABF.tiles(2, W)
    gsil = ABF.tiles(2, W)
    mtb = ABF.tiles(2, W)
    bcst = ABF.alloc(W)
    hxb = ABF.alloc(KD)
    hxq = ABF.alloc(KD)
    xrb, kab = blockA, blockA
    xkb = blockB
    zb = blockA + blockB
    rb, t1b = rbt, rbt

    pstate = {"i": 0, "cons": []}

    def load_piece(src, p, n_k=KD):
        i = pstate["i"]
        buf = sA if i % 2 == 0 else sB
        if i >= 2:
            s0, v0 = pstate["cons"][i - 2]
            R.w("sync", s0, v0)
        R.dma("sync", buf[:, :n_k * 512],
              src[:, p * n_k * 512:(p + 1) * n_k * 512], sw[i % 2])
        pstate["cons"].append(None)
        pstate["i"] += 1
        return buf, sw[i % 2], i

    def piece_done(i):
        pstate["cons"][i] = (tt, tt.n)

    pre_ms = []

    def pre_chunk(j):
        c0 = j * W
        g = j // AGC
        c2 = j % AGC
        prev = pre_ms[-1] if pre_ms else None
        cast_st = []

        def cast_x(k):
            if k < 2:
                if prev is not None:
                    R.w("sync", sv, prev["sv"])
                    R.w("sync", st, prev["st"])
            else:
                R.w("sync", st, cast_st[k - 2])
            R.dma("sync", xf[k % 2][:, :],
                  din["xT"][k * 128:(k + 1) * 128, 1 + c0:1 + c0 + W],
                  sx[k % 2])
            R.w("scalar", sx[k % 2])
            if j == 0 and k == 0:
                R.w("scalar", sL)
                R.w("vector", sL)
                R.w("tensor", sL)
            R.do("scalar", "activation", xb[k][:, :], xf[k % 2][:, :], AF.Copy,
                 inc=st)
            R.do("scalar", "activation", xq[k % 2][:, :], xf[k % 2][:, :],
                 AF.Square, inc=st)
            cast_st.append(st.n)
            return xb[k][:, :], xq[k % 2][:, :]

        ln_stats(cast_x, W)
        if j == 0:
            for k in range(KD):
                R.dma("sync", hx[:, k:k + 1],
                      din["xT"][k * 128:(k + 1) * 128, 0:1], sL, nc_ok=True)
            R.w("scalar", sL)
            mini_ln_cols(hx, hxb, hxq, KD)
            R.do("vector", "tensor_single_scalar", hx[:, :], hx[:, :],
                 PS[5][:, 0:1], ALU.subtract)
            R.do("vector", "tensor_single_scalar", xn0_halo[:, :], hx[:, :],
                 PS[5][:, 1:2], ALU.mult, inc=sv)
            bank_free[5] = (sv, sv.n)
            for k in range(KD):
                R.do("vector", "tensor_copy", xn0b[k][:, 0:1],
                     xn0_halo[:, k:k + 1])
        else:
            for k in range(KD):
                R.do("vector", "tensor_copy", xn0b[k][:, 0:1],
                     xn0b[k][:, W:W + 1])
        for k in range(KD):
            R.do("vector", "tensor_sub", tmpf[k % 2][:, :], xb[k][:, :],
                 PS[3][:, :W])
            R.do("vector", "tensor_mul", xn0b[k][:, 1:1 + W], tmpf[k % 2][:, :],
                 PS[4][:, :W], inc=sv if k == KD - 1 else None)
        bank_free[3] = (sv, sv.n)
        bank_free[4] = (sv, sv.n)
        for k in range(KD):
            cur = xn0b[k][:, 1:1 + W]
            prv = xn0b[k][:, 0:W]
            R.do("vector", "tensor_sub", mtb[k % 2][:, :], cur, prv)
            R.do("vector", "scalar_tensor_tensor", xrb[k][:, :], mtb[k % 2][:, :],
                 mur_sb[:, k:k + 1], prv, ALU.mult, ALU.add)
            R.do("vector", "scalar_tensor_tensor", xkb[k][:, :], mtb[k % 2][:, :],
                 muk_sb[:, k:k + 1], prv, ALU.mult, ALU.add,
                 inc=sv if k == KD - 1 else None)
        mixes = sv.n
        for mh in range(2):
            buf, wld, pidx = load_piece(wr_b, mh)
            for ml in range(4):
                mo = mh * 4 + ml
                bk = (0, 1, 6)[mo % 3]
                R.w("tensor", sv, mixes)
                R.w("tensor", wld)
                mm_group(PS[bk][:, :W], bk,
                         [(buf[:, k * 512 + ml * 128:k * 512 + (ml + 1) * 128],
                           xrb[k][:, :]) for k in range(KD)])
                R.w("scalar", tt, tt.n)
                R.do("scalar", "activation", rb[mo][:, :], PS[bk][:, :W],
                     AF.Sigmoid, inc=st)
                bank_free[bk] = (st, st.n)
            piece_done(pidx)
        for mh in range(2):
            buf, wld, pidx = load_piece(wk_b, mh)
            for ml in range(4):
                mo = mh * 4 + ml
                bk = (0, 1, 6)[mo % 3]
                R.w("tensor", wld)
                mm_group(PS[bk][:, :W], bk,
                         [(buf[:, k * 512 + ml * 128:k * 512 + (ml + 1) * 128],
                           xkb[k][:, :]) for k in range(KD)])
                R.w("scalar", tt, tt.n)
                R.do("scalar", "activation", xq[0][:, :], PS[bk][:, :W],
                     AF.Relu, inc=st)
                bank_free[bk] = (st, st.n)
                R.do("scalar", "activation", kab[mo][:, :], xq[0][:, :],
                     AF.Square, inc=st)
            piece_done(pidx)
        ka_done = st.n
        xld = []
        for mh in range(2):
            buf, wld, pidx = load_piece(wv_b, mh)
            for ml in range(4):
                mo = mh * 4 + ml
                bk = (0, 1, 6)[mo % 3]
                R.w("tensor", st, ka_done)
                R.w("tensor", wld)
                mm_group(PS[bk][:, :W], bk,
                         [(buf[:, k * 512 + ml * 128:k * 512 + (ml + 1) * 128],
                           kab[k][:, :]) for k in range(KD)])
                if mo >= 2:
                    R.w("sync", sv, xld[mo - 2])
                else:
                    R.w("sync", st, cast_st[-1])
                R.dma("sync", xf[mo % 2][:, :],
                      din["xT"][mo * 128:(mo + 1) * 128, 1 + c0:1 + c0 + W],
                      sx[mo % 2])
                R.w("vector", tt, tt.n)
                R.w("vector", sx[mo % 2])
                if j > 0 and mo == 0:
                    R.w("vector", sr12)
                R.do("vector", "tensor_mul", rvf[mo % 2][:, :], rb[mo][:, :],
                     PS[bk][:, :W])
                R.do("vector", "scalar_tensor_tensor", y1f[mo][:, :],
                     rvf[mo % 2][:, :], 0.5, xf[mo % 2][:, :], ALU.mult,
                     ALU.add, inc=sv)
                bank_free[bk] = (sv, sv.n)
                xld.append(sv.n)
            piece_done(pidx)
        y1_done = sv.n

        def cast_y1(k):
            R.w("scalar", sv, y1_done)
            R.do("scalar", "activation", xb[k][:, :], y1f[k][:, :], AF.Copy,
                 inc=st)
            R.do("scalar", "activation", xq[k % 2][:, :], y1f[k][:, :],
                 AF.Square, inc=st)
            return xb[k][:, :], xq[k % 2][:, :]

        ln_stats(cast_y1, W)
        for k in range(KD):
            R.do("vector", "tensor_sub", tmpf[k % 2][:, :], y1f[k][:, :],
                 PS[3][:, :W])
            R.do("vector", "tensor_mul", t1b[k][:, :], tmpf[k % 2][:, :],
                 PS[4][:, :W])
            R.do("vector", "tensor_add", y1f[k][:, :], y1f[k][:, :],
                 t1b[k][:, :], inc=sv)
        bank_free[3] = (sv, sv.n)
        bank_free[4] = (sv, sv.n)
        r12_done = sv.n
        R.w("sync", sv, r12_done)
        for k in range(KD):
            R.dma("sync", r12_sp[:, k * TOK + c0:k * TOK + c0 + W],
                  y1f[k][:, :], sr12)

        def cast_t1(k):
            R.w("scalar", sv, r12_done)
            R.do("scalar", "activation", xq[k % 2][:, :], t1b[k][:, :],
                 AF.Square, inc=st)
            return t1b[k][:, :], xq[k % 2][:, :]

        ln_stats(cast_t1, W)
        for k in range(KD):
            R.do("vector", "tensor_sub", tmpf[k % 2][:, :], t1b[k][:, :],
                 PS[3][:, :W])
            R.do("vector", "tensor_mul", tmpf[k % 2][:, :], tmpf[k % 2][:, :],
                 PS[4][:, :W])
            R.do("vector", "tensor_scalar", xb[k][:, :], tmpf[k % 2][:, :],
                 nw_sb[:, k:k + 1], nb_sb[:, k:k + 1], ALU.mult, ALU.add,
                 inc=sv if k == KD - 1 else None)
        xn_ready = sv.n
        bank_free[3] = (sv, sv.n)
        bank_free[4] = (sv, sv.n)
        for p in range(4):
            buf, wld, pidx = load_piece(win_b, p)
            for ml in range(4):
                mo = p * 4 + ml
                bk = (0, 1, 6)[mo % 3]
                R.w("tensor", sv, xn_ready)
                R.w("tensor", wld)
                mm_group(PS[bk][:, :W], bk,
                         [(buf[:, k * 512 + ml * 128:k * 512 + (ml + 1) * 128],
                           xb[k][:, :]) for k in range(KD)])
                R.w("scalar", tt, tt.n)
                R.do("scalar", "activation", zb[mo][:, :], PS[bk][:, :W],
                     AF.Copy, inc=st)
                bank_free[bk] = (st, st.n)
            piece_done(pidx)
        for p in range(4):
            buf, wld, pidx = load_piece(wgate_b, p)
            for ml in range(4):
                mo = p * 4 + ml
                bk = (0, 1, 6)[mo % 3]
                R.w("tensor", wld)
                mm_group(PS[bk][:, :W], bk,
                         [(buf[:, k * 512 + ml * 128:k * 512 + (ml + 1) * 128],
                           xb[k][:, :]) for k in range(KD)])
                R.w("scalar", tt, tt.n)
                R.w("scalar", sg[mo % 2])
                R.do("scalar", "activation", gsil[mo % 2][:, :], PS[bk][:, :W],
                     AF.Silu, inc=st)
                bank_free[bk] = (st, st.n)
                R.w("sync", st, st.n)
                R.dma("sync", gate_sp[:, mo * TOK + c0:mo * TOK + c0 + W],
                      gsil[mo % 2][:, :], sg[mo % 2])
            piece_done(pidx)
        for mq in range(4):
            b0 = load_piece(wdt_b, 0 * 4 + mq)
            b1 = load_piece(wdt_b, 1 * 4 + mq)
            mq_tt = []
            for ml in range(4):
                mo = mq * 4 + ml
                bk = (0, 1, 6, 7)[ml]
                s0b, v0b = bank_free[bk]
                R.w("tensor", s0b, v0b)
                R.w("tensor", b0[1])
                R.w("tensor", b1[1])
                pairs = []
                for kk in range(KI):
                    buf = (b0 if kk < KD else b1)[0]
                    k = kk % KD
                    pairs.append(
                        (buf[:, k * 512 + ml * 128:k * 512 + (ml + 1) * 128],
                         zb[kk][:, :]))
                mm_group(PS[bk][:, :W], bk, pairs, guard=False)
                mq_tt.append(tt.n)
            piece_done(b0[2])
            piece_done(b1[2])
            for ml in range(4):
                mo = mq * 4 + ml
                bk = (0, 1, 6, 7)[ml]
                R.w("scalar", tt, mq_tt[ml])
                R.w("scalar", sv, sv.n)
                R.do("scalar", "activation", dtt[ml % 2][:, :], PS[bk][:, :W],
                     AF.Exp, bias=bdt_sb[:, mo:mo + 1], inc=st)
                R.do("scalar", "activation", dtt[ml % 2][:, :],
                     dtt[ml % 2][:, :], AF.Ln, bias=cone[:, 0:1], inc=st)
                bank_free[bk] = (st, st.n)
                R.w("vector", st, st.n)
                R.w("vector", sd[ml % 2])
                R.do("vector", "tensor_scalar", dtf[ml % 2][:, :],
                     dtt[ml % 2][:, :], 1e-3, 1.0, ALU.max, ALU.min)
                R.do("vector", "tensor_mul", ub[ml % 2][:, :], dtf[ml % 2][:, :],
                     zb[mo][:, :], inc=sv)
                R.w("sync", sv, sv.n)
                base = mo * GA + c2 * W
                R.dma("sync", dtu_in[g][:, base:base + W], dtf[ml % 2][:, :],
                      sd[ml % 2])
                R.dma("sync", dtu_in[g][:, KI * GA + base:KI * GA + base + W],
                      ub[ml % 2][:, :], sd[ml % 2])
        s0b, v0b = bank_free[5]
        R.w("tensor", s0b, v0b)
        R.w("tensor", sv, wbc_ready)
        for kk in range(KI):
            R.do("tensor", "matmul", PS[5][0:16, :W],
                 wb_sb[:, kk * S:(kk + 1) * S], zb[kk][:, :],
                 start=(kk == 0), stop=(kk == KI - 1))
        for kk in range(KI):
            R.do("tensor", "matmul", PS[5][32:48, :W],
                 wc_sb[:, kk * S:(kk + 1) * S], zb[kk][:, :],
                 start=(kk == 0), stop=(kk == KI - 1), tile_position=(0, 32),
                 inc=tt if kk == KI - 1 else None)
        R.w("vector", tt, tt.n)
        R.w("vector", sbcst)
        R.do("vector", "tensor_copy", bcst[0:16, :W], PS[5][0:16, :W])
        R.do("vector", "tensor_copy", bcst[32:48, :W], PS[5][32:48, :W], inc=sv)
        bank_free[5] = (sv, sv.n)
        R.w("sync", sv, sv.n)
        bco = 2 * KI * GA + c2 * W
        R.dma("sync", dtu_in[g][0:16, bco:bco + W], bcst[0:16, :W], sbcst)
        R.dma("sync", dtu_in[g][16:32, bco:bco + W], bcst[32:48, :W], sbcst)
        pre_ms.append({"sv": sv.n, "st": st.n, "tt": tt.n})

    scan_ms = []

    def scan_tg(tg, ag_sem):
        half = tg // NAG
        gg = tg % NAG
        eS = ENG_SCAN
        semS = pv if eS == "gpsimd" else sv
        src = dtu_out[gg]
        R.w("sync", ag_sem)
        if scan_ms:
            R.w("sync", sv, scan_ms[-1]["sv"])
            if eS == "gpsimd":
                R.w("sync", pv, scan_ms[-1]["pv"])
        bco = 2 * KI * GA
        for si in range(SH):
            R.w("sync", sv, sv.n)
            R.dma("sync", brow[0][:, :],
                  src[half, si:si + 1, bco:bco + GA], sbc)
            R.dma("sync", brow[1][:, :],
                  src[half, SH + si:SH + si + 1, bco:bco + GA], sbc)
            R.dma("sync", brow[2][:, :],
                  src[half, 16 + si:16 + si + 1, bco:bco + GA], sbc)
            R.dma("sync", brow[3][:, :],
                  src[half, 16 + SH + si:16 + SH + si + 1, bco:bco + GA], sbc)
            R.w("vector", sbc)
            R.do("vector", "tensor_single_scalar", bwork[0][:, :], brow[0][:, :],
                 hm0_sb[0:1, 0:1], ALU.mult)
            R.do("vector", "scalar_tensor_tensor", bwork[0][:, :], brow[1][:, :],
                 hmask_sb[0:1, 0:1], bwork[0][:, :], ALU.mult, ALU.add)
            R.do("vector", "tensor_single_scalar", bwork[1][:, :], brow[2][:, :],
                 hm0_sb[0:1, 0:1], ALU.mult)
            R.do("vector", "scalar_tensor_tensor", bwork[1][:, :], brow[3][:, :],
                 hmask_sb[0:1, 0:1], bwork[1][:, :], ALU.mult, ALU.add, inc=sv)
            R.w("sync", sv, sv.n)
            R.dma("sync", bsel_d[0:1, :], bwork[0][:, :], sbc)
            R.dma("sync", bsel_d[1:2, :], bwork[1][:, :], sbc)
            R.w("sync", sbc)
            R.dma("sync", Bbc[si][:, :],
                  bsel_d[0:1, :].to_broadcast((128, GA)), sbc)
            R.dma("sync", Cbc[si][:, :],
                  bsel_d[1:2, :].to_broadcast((128, GA)), sbc)
        tile_ms = []
        for k in range(KI):
            kb = k % 2
            if k >= 2:
                m = tile_ms[k - 2]
                R.w("sync", st, m["st"])
                R.w("sync", sv, m["sv"])
                if eS == "gpsimd":
                    R.w("sync", pv, m["pv"])
            R.dma("sync", dts[kb][:, :],
                  src[half, :, k * GA:(k + 1) * GA], sdu[kb])
            R.dma("sync", uts[kb][:, :],
                  src[half, :, KI * GA + k * GA:KI * GA + (k + 1) * GA],
                  sdu[kb])
            scan_hist = []
            hc_hist = []
            for si in range(SH):
                col = k * SH + si
                ab = si % 2
                R.w("scalar", sdu[kb])
                if si >= 2:
                    R.w("scalar", semS, scan_hist[si - 2])
                R.do("scalar", "activation", dAr[ab][:, :], dts[kb][:, :],
                     AF.Exp, scale=A_sb[:, col:col + 1], inc=st)
                R.w("vector", sdu[kb])
                R.w("vector", sbc)
                if si >= 2 and eS == "gpsimd":
                    R.w("vector", semS, scan_hist[si - 2])
                R.do("vector", "tensor_mul", (hca if ab == 0 else hcb)[:, :],
                     uts[kb][:, :], Bbc[si][:, :], inc=sv)
                bur_v = sv.n
                R.w(eS, st, st.n)
                if eS == "gpsimd":
                    R.w(eS, sv, bur_v)
                    if si >= 2:
                        R.w(eS, sv, hc_hist[si - 2])
                R.do(eS, "tensor_tensor_scan", hbr[ab][:, :], dAr[ab][:, :],
                     (hca if ab == 0 else hcb)[:, :], carry[:, col:col + 1],
                     ALU.mult, ALU.add, inc=semS)
                scan_hist.append(semS.n)
                R.do(eS, "tensor_copy", carry[:, col:col + 1],
                     hbr[ab][:, GA - 1:GA], inc=semS)
                if eS == "gpsimd":
                    R.w("vector", semS, scan_hist[-1])
                R.do("vector", "tensor_mul", (hca if ab == 0 else hcb)[:, :],
                     hbr[ab][:, :], Cbc[si][:, :], inc=sv)
                hc_hist.append(sv.n)
                if si % 2 == 1:
                    if si == 1:
                        R.do("vector", "tensor_add", hp[0][:, :], hca[:, :],
                             hcb[:, :], inc=sv)
                    else:
                        R.do("vector", "tensor_add", hp[1][:, :], hca[:, :],
                             hcb[:, :])
                        if si < SH - 1:
                            R.do("vector", "tensor_add", hp[0][:, :],
                                 hp[0][:, :], hp[1][:, :], inc=sv)
                        else:
                            if k >= 2:
                                R.w("vector", ssy[tg])
                            elif tg >= 1:
                                R.w("vector", ssy[tg - 1])
                            R.do("vector", "tensor_add", ysc[k % 2][:, :],
                                 hp[0][:, :], hp[1][:, :], inc=sv)
            if tg == NAG - 1:
                R.do("vector", "tensor_copy", y4095[:, k:k + 1],
                     ysc[k % 2][:, GA - 1:GA], inc=sv)
            R.w("sync", sv, sv.n)
            R.dma("sync", y_in[gg][half, :, k * GA:(k + 1) * GA],
                  ysc[k % 2][:, :], ssy[tg])
            tile_ms.append({"st": st.n, "sv": sv.n, "pv": pv.n})
        scan_ms.append({"sv": sv.n, "pv": pv.n})

    ag_sems = {}
    yc_sem = [None]
    emitted = [0]

    R.w("sync", scs[0])
    R.w("sync", scs[1])

    def emit_ag(g):
        R.w("gpsimd", sd[0])
        R.w("gpsimd", sd[1])
        R.w("gpsimd", sbcst)
        s = next_cc()
        R.do("gpsimd", "collective_compute", "AllGather", ALU.bypass,
             GROUPS, [dtu_in[g].ap().opt()], [dtu_out[g].ap().opt()], inc=s)
        ag_sems[g] = s

    def emit_scan_next():
        tg = emitted[0]
        scan_tg(tg, ag_sems[tg % NAG])
        emitted[0] += 1
        if emitted[0] == NAG:
            R.w("sync", sv, scan_ms[-1]["sv"])
            R.dma("sync", yc_in[:, :], y4095[:, :], syc)
            R.w("gpsimd", syc)
            s = next_cc()
            R.do("gpsimd", "collective_compute", "AllGather", ALU.bypass,
                 GROUPS, [yc_in.ap().opt()], [yc_out.ap().opt()], inc=s)
            yc_sem[0] = s

    for j in range(NCH):
        pre_chunk(j)
        if j % AGC == AGC - 1:
            emit_ag(j // AGC)
            if j // AGC >= 1:
                emit_scan_next()
    while emitted[0] < NTG:
        emit_scan_next()

    rs_sems = {}
    for g in range(NAG):
        R.w("gpsimd", ssy[g])
        R.w("gpsimd", ssy[NAG + g])
        s = next_cc()
        R.do("gpsimd", "collective_compute", "ReduceScatter", ALU.add,
             GROUPS, [y_in[g].ap().opt()], [y_out[g].ap().opt()], inc=s)
        rs_sems[g] = s

    AF32.pop()
    ABF.pop()

    AF32.push()
    ABF.push()
    y2f = AF32.tiles(KD, W)
    tmpe = AF32.tiles(2, W)
    r12e = AF32.tiles(2, W)
    rv3 = AF32.tiles(2, W)
    outf = AF32.tiles(2, W)
    r12col = AF32.alloc(KD)
    rhl = AF32.alloc(KD)
    yca = AF32.tiles(2, KI)
    blockE = ABF.tiles(KI, W)
    ytl = ABF.tiles(2, W)
    gte = ABF.tiles(2, W)
    y2q = ABF.tiles(2, W)
    xn3b = [ABF.alloc(1 + W) for _ in range(KD)]
    xr3 = ABF.tiles(KD, W)
    xk3 = ABF.tiles(KD, W)
    mt3 = ABF.tiles(2, W)
    gcol = ABF.alloc(KI)
    ygcol = ABF.alloc(KI)
    hxb3 = ABF.alloc(KD)
    hxq3 = ABF.alloc(KD)
    ygb = blockE
    y2b = blockE[0:KD]
    rb3 = blockE[KD:KI]
    ka3 = xr3

    R.w("sync", yc_sem[0])
    R.dma("sync", yca[0][:, :], yc_out[0], sL)
    R.dma("sync", yca[1][:, :], yc_out[1], sL)
    R.w("vector", sL)
    R.do("vector", "tensor_add", y4095[:, :], yca[0][:, :], yca[1][:, :],
         inc=sv)
    yca_done = sv.n
    R.w("sync", sv, yca_done)
    R.w("sync", sg[0])
    R.w("sync", sg[1])
    R.w("sync", sr12)
    for kk in range(KI):
        R.dma("sync", gcol[:, kk:kk + 1],
              gate_sp[:, kk * TOK + TOK - 1:kk * TOK + TOK], sL, nc_ok=True)
    R.w("vector", sL)
    R.do("vector", "tensor_mul", ygcol[:, :], y4095[:, :], gcol[:, :], inc=sv)
    yg_col_v = sv.n
    s0b, v0b = bank_free[5]
    R.w("tensor", s0b, v0b)
    R.w("tensor", sv, yg_col_v)
    first = True
    for pz in range(4):
        mh, kh = pz // 2, pz % 2
        buf, wld, pidx = load_piece(wout_b, pz)
        R.w("tensor", wld)
        for ml in range(4):
            mo = mh * 4 + ml
            for k in range(KD):
                kk = kh * KD + k
                last = (pz == 3 and ml == 3 and k == KD - 1)
                piece_last = (ml == 3 and k == KD - 1)
                R.do("tensor", "matmul", PS[5][:, mo:mo + 1],
                     buf[:, k * 512 + ml * 128:k * 512 + (ml + 1) * 128],
                     ygcol[:, kk:kk + 1],
                     start=first, stop=last, inc=tt if piece_last else None)
                first = False
        piece_done(pidx)
    R.w("sync", sv, yg_col_v)
    for k in range(KD):
        R.dma("sync", r12col[:, k:k + 1],
              r12_sp[:, k * TOK + TOK - 1:k * TOK + TOK], sL, nc_ok=True)
    R.w("vector", sL)
    R.w("vector", tt, tt.n)
    R.do("vector", "tensor_add", r12col[:, :], r12col[:, :], PS[5][:, 0:KD],
         inc=sv)
    bank_free[5] = (sv, sv.n)
    R.w("scalar", sv, sv.n)
    mini_ln_cols(r12col, hxb3, hxq3, KD)
    R.do("vector", "tensor_single_scalar", r12col[:, :], r12col[:, :],
         PS[5][:, 0:1], ALU.subtract)
    R.do("vector", "tensor_single_scalar", r12col[:, :], r12col[:, :],
         PS[5][:, 1:2], ALU.mult, inc=sv)
    bank_free[5] = (sv, sv.n)
    R.w("sync", sv, sv.n)
    R.dma("sync", rh_in[:, :], r12col[:, :], srh)
    R.w("gpsimd", srh)
    rh_sem = next_cc()
    R.do("gpsimd", "collective_compute", "AllGather", ALU.bypass,
         GROUPS, [rh_in.ap().opt()], [rh_out.ap().opt()], inc=rh_sem)
    R.w("sync", rh_sem)
    R.dma("sync", rhl[:, :], rh_out[0], sL)
    R.w("vector", sL)
    R.do("vector", "tensor_single_scalar", xn3_halo[:, :], rhl[:, :],
         hmask_sb[:, 0:1], ALU.mult, inc=sv)
    halo3_done = sv.n

    epi_ms = []

    def epi_chunk(e):
        c0 = e * W
        g = e // AGC
        prev = epi_ms[-1] if epi_ms else None
        R.w("sync", rs_sems[g])
        if prev is None:
            R.w("sync", sg[0])
            R.w("sync", sg[1])
            R.w("sync", sr12)
        if prev is not None:
            R.w("sync", sv, prev["sv"])
            R.w("sync", st, prev["st"])
        else:
            R.w("sync", sv, halo3_done)
        yg_st = []
        for kk in range(KI):
            if kk >= 2:
                R.w("sync", sv, yg_st[kk - 2])
            R.dma("sync", ytl[kk % 2][:, :],
                  y_out[g][:, kk * GA + (e % AGC) * W:
                           kk * GA + (e % AGC + 1) * W], sey[kk % 2])
            R.dma("sync", gte[kk % 2][:, :],
                  gate_sp[:, kk * TOK + c0:kk * TOK + c0 + W], sey[kk % 2])
            R.w("vector", sey[kk % 2])
            R.do("vector", "tensor_mul", ygb[kk][:, :], ytl[kk % 2][:, :],
                 gte[kk % 2][:, :], inc=sv)
            yg_st.append(sv.n)
        yg_done = sv.n
        for mh in range(2):
            bA = load_piece(wout_b, mh * 2 + 0)
            bX = load_piece(wout_b, mh * 2 + 1)
            for ml in range(4):
                mo = mh * 4 + ml
                bk = (0, 1, 6)[mo % 3]
                s0b, v0b = bank_free[bk]
                R.w("tensor", s0b, v0b)
                R.w("tensor", bA[1])
                R.w("tensor", bX[1])
                R.w("tensor", sv, yg_done)
                pairs = []
                for kk in range(KI):
                    buf = (bA if kk < KD else bX)[0]
                    k = kk % KD
                    pairs.append(
                        (buf[:, k * 512 + ml * 128:k * 512 + (ml + 1) * 128],
                         ygb[kk][:, :]))
                mm_group(PS[bk][:, :W], bk, pairs, guard=False)
                if mo >= 2:
                    R.w("sync", sv, sv.n)
                R.dma("sync", r12e[mo % 2][:, :],
                      r12_sp[:, mo * TOK + c0:mo * TOK + c0 + W], ser[mo % 2])
                R.w("vector", tt, tt.n)
                R.w("vector", ser[mo % 2])
                R.do("vector", "tensor_add", y2f[mo][:, :], PS[bk][:, :W],
                     r12e[mo % 2][:, :], inc=sv)
                bank_free[bk] = (sv, sv.n)
            piece_done(bA[2])
            piece_done(bX[2])
        y2_done = sv.n

        def cast_y2(k):
            R.w("scalar", sv, y2_done)
            R.do("scalar", "activation", y2b[k][:, :], y2f[k][:, :], AF.Copy,
                 inc=st)
            R.do("scalar", "activation", y2q[k % 2][:, :], y2f[k][:, :],
                 AF.Square, inc=st)
            return y2b[k][:, :], y2q[k % 2][:, :]

        ln_stats(cast_y2, W)
        if e == 0:
            R.w("vector", sv, halo3_done)
            for k in range(KD):
                R.do("vector", "tensor_copy", xn3b[k][:, 0:1],
                     xn3_halo[:, k:k + 1])
        else:
            for k in range(KD):
                R.do("vector", "tensor_copy", xn3b[k][:, 0:1],
                     xn3b[k][:, W:W + 1])
        for k in range(KD):
            R.do("vector", "tensor_sub", tmpe[k % 2][:, :], y2f[k][:, :],
                 PS[3][:, :W])
            R.do("vector", "tensor_mul", xn3b[k][:, 1:1 + W], tmpe[k % 2][:, :],
                 PS[4][:, :W], inc=sv if k == KD - 1 else None)
        bank_free[3] = (sv, sv.n)
        bank_free[4] = (sv, sv.n)
        for k in range(KD):
            cur = xn3b[k][:, 1:1 + W]
            prv = xn3b[k][:, 0:W]
            R.do("vector", "tensor_sub", mt3[k % 2][:, :], cur, prv)
            R.do("vector", "scalar_tensor_tensor", xr3[k][:, :], mt3[k % 2][:, :],
                 mur_sb[:, k:k + 1], prv, ALU.mult, ALU.add)
            R.do("vector", "scalar_tensor_tensor", xk3[k][:, :], mt3[k % 2][:, :],
                 muk_sb[:, k:k + 1], prv, ALU.mult, ALU.add,
                 inc=sv if k == KD - 1 else None)
        mixes = sv.n
        for mh in range(2):
            buf, wld, pidx = load_piece(wr_b, mh)
            for ml in range(4):
                mo = mh * 4 + ml
                bk = (0, 1, 6)[mo % 3]
                R.w("tensor", sv, mixes)
                R.w("tensor", wld)
                mm_group(PS[bk][:, :W], bk,
                         [(buf[:, k * 512 + ml * 128:k * 512 + (ml + 1) * 128],
                           xr3[k][:, :]) for k in range(KD)])
                R.w("scalar", tt, tt.n)
                R.do("scalar", "activation", rb3[mo % KD][:, :], PS[bk][:, :W],
                     AF.Sigmoid, inc=st)
                bank_free[bk] = (st, st.n)
            piece_done(pidx)
        for mh in range(2):
            buf, wld, pidx = load_piece(wk_b, mh)
            for ml in range(4):
                mo = mh * 4 + ml
                bk = (0, 1, 6)[mo % 3]
                R.w("tensor", wld)
                mm_group(PS[bk][:, :W], bk,
                         [(buf[:, k * 512 + ml * 128:k * 512 + (ml + 1) * 128],
                           xk3[k][:, :]) for k in range(KD)])
                R.w("scalar", tt, tt.n)
                R.do("scalar", "activation", y2q[0][:, :], PS[bk][:, :W],
                     AF.Relu, inc=st)
                bank_free[bk] = (st, st.n)
                R.do("scalar", "activation", ka3[mo][:, :], y2q[0][:, :],
                     AF.Square, inc=st)
            piece_done(pidx)
        ka_done = st.n
        for mh in range(2):
            buf, wld, pidx = load_piece(wv_b, mh)
            for ml in range(4):
                mo = mh * 4 + ml
                bk = (0, 1, 6)[mo % 3]
                R.w("tensor", st, ka_done)
                R.w("tensor", wld)
                mm_group(PS[bk][:, :W], bk,
                         [(buf[:, k * 512 + ml * 128:k * 512 + (ml + 1) * 128],
                           ka3[k][:, :]) for k in range(KD)])
                R.w("vector", tt, tt.n)
                R.w("vector", sout[mo % 2])
                R.do("vector", "tensor_mul", rv3[mo % 2][:, :],
                     rb3[mo % KD][:, :], PS[bk][:, :W])
                R.do("vector", "scalar_tensor_tensor", outf[mo % 2][:, :],
                     rv3[mo % 2][:, :], 0.5, y2f[mo][:, :], ALU.mult, ALU.add,
                     inc=sv)
                bank_free[bk] = (sv, sv.n)
                R.w("sync", sv, sv.n)
                R.dma("sync", outT[mo * 128:(mo + 1) * 128, c0:c0 + W],
                      outf[mo % 2][:, :], sout[mo % 2])
            piece_done(pidx)
        epi_ms.append({"sv": sv.n, "st": st.n})

    for e in range(NCH):
        epi_chunk(e)

    AF32.pop()
    ABF.pop()

    R.w("gpsimd", sout[0])
    R.w("gpsimd", sout[1])

    with ctx:
        with nc.Block() as block:
            @block.sync
            def _(e):
                for f in R.q["sync"]:
                    f(e)

            @block.gpsimd
            def _(e):
                for f in R.q["gpsimd"]:
                    f(e)

            @block.scalar
            def _(e):
                for f in R.q["scalar"]:
                    f(e)

            @block.vector
            def _(e):
                for f in R.q["vector"]:
                    f(e)

            @block.tensor
            def _(e):
                for f in R.q["tensor"]:
                    f(e)

    return nc


_NC_CACHE = {}


def _get_nc():
    key = (T, TCP)
    if key not in _NC_CACHE:
        _NC_CACHE[key] = build()
    return _NC_CACHE[key]


def kernel(
    L0_attn_x, L0_attn_norm_w, L0_attn_norm_b, L0_attn_W_in, L0_attn_W_gate,
    L0_attn_W_dt, L0_attn_b_dt, L0_attn_W_B, L0_attn_W_C, L0_attn_A_log,
    L0_attn_W_out, L0_ffn_W_r, L0_ffn_W_k, L0_ffn_W_v, L0_ffn_mu_r, L0_ffn_mu_k,
):
    f32 = np.float32
    x = np.asarray(L0_attn_x, f32)
    shared = {
        "wrT": np.ascontiguousarray(np.asarray(L0_ffn_W_r, f32).T),
        "wkT": np.ascontiguousarray(np.asarray(L0_ffn_W_k, f32).T),
        "wvT": np.ascontiguousarray(np.asarray(L0_ffn_W_v, f32).T),
        "win": np.ascontiguousarray(np.asarray(L0_attn_W_in, f32)),
        "wgate": np.ascontiguousarray(np.asarray(L0_attn_W_gate, f32)),
        "wdt": np.ascontiguousarray(np.asarray(L0_attn_W_dt, f32)),
        "wb16": np.ascontiguousarray(np.asarray(L0_attn_W_B, f32)),
        "wc16": np.ascontiguousarray(np.asarray(L0_attn_W_C, f32)),
        "wout": np.ascontiguousarray(np.asarray(L0_attn_W_out, f32)),
        "bdt": np.ascontiguousarray(np.asarray(L0_attn_b_dt, f32).reshape(KI, 128).T),
        "mur": np.ascontiguousarray(np.asarray(L0_ffn_mu_r, f32).reshape(KD, 128).T),
        "muk": np.ascontiguousarray(np.asarray(L0_ffn_mu_k, f32).reshape(KD, 128).T),
        "nw": np.ascontiguousarray(np.asarray(L0_attn_norm_w, f32).reshape(KD, 128).T),
        "nb": np.ascontiguousarray(np.asarray(L0_attn_norm_b, f32).reshape(KD, 128).T),
    }
    alog = np.asarray(L0_attn_A_log, f32)
    in_maps = []
    for c in range(NC):
        b, h = c // 2, c % 2
        xt = np.ascontiguousarray(x[b].T)
        halo = (np.zeros((D, 1), f32) if h == 0
                else np.ascontiguousarray(xt[:, TOK - 1:TOK]))
        m = dict(shared)
        m["xT"] = np.ascontiguousarray(
            np.concatenate([halo, xt[:, h * TOK:(h + 1) * TOK]], axis=1))
        asl = alog[:, h * SH:(h + 1) * SH].reshape(KI, 128, SH)
        m["alog"] = np.ascontiguousarray(
            asl.transpose(1, 0, 2).reshape(128, KI * SH))
        m["hmask"] = np.full((128, 1), float(h), f32)
        in_maps.append(m)

    nc = _get_nc()
    res = run_bass_kernel_spmd(nc, in_maps, core_ids=list(range(NC)))
    out = np.empty((B, T, D), f32)
    for c in range(NC):
        b, h = c // 2, c % 2
        out[b, h * TOK:(h + 1) * TOK, :] = np.asarray(res.results[c]["outT"]).T
    return out


# version 05
# speedup vs baseline: 1.0892x; 1.0892x over previous
import numpy as np
from contextlib import ExitStack

import concourse.bass as bass
import concourse.mybir as mybir
from concourse.bass_utils import run_bass_kernel_spmd

F32 = mybir.dt.float32
BF16 = mybir.dt.bfloat16
AF = mybir.ActivationFunctionType
ALU = mybir.AluOpType
AX = mybir.AxisListType

B, T, D, DI, S = 4, 8192, 1024, 2048, 16
NC = 8
KD = D // 128
KI = DI // 128
SH = S // 2
LNEPS = 1e-5

TCP = 512
AGC = 2

ENG_SCAN = "vector"

ENGS = ("sync", "gpsimd", "scalar", "vector", "tensor")
GROUPS = [[0, 1], [2, 3], [4, 5], [6, 7]]


def _derive():
    global TOK, NCH, GA, NAG, NTG, DTU_COLS
    TOK = T // 2
    NCH = TOK // TCP
    GA = AGC * TCP
    NAG = NCH // AGC
    NTG = T // GA
    DTU_COLS = 2 * KI * GA + GA


_derive()


class Sem:
    __slots__ = ("h", "n")


class Rec:
    def __init__(self, nc, ctx):
        self.nc = nc
        self.ctx = ctx
        self.q = {e: [] for e in ENGS}
        self.auto = {}

    def sem(self, name):
        s = Sem()
        s.h = self.ctx.enter_context(self.nc.semaphore(name))
        s.n = 0
        return s

    def w(self, eng, sem, val=None):
        v = sem.n if val is None else val
        assert v <= sem.n, f"wait on future semaphore value {v} > {sem.n}"
        if v <= 0:
            return
        self.q[eng].append(lambda e, h=sem.h, v=v: e.wait_ge(h, v))

    def do(self, eng, meth, *args, inc=None, inck=1, lag=1, **kw):
        s = self.auto.get(eng) if meth not in ("dma_start",
                                               "collective_compute") else None
        if s is not None and s.n - (lag - 1) > 0:
            self.w(eng, s, s.n - (lag - 1))
        if s is not None and inc is None:
            inc = s
        if inc is not None:
            inc.n += inck

        def f(e, meth=meth, args=args, kw=kw, inc=inc, inck=inck):
            ins = getattr(e, meth)(*args, **kw)
            if inc is not None:
                ins.then_inc(inc.h, inck)

        self.q[eng].append(f)

    def dma(self, eng, out, in_, inc, nc_ok=False):
        kw = {"allow_slow_non_contiguous": True} if nc_ok else {}
        self.do(eng, "dma_start", inc=inc, inck=16, out=out, in_=in_, **kw)


class Arena:
    def __init__(self, handle, nelem, gran):
        self.h = handle
        self.ne = nelem
        self.off = 0
        self.gran = gran
        self.marks = []

    def alloc(self, n):
        nr = -(-n // self.gran) * self.gran
        assert self.off + nr <= self.ne, f"arena overflow {self.off}+{nr}>{self.ne}"
        ap = self.h[:, self.off: self.off + n]
        self.off += nr
        return ap

    def tiles(self, k, w):
        return [self.alloc(w) for _ in range(k)]

    def push(self):
        self.marks.append(self.off)

    def pop(self):
        self.off = self.marks.pop()


def build():
    nc = bass.Bass("TRN2", target_bir_lowering=False)
    ctx = ExitStack()
    W = TCP

    din = {}
    for name, shape in [
        ("xT", [D, 1 + TOK]),
        ("wrT", [D, D]), ("wkT", [D, D]), ("wvT", [D, D]),
        ("win", [D, DI]), ("wgate", [D, DI]), ("wdt", [DI, DI]),
        ("wb16", [DI, S]), ("wc16", [DI, S]), ("wout", [DI, D]),
        ("bdt", [128, KI]), ("mur", [128, KD]), ("muk", [128, KD]),
        ("nw", [128, KD]), ("nb", [128, KD]),
        ("alog", [128, KI * SH]), ("hmask", [128, 1]),
    ]:
        din[name] = nc.dram_tensor(name, shape, F32, kind="ExternalInput")
    outT = nc.dram_tensor("outT", [D, TOK], F32, kind="ExternalOutput")

    wr_b = nc.dram_tensor("wr_b", [128, 2 * KD * 512], BF16)
    wk_b = nc.dram_tensor("wk_b", [128, 2 * KD * 512], BF16)
    wv_b = nc.dram_tensor("wv_b", [128, 2 * KD * 512], BF16)
    win_b = nc.dram_tensor("win_b", [128, 4 * KD * 512], BF16)
    wgate_b = nc.dram_tensor("wgate_b", [128, 4 * KD * 512], BF16)
    wdt_b = nc.dram_tensor("wdt_b", [128, 8 * KD * 512], BF16)
    wout_b = nc.dram_tensor("wout_b", [128, 4 * KD * 512], BF16)

    r12_sp = nc.dram_tensor("r12_sp", [128, KD * TOK], F32)
    gate_sp = nc.dram_tensor("gate_sp", [128, KI * TOK], BF16)

    dtu_in = [nc.dram_tensor(f"dtu_in{g}", [128, DTU_COLS], BF16) for g in range(NAG)]
    dtu_out = [nc.dram_tensor(f"dtu_out{g}", [2, 128, DTU_COLS], BF16)
               for g in range(NAG)]
    y_in = [nc.dram_tensor(f"y_in{g}", [2, 128, KI * GA], BF16) for g in range(NAG)]
    y_out = [nc.dram_tensor(f"y_out{g}", [128, KI * GA], BF16) for g in range(NAG)]
    yc_in = nc.dram_tensor("yc_in", [128, KI], F32)
    yc_out = nc.dram_tensor("yc_out", [2, 128, KI], F32)
    rh_in = nc.dram_tensor("rh_in", [128, KD], F32)
    bsel_d = nc.dram_tensor("bsel_d", [2, GA], BF16)
    rh_out = nc.dram_tensor("rh_out", [2, 128, KD], F32)

    sb = lambda nm, shape, dt: ctx.enter_context(nc.sbuf_tensor(nm, shape, dt))
    AF32 = Arena(sb("af32", [128, 8600], F32), 8600, 4)
    ABF = Arena(sb("abf", [128, 26400], BF16), 26400, 16)
    sA = sb("sA", [128, KD * 512], BF16)
    sB = sb("sB", [128, KD * 512], BF16)
    wb_sb = sb("wb_sb", [128, KI * S], BF16)
    wc_sb = sb("wc_sb", [128, KI * S], BF16)
    bdt_sb = sb("bdt_sb", [128, KI], F32)
    mur_sb = sb("mur_sb", [128, KD], F32)
    muk_sb = sb("muk_sb", [128, KD], F32)
    nw_sb = sb("nw_sb", [128, KD], F32)
    nb_sb = sb("nb_sb", [128, KD], F32)
    A_sb = sb("A_sb", [128, KI * SH], F32)
    hmask_sb = sb("hmask_sb", [128, 1], F32)
    ceps = sb("ceps", [128, 1], F32)
    cone = sb("cone", [128, 1], F32)
    ones_cb = sb("ones_cb", [128, 1], BF16)
    ones_r = sb("ones_r", [1, 128], F32)
    carry = sb("carry", [128, KI * SH], F32)
    row_mean = sb("row_mean", [1, 512], F32)
    row_msq = sb("row_msq", [1, 512], F32)
    row_tmp = sb("row_tmp", [1, 512], F32)
    row_rstd = sb("row_rstd", [1, 512], F32)
    row_sd = sb("row_sd", [1, 512], F32)
    xn0_halo = sb("xn0_halo", [128, KD], BF16)
    xn3_halo = sb("xn3_halo", [128, KD], F32)
    y4095 = sb("y4095", [128, KI], F32)
    dts = [sb(f"dts{i}", [128, GA], BF16) for i in range(2)]
    uts = [sb(f"uts{i}", [128, GA], BF16) for i in range(2)]
    Bbc_all = sb("Bbc_all", [128, SH * GA], BF16)
    Cbc_all = sb("Cbc_all", [128, SH * GA], BF16)
    Bbc = [Bbc_all[:, i * GA:(i + 1) * GA] for i in range(SH)]
    Cbc = [Cbc_all[:, i * GA:(i + 1) * GA] for i in range(SH)]
    dAr = [sb(f"dAr{i}", [128, GA], F32) for i in range(2)]
    hbr2 = sb("hbr2", [128, 2 * GA], BF16)
    burb = [sb(f"burb{i}", [128, GA], BF16) for i in range(4)]
    hcp = [sb(f"hcp{i}", [128, 2 * GA], BF16) for i in range(2)]
    pp = [sb(f"pp{i}", [128, GA], BF16) for i in range(3)]
    ysc = [sb(f"ysc{i}", [128, GA], BF16) for i in range(2)]
    zero5 = sb("zero5", [128, 512], BF16)
    hm0_sb = sb("hm0_sb", [128, 1], F32)
    brow = [sb(f"brow{i}", [1, GA], BF16) for i in range(4)]
    bwork = [sb(f"bwork{i}", [1, GA], BF16) for i in range(2)]

    PS = [ctx.enter_context(nc.psum_tensor(f"ps{i}", [128, 512], F32))
          for i in range(8)]

    R = Rec(nc, ctx)
    st = R.sem("st")
    sv = R.sem("sv")
    pv = R.sem("pv")
    tt = R.sem("tt")
    sL = R.sem("sL")
    scf = [R.sem(f"scf{i}") for i in range(2)]
    scs = [R.sem(f"scs{i}") for i in range(2)]
    sx = [R.sem(f"sx{i}") for i in range(2)]
    sw = [R.sem(f"sw{i}") for i in range(2)]
    sr12 = R.sem("sr12")
    sg = [R.sem(f"sg{i}") for i in range(2)]
    sd = [R.sem(f"sd{i}") for i in range(2)]
    sbcst = R.sem("sbcst")
    sbc = R.sem("sbc")
    sdu = [R.sem(f"sdu{i}") for i in range(2)]
    ssy = [R.sem(f"ssy{i}") for i in range(NTG)]
    syc = R.sem("syc")
    srh = R.sem("srh")
    sey = [R.sem(f"sey{i}") for i in range(2)]
    ser = [R.sem(f"ser{i}") for i in range(2)]
    sout = [R.sem(f"sout{i}") for i in range(2)]
    ccs = [R.sem(f"cc{i}") for i in range(2 * NAG + 3)]
    cc_n = [0]

    def next_cc():
        s = ccs[cc_n[0]]
        cc_n[0] += 1
        return s

    R.auto = {"vector": sv, "scalar": st, "gpsimd": pv}
    bank_free = [(sv, 0)] * 8

    def mm_group(bank_ap, bank_idx, pairs, guard=True):
        if guard:
            s0, v0 = bank_free[bank_idx]
            R.w("tensor", s0, v0)
        for i, (lh, rh) in enumerate(pairs):
            R.do("tensor", "matmul", bank_ap, lh, rh,
                 start=(i == 0), stop=(i == len(pairs) - 1),
                 inc=tt if i == len(pairs) - 1 else None)

    def ln_stats(cast_fn, w, kd=KD):
        s0, v0 = bank_free[2]
        ttv = []
        for k in range(kd):
            if k >= 2:
                R.w("scalar", tt, ttv[k - 2])
            vb, qb = cast_fn(k)
            if k == 0:
                R.w("tensor", s0, v0)
            R.w("tensor", st, st.n)
            R.do("tensor", "matmul", PS[2][0:1, :w], ones_cb[:, :], vb,
                 start=(k == 0), stop=(k == kd - 1))
            R.do("tensor", "matmul", PS[2][32:33, :w], ones_cb[:, :], qb,
                 start=(k == 0), stop=(k == kd - 1), tile_position=(0, 32),
                 inc=tt)
            ttv.append(tt.n)
        R.w("vector", tt, tt.n)
        inv = 1.0 / (kd * 128)
        R.do("vector", "tensor_scalar_mul", row_mean[:, :w], PS[2][0:1, :w], inv)
        R.do("vector", "tensor_scalar_mul", row_msq[:, :w], PS[2][32:33, :w], inv)
        R.do("vector", "tensor_mul", row_tmp[:, :w], row_mean[:, :w],
             row_mean[:, :w])
        R.do("vector", "tensor_sub", row_tmp[:, :w], row_msq[:, :w],
             row_tmp[:, :w], inc=sv)
        bank_free[2] = (sv, sv.n)
        R.w("scalar", sv, sv.n)
        R.do("scalar", "activation", row_sd[:, :w], row_tmp[:, :w], AF.Sqrt,
             bias=ceps[0:1, 0:1], inc=st)
        R.w("vector", st, st.n)
        R.do("vector", "reciprocal", row_rstd[:, :w], row_sd[:, :w], inc=sv)
        for bank, row in ((3, row_mean), (4, row_rstd)):
            s0, v0 = bank_free[bank]
            R.w("tensor", s0, v0)
            R.w("tensor", sv, sv.n)
            R.do("tensor", "matmul", PS[bank][:, :w], ones_r[:, :], row[:, :w],
                 start=True, stop=True, inc=tt)
        R.w("vector", tt, tt.n)

    def mini_ln_cols(vals_f32, vb, qb, kd):
        R.do("scalar", "activation", vb[:, :], vals_f32[:, :], AF.Copy, inc=st)
        R.do("scalar", "activation", qb[:, :], vals_f32[:, :], AF.Square, inc=st)
        s0b, v0b = bank_free[5]
        R.w("tensor", s0b, v0b)
        R.w("tensor", st, st.n)
        R.do("tensor", "matmul", PS[5][0:1, 0:kd], ones_cb[:, :], vb[:, :],
             start=True, stop=True)
        R.do("tensor", "matmul", PS[5][32:33, 0:kd], ones_cb[:, :], qb[:, :],
             start=True, stop=True, tile_position=(0, 32), inc=tt)
        R.w("vector", tt, tt.n)
        R.do("vector", "tensor_reduce", row_tmp[0:1, 8:9], PS[5][0:1, 0:kd],
             AX.X, ALU.add)
        R.do("vector", "tensor_reduce", row_tmp[0:1, 9:10],
             PS[5][32:33, 0:kd], AX.X, ALU.add)
        R.do("vector", "tensor_scalar_mul", row_tmp[0:1, 8:9],
             row_tmp[0:1, 8:9], 1.0 / D)
        R.do("vector", "tensor_scalar_mul", row_tmp[0:1, 9:10],
             row_tmp[0:1, 9:10], 1.0 / D)
        R.do("vector", "tensor_mul", row_tmp[0:1, 10:11], row_tmp[0:1, 8:9],
             row_tmp[0:1, 8:9])
        R.do("vector", "tensor_sub", row_tmp[0:1, 9:10], row_tmp[0:1, 9:10],
             row_tmp[0:1, 10:11], inc=sv)
        bank_free[5] = (sv, sv.n)
        R.w("scalar", sv, sv.n)
        R.do("scalar", "activation", row_tmp[0:1, 12:13], row_tmp[0:1, 9:10],
             AF.Sqrt, bias=ceps[0:1, 0:1], inc=st)
        R.w("vector", st, st.n)
        R.do("vector", "reciprocal", row_tmp[0:1, 11:12], row_tmp[0:1, 12:13],
             inc=sv)
        s0b, v0b = bank_free[5]
        R.w("tensor", s0b, v0b)
        R.w("tensor", sv, sv.n)
        R.do("tensor", "matmul", PS[5][:, 0:1], ones_r[:, :],
             row_tmp[0:1, 8:9], start=True, stop=True)
        R.do("tensor", "matmul", PS[5][:, 1:2], ones_r[:, :],
             row_tmp[0:1, 11:12], start=True, stop=True, inc=tt)
        R.w("vector", tt, tt.n)

    for name, tile in [
        ("bdt", bdt_sb), ("mur", mur_sb), ("muk", muk_sb), ("nw", nw_sb),
        ("nb", nb_sb), ("hmask", hmask_sb),
    ]:
        R.dma("sync", tile[:, :], din[name][:, :], sL)
    R.do("vector", "memset", ceps[:, :], LNEPS)
    R.do("vector", "memset", cone[:, :], 1.0)
    R.do("vector", "memset", ones_cb[:, :], 1.0)
    R.do("vector", "memset", ones_r[:, :], 1.0)
    R.do("vector", "memset", carry[:, :], 0.0)
    R.do("vector", "memset", zero5[:, :], 0.0)
    R.w("vector", sL)
    R.do("vector", "tensor_scalar", hm0_sb[:, :], hmask_sb[:, :], -1.0, 1.0,
         ALU.mult, ALU.add)
    R.do("vector", "memset", xn3_halo[:, :], 0.0, inc=sv)
    R.w("sync", sv, sv.n)
    for _g in range(NAG):
        for _c2 in range(AGC):
            _bco = 2 * KI * GA + _c2 * TCP
            R.dma("sync", dtu_in[_g][32:128, _bco:_bco + TCP],
                  zero5[0:96, :TCP], sbcst)
    AF32.push()
    atmp = AF32.alloc(KI * SH)
    R.dma("sync", atmp[:, :], din["alog"][:, :], sL)
    R.w("scalar", sL)
    R.do("scalar", "activation", A_sb[:, :], atmp[:, :], AF.Exp, inc=st)
    R.w("vector", st, st.n)
    R.do("vector", "tensor_scalar", A_sb[:, :], A_sb[:, :], 1e-6, 1.0, ALU.max,
         ALU.min)
    R.do("vector", "tensor_scalar_mul", A_sb[:, :], A_sb[:, :], -1.0, inc=sv)
    AF32.pop()

    R.w("sync", st, st.n)
    AF32.push()
    ABF.push()
    cast_f = AF32.tiles(2, 512)
    cast_b = ABF.tiles(2, 512)
    cast_sv = []
    cast_n = [0]

    def cast_tile(src_ap, dst_ap):
        i = cast_n[0] % 2
        if cast_n[0] >= 2:
            R.w("sync", sv, cast_sv[cast_n[0] - 2])
        R.dma("sync", cast_f[i][:, :512], src_ap, scf[i])
        R.w("vector", scf[i])
        R.w("vector", scs[i])
        R.do("vector", "tensor_copy", cast_b[i][:, :512], cast_f[i][:, :512],
             inc=sv)
        cast_sv.append(sv.n)
        R.w("sync", sv, sv.n)
        R.dma("sync", dst_ap, cast_b[i][:, :512], scs[i])
        cast_n[0] += 1

    def cast_w(src, dst, n_k, n_pieces):
        for p in range(n_pieces):
            for k in range(n_k):
                cast_tile(src[k * 128:(k + 1) * 128, p * 512:(p + 1) * 512],
                          dst[:, (p * n_k + k) * 512:(p * n_k + k + 1) * 512])

    cast_w(din["wrT"], wr_b, KD, 2)
    cast_w(din["wkT"], wk_b, KD, 2)
    cast_w(din["wvT"], wv_b, KD, 2)
    cast_w(din["win"], win_b, KD, 4)
    cast_w(din["wgate"], wgate_b, KD, 4)
    cast_w(din["wout"], wout_b, KI, 2)
    for kh in range(2):
        for mq in range(4):
            for k in range(KD):
                p = kh * 4 + mq
                cast_tile(
                    din["wdt"][(kh * KD + k) * 128:(kh * KD + k + 1) * 128,
                               mq * 512:(mq + 1) * 512],
                    wdt_b[:, (p * KD + k) * 512:(p * KD + k + 1) * 512])
    for src, dst in [(din["wb16"], wb_sb), (din["wc16"], wc_sb)]:
        for k in range(KI):
            i = cast_n[0] % 2
            R.w("sync", sv, cast_sv[cast_n[0] - 2])
            R.dma("sync", cast_f[i][:, :S], src[k * 128:(k + 1) * 128, :],
                  scf[i])
            R.w("vector", scf[i])
            R.do("vector", "tensor_copy", dst[:, k * S:(k + 1) * S],
                 cast_f[i][:, :S], inc=sv)
            cast_sv.append(sv.n)
            cast_n[0] += 1
    wbc_ready = sv.n
    AF32.pop()
    ABF.pop()
    R.w("sync", sv, wbc_ready)

    AF32.push()
    ABF.push()
    xf = AF32.tiles(2, W)
    tmpf = AF32.tiles(2, W)
    y1f = AF32.tiles(KD, W)
    dtt = AF32.tiles(2, W)
    rvf = AF32.tiles(2, W)
    hx = AF32.alloc(KD)
    xb = ABF.tiles(KD, W)
    xq = ABF.tiles(2, W)
    xn0b = [ABF.alloc(1 + W) for _ in range(KD)]
    blockA = ABF.tiles(KD, W)
    blockB = ABF.tiles(KD, W)
    rbt = ABF.tiles(KD, W)
    dtf = ABF.tiles(2, W)
    ub = ABF.tiles(2, W)
    gsil = ABF.tiles(2, W)
    mtb = ABF.tiles(2, W)
    bcst = ABF.alloc(W)
    hxb = ABF.alloc(KD)
    hxq = ABF.alloc(KD)
    xrb, kab = blockA, blockA
    xkb = blockB
    zb = blockA + blockB
    rb, t1b = rbt, rbt

    pstate = {"i": 0, "cons": []}

    def load_piece(src, p, n_k=KD):
        i = pstate["i"]
        buf = sA if i % 2 == 0 else sB
        if i >= 2:
            s0, v0 = pstate["cons"][i - 2]
            R.w("sync", s0, v0)
        R.dma("sync", buf[:, :n_k * 512],
              src[:, p * n_k * 512:(p + 1) * n_k * 512], sw[i % 2])
        pstate["cons"].append(None)
        pstate["i"] += 1
        return buf, sw[i % 2], i

    def piece_done(i):
        pstate["cons"][i] = (tt, tt.n)

    pre_ms = []

    def pre_chunk(j):
        c0 = j * W
        g = j // AGC
        c2 = j % AGC
        prev = pre_ms[-1] if pre_ms else None
        cast_st = []

        def cast_x(k):
            if k < 2:
                if prev is not None:
                    R.w("sync", sv, prev["sv"])
                    R.w("sync", st, prev["st"])
            else:
                R.w("sync", st, cast_st[k - 2])
            R.dma("sync", xf[k % 2][:, :],
                  din["xT"][k * 128:(k + 1) * 128, 1 + c0:1 + c0 + W],
                  sx[k % 2])
            R.w("scalar", sx[k % 2])
            if j == 0 and k == 0:
                R.w("scalar", sL)
                R.w("vector", sL)
                R.w("tensor", sL)
            R.do("scalar", "activation", xb[k][:, :], xf[k % 2][:, :], AF.Copy,
                 inc=st)
            R.do("scalar", "activation", xq[k % 2][:, :], xf[k % 2][:, :],
                 AF.Square, inc=st)
            cast_st.append(st.n)
            return xb[k][:, :], xq[k % 2][:, :]

        ln_stats(cast_x, W)
        if j == 0:
            for k in range(KD):
                R.dma("sync", hx[:, k:k + 1],
                      din["xT"][k * 128:(k + 1) * 128, 0:1], sL, nc_ok=True)
            R.w("scalar", sL)
            mini_ln_cols(hx, hxb, hxq, KD)
            R.do("vector", "tensor_single_scalar", hx[:, :], hx[:, :],
                 PS[5][:, 0:1], ALU.subtract)
            R.do("vector", "tensor_single_scalar", xn0_halo[:, :], hx[:, :],
                 PS[5][:, 1:2], ALU.mult, inc=sv)
            bank_free[5] = (sv, sv.n)
            for k in range(KD):
                R.do("vector", "tensor_copy", xn0b[k][:, 0:1],
                     xn0_halo[:, k:k + 1])
        else:
            for k in range(KD):
                R.do("vector", "tensor_copy", xn0b[k][:, 0:1],
                     xn0b[k][:, W:W + 1])
        for k in range(KD):
            R.do("vector", "tensor_sub", tmpf[k % 2][:, :], xb[k][:, :],
                 PS[3][:, :W])
            R.do("vector", "tensor_mul", xn0b[k][:, 1:1 + W], tmpf[k % 2][:, :],
                 PS[4][:, :W], inc=sv if k == KD - 1 else None)
        bank_free[3] = (sv, sv.n)
        bank_free[4] = (sv, sv.n)
        for k in range(KD):
            cur = xn0b[k][:, 1:1 + W]
            prv = xn0b[k][:, 0:W]
            R.do("vector", "tensor_sub", mtb[k % 2][:, :], cur, prv)
            R.do("vector", "scalar_tensor_tensor", xrb[k][:, :], mtb[k % 2][:, :],
                 mur_sb[:, k:k + 1], prv, ALU.mult, ALU.add)
            R.do("vector", "scalar_tensor_tensor", xkb[k][:, :], mtb[k % 2][:, :],
                 muk_sb[:, k:k + 1], prv, ALU.mult, ALU.add,
                 inc=sv if k == KD - 1 else None)
        mixes = sv.n
        for mh in range(2):
            buf, wld, pidx = load_piece(wr_b, mh)
            for ml in range(4):
                mo = mh * 4 + ml
                bk = (0, 1, 6)[mo % 3]
                R.w("tensor", sv, mixes)
                R.w("tensor", wld)
                mm_group(PS[bk][:, :W], bk,
                         [(buf[:, k * 512 + ml * 128:k * 512 + (ml + 1) * 128],
                           xrb[k][:, :]) for k in range(KD)])
                R.w("scalar", tt, tt.n)
                R.do("scalar", "activation", rb[mo][:, :], PS[bk][:, :W],
                     AF.Sigmoid, inc=st)
                bank_free[bk] = (st, st.n)
            piece_done(pidx)
        for mh in range(2):
            buf, wld, pidx = load_piece(wk_b, mh)
            for ml in range(4):
                mo = mh * 4 + ml
                bk = (0, 1, 6)[mo % 3]
                R.w("tensor", wld)
                mm_group(PS[bk][:, :W], bk,
                         [(buf[:, k * 512 + ml * 128:k * 512 + (ml + 1) * 128],
                           xkb[k][:, :]) for k in range(KD)])
                R.w("scalar", tt, tt.n)
                R.do("scalar", "activation", xq[0][:, :], PS[bk][:, :W],
                     AF.Relu, inc=st)
                bank_free[bk] = (st, st.n)
                R.do("scalar", "activation", kab[mo][:, :], xq[0][:, :],
                     AF.Square, inc=st)
            piece_done(pidx)
        ka_done = st.n
        xld = []
        for mh in range(2):
            buf, wld, pidx = load_piece(wv_b, mh)
            for ml in range(4):
                mo = mh * 4 + ml
                bk = (0, 1, 6)[mo % 3]
                R.w("tensor", st, ka_done)
                R.w("tensor", wld)
                mm_group(PS[bk][:, :W], bk,
                         [(buf[:, k * 512 + ml * 128:k * 512 + (ml + 1) * 128],
                           kab[k][:, :]) for k in range(KD)])
                if mo >= 2:
                    R.w("sync", sv, xld[mo - 2])
                else:
                    R.w("sync", st, cast_st[-1])
                R.dma("sync", xf[mo % 2][:, :],
                      din["xT"][mo * 128:(mo + 1) * 128, 1 + c0:1 + c0 + W],
                      sx[mo % 2])
                R.w("vector", tt, tt.n)
                R.w("vector", sx[mo % 2])
                if j > 0 and mo == 0:
                    R.w("vector", sr12)
                R.do("vector", "tensor_mul", rvf[mo % 2][:, :], rb[mo][:, :],
                     PS[bk][:, :W])
                R.do("vector", "scalar_tensor_tensor", y1f[mo][:, :],
                     rvf[mo % 2][:, :], 0.5, xf[mo % 2][:, :], ALU.mult,
                     ALU.add, inc=sv)
                bank_free[bk] = (sv, sv.n)
                xld.append(sv.n)
            piece_done(pidx)
        y1_done = sv.n

        def cast_y1(k):
            R.w("scalar", sv, y1_done)
            R.do("scalar", "activation", xb[k][:, :], y1f[k][:, :], AF.Copy,
                 inc=st)
            R.do("scalar", "activation", xq[k % 2][:, :], y1f[k][:, :],
                 AF.Square, inc=st)
            return xb[k][:, :], xq[k % 2][:, :]

        ln_stats(cast_y1, W)
        for k in range(KD):
            R.do("vector", "tensor_sub", tmpf[k % 2][:, :], y1f[k][:, :],
                 PS[3][:, :W])
            R.do("vector", "tensor_mul", t1b[k][:, :], tmpf[k % 2][:, :],
                 PS[4][:, :W])
            R.do("vector", "tensor_add", y1f[k][:, :], y1f[k][:, :],
                 t1b[k][:, :], inc=sv)
        bank_free[3] = (sv, sv.n)
        bank_free[4] = (sv, sv.n)
        r12_done = sv.n
        R.w("sync", sv, r12_done)
        for k in range(KD):
            R.dma("sync", r12_sp[:, k * TOK + c0:k * TOK + c0 + W],
                  y1f[k][:, :], sr12)

        def cast_t1(k):
            R.w("scalar", sv, r12_done)
            R.do("scalar", "activation", xq[k % 2][:, :], t1b[k][:, :],
                 AF.Square, inc=st)
            return t1b[k][:, :], xq[k % 2][:, :]

        ln_stats(cast_t1, W)
        for k in range(KD):
            R.do("vector", "tensor_sub", tmpf[k % 2][:, :], t1b[k][:, :],
                 PS[3][:, :W])
            R.do("vector", "tensor_mul", tmpf[k % 2][:, :], tmpf[k % 2][:, :],
                 PS[4][:, :W])
            R.do("vector", "tensor_scalar", xb[k][:, :], tmpf[k % 2][:, :],
                 nw_sb[:, k:k + 1], nb_sb[:, k:k + 1], ALU.mult, ALU.add,
                 inc=sv if k == KD - 1 else None)
        xn_ready = sv.n
        bank_free[3] = (sv, sv.n)
        bank_free[4] = (sv, sv.n)
        for p in range(4):
            buf, wld, pidx = load_piece(win_b, p)
            for ml in range(4):
                mo = p * 4 + ml
                bk = (0, 1, 6)[mo % 3]
                R.w("tensor", sv, xn_ready)
                R.w("tensor", wld)
                mm_group(PS[bk][:, :W], bk,
                         [(buf[:, k * 512 + ml * 128:k * 512 + (ml + 1) * 128],
                           xb[k][:, :]) for k in range(KD)])
                R.w("scalar", tt, tt.n)
                R.do("scalar", "activation", zb[mo][:, :], PS[bk][:, :W],
                     AF.Copy, inc=st)
                bank_free[bk] = (st, st.n)
            piece_done(pidx)
        for p in range(4):
            buf, wld, pidx = load_piece(wgate_b, p)
            for ml in range(4):
                mo = p * 4 + ml
                bk = (0, 1, 6)[mo % 3]
                R.w("tensor", wld)
                mm_group(PS[bk][:, :W], bk,
                         [(buf[:, k * 512 + ml * 128:k * 512 + (ml + 1) * 128],
                           xb[k][:, :]) for k in range(KD)])
                R.w("scalar", tt, tt.n)
                R.w("scalar", sg[mo % 2])
                R.do("scalar", "activation", gsil[mo % 2][:, :], PS[bk][:, :W],
                     AF.Silu, inc=st)
                bank_free[bk] = (st, st.n)
                R.w("sync", st, st.n)
                R.dma("sync", gate_sp[:, mo * TOK + c0:mo * TOK + c0 + W],
                      gsil[mo % 2][:, :], sg[mo % 2])
            piece_done(pidx)
        for mq in range(4):
            b0 = load_piece(wdt_b, 0 * 4 + mq)
            b1 = load_piece(wdt_b, 1 * 4 + mq)
            mq_tt = []
            for ml in range(4):
                mo = mq * 4 + ml
                bk = (0, 1, 6, 7)[ml]
                s0b, v0b = bank_free[bk]
                R.w("tensor", s0b, v0b)
                R.w("tensor", b0[1])
                R.w("tensor", b1[1])
                pairs = []
                for kk in range(KI):
                    buf = (b0 if kk < KD else b1)[0]
                    k = kk % KD
                    pairs.append(
                        (buf[:, k * 512 + ml * 128:k * 512 + (ml + 1) * 128],
                         zb[kk][:, :]))
                mm_group(PS[bk][:, :W], bk, pairs, guard=False)
                mq_tt.append(tt.n)
            piece_done(b0[2])
            piece_done(b1[2])
            for ml in range(4):
                mo = mq * 4 + ml
                bk = (0, 1, 6, 7)[ml]
                R.w("scalar", tt, mq_tt[ml])
                R.w("scalar", sv, sv.n)
                R.do("scalar", "activation", dtt[ml % 2][:, :], PS[bk][:, :W],
                     AF.Exp, bias=bdt_sb[:, mo:mo + 1], inc=st)
                R.do("scalar", "activation", dtt[ml % 2][:, :],
                     dtt[ml % 2][:, :], AF.Ln, bias=cone[:, 0:1], inc=st)
                bank_free[bk] = (st, st.n)
                R.w("vector", st, st.n)
                R.w("vector", sd[ml % 2])
                R.do("vector", "tensor_scalar", dtf[ml % 2][:, :],
                     dtt[ml % 2][:, :], 1e-3, 1.0, ALU.max, ALU.min)
                R.do("vector", "tensor_mul", ub[ml % 2][:, :], dtf[ml % 2][:, :],
                     zb[mo][:, :], inc=sv)
                R.w("sync", sv, sv.n)
                base = mo * GA + c2 * W
                R.dma("sync", dtu_in[g][:, base:base + W], dtf[ml % 2][:, :],
                      sd[ml % 2])
                R.dma("sync", dtu_in[g][:, KI * GA + base:KI * GA + base + W],
                      ub[ml % 2][:, :], sd[ml % 2])
        s0b, v0b = bank_free[5]
        R.w("tensor", s0b, v0b)
        R.w("tensor", sv, wbc_ready)
        for kk in range(KI):
            R.do("tensor", "matmul", PS[5][0:16, :W],
                 wb_sb[:, kk * S:(kk + 1) * S], zb[kk][:, :],
                 start=(kk == 0), stop=(kk == KI - 1))
        for kk in range(KI):
            R.do("tensor", "matmul", PS[5][32:48, :W],
                 wc_sb[:, kk * S:(kk + 1) * S], zb[kk][:, :],
                 start=(kk == 0), stop=(kk == KI - 1), tile_position=(0, 32),
                 inc=tt if kk == KI - 1 else None)
        R.w("vector", tt, tt.n)
        R.w("vector", sbcst)
        R.do("vector", "tensor_copy", bcst[0:16, :W], PS[5][0:16, :W])
        R.do("vector", "tensor_copy", bcst[32:48, :W], PS[5][32:48, :W], inc=sv)
        bank_free[5] = (sv, sv.n)
        R.w("sync", sv, sv.n)
        bco = 2 * KI * GA + c2 * W
        R.dma("sync", dtu_in[g][0:16, bco:bco + W], bcst[0:16, :W], sbcst)
        R.dma("sync", dtu_in[g][16:32, bco:bco + W], bcst[32:48, :W], sbcst)
        pre_ms.append({"sv": sv.n, "st": st.n, "tt": tt.n})

    scan_ms = []

    def scan_tg(tg, ag_sem):
        half = tg // NAG
        gg = tg % NAG
        eS = ENG_SCAN
        semS = pv if eS == "gpsimd" else sv
        src = dtu_out[gg]
        R.w("sync", ag_sem)
        if scan_ms:
            R.w("sync", sv, scan_ms[-1]["sv"])
            if eS == "gpsimd":
                R.w("sync", pv, scan_ms[-1]["pv"])
        bco = 2 * KI * GA
        for si in range(SH):
            R.w("sync", sv, sv.n)
            R.dma("sync", brow[0][:, :],
                  src[half, si:si + 1, bco:bco + GA], sbc)
            R.dma("sync", brow[1][:, :],
                  src[half, SH + si:SH + si + 1, bco:bco + GA], sbc)
            R.dma("sync", brow[2][:, :],
                  src[half, 16 + si:16 + si + 1, bco:bco + GA], sbc)
            R.dma("sync", brow[3][:, :],
                  src[half, 16 + SH + si:16 + SH + si + 1, bco:bco + GA], sbc)
            R.w("vector", sbc)
            R.do("vector", "tensor_single_scalar", bwork[0][:, :], brow[0][:, :],
                 hm0_sb[0:1, 0:1], ALU.mult)
            R.do("vector", "scalar_tensor_tensor", bwork[0][:, :], brow[1][:, :],
                 hmask_sb[0:1, 0:1], bwork[0][:, :], ALU.mult, ALU.add)
            R.do("vector", "tensor_single_scalar", bwork[1][:, :], brow[2][:, :],
                 hm0_sb[0:1, 0:1], ALU.mult)
            R.do("vector", "scalar_tensor_tensor", bwork[1][:, :], brow[3][:, :],
                 hmask_sb[0:1, 0:1], bwork[1][:, :], ALU.mult, ALU.add, inc=sv)
            R.w("sync", sv, sv.n)
            R.dma("sync", bsel_d[0:1, :], bwork[0][:, :], sbc)
            R.dma("sync", bsel_d[1:2, :], bwork[1][:, :], sbc)
            R.w("sync", sbc)
            R.dma("sync", Bbc[si][:, :],
                  bsel_d[0:1, :].to_broadcast((128, GA)), sbc)
            R.dma("sync", Cbc[si][:, :],
                  bsel_d[1:2, :].to_broadcast((128, GA)), sbc)
        tile_ms = []
        st_hist = {}

        def emit_exp(k, kb, si, scan_hist):
            col = k * SH + si
            R.w("scalar", sdu[kb])
            if si >= 2:
                R.w("scalar", sv, scan_hist[si - 2])
            R.do("scalar", "activation", dAr[si % 2][:, :], dts[kb][:, :],
                 AF.Exp, scale=A_sb[:, col:col + 1], inc=st, lag=2)
            st_hist[si] = st.n

        for k in range(KI):
            kb = k % 2
            if k >= 2:
                m = tile_ms[k - 2]
                R.w("sync", st, m["st"])
                R.w("sync", sv, m["sv"])
            R.dma("sync", dts[kb][:, :],
                  src[half, :, k * GA:(k + 1) * GA], sdu[kb])
            R.dma("sync", uts[kb][:, :],
                  src[half, :, KI * GA + k * GA:KI * GA + (k + 1) * GA],
                  sdu[kb])
            scan_hist = []

            def B(si, lag=4):
                R.w("vector", sdu[kb])
                R.w("vector", sbc)
                R.do("vector", "tensor_mul", burb[si % 4][:, :], uts[kb][:, :],
                     Bbc[si][:, :], inc=sv, lag=lag)

            def Sc(si, lag=2):
                emit_exp(k, kb, si, scan_hist)
                R.w("vector", st, st_hist[si])
                col = k * SH + si
                o = (si % 2) * GA
                R.do("vector", "tensor_tensor_scan", hbr2[:, o:o + GA],
                     dAr[si % 2][:, :], burb[si % 4][:, :],
                     carry[:, col:col + 1], ALU.mult, ALU.add, inc=sv, lag=lag)
                scan_hist.append(sv.n)

            def Cc(si, lag=4):
                col = k * SH + si
                o = (si % 2) * GA
                R.do("vector", "tensor_copy", carry[:, col:col + 1],
                     hbr2[:, o + GA - 1:o + GA], inc=sv, lag=lag)

            def H(j, lag=2):
                R.do("vector", "tensor_mul", hcp[j % 2][:, :], hbr2[:, :],
                     Cbc_all[:, 2 * j * GA:(2 * j + 2) * GA], inc=sv, lag=lag)

            def P(j, dst, lag=2):
                h = hcp[j % 2]
                R.do("vector", "tensor_add", dst[:, :], h[:, 0:GA],
                     h[:, GA:2 * GA], inc=sv, lag=lag)

            B(0)
            B(1)
            B(2)
            B(3)
            Sc(0, lag=4)
            Sc(1, lag=4)
            B(4, lag=2)
            H(0)
            Cc(0)
            Cc(1)
            B(5, lag=4)
            Sc(2, lag=3)
            Sc(3, lag=3)
            P(0, pp[0], lag=4)
            H(1, lag=2)
            Cc(2)
            Cc(3)
            B(6, lag=6)
            Sc(4, lag=3)
            B(7, lag=6)
            Sc(5, lag=2)
            P(1, pp[1], lag=6)
            H(0 + 2, lag=2)
            Cc(4)
            Cc(5)
            R.do("vector", "tensor_add", pp[2][:, :], pp[0][:, :],
                 pp[1][:, :], inc=sv, lag=4)
            Sc(6, lag=3)
            Sc(7, lag=2)
            P(2, pp[0], lag=3)
            H(1 + 2, lag=2)
            Cc(6)
            Cc(7)
            R.do("vector", "tensor_add", pp[1][:, :], pp[2][:, :],
                 pp[0][:, :], inc=sv, lag=4)
            P(3, hbr2[:, 0:GA], lag=2)
            if k >= 2:
                R.w("vector", ssy[tg])
            elif tg >= 1:
                R.w("vector", ssy[tg - 1])
            R.do("vector", "tensor_add", ysc[k % 2][:, :], pp[1][:, :],
                 hbr2[:, 0:GA], inc=sv, lag=1)
            if tg == NAG - 1:
                R.do("vector", "tensor_copy", y4095[:, k:k + 1],
                     ysc[k % 2][:, GA - 1:GA], inc=sv)
            R.w("sync", sv, sv.n)
            R.dma("sync", y_in[gg][half, :, k * GA:(k + 1) * GA],
                  ysc[k % 2][:, :], ssy[tg])
            tile_ms.append({"st": st.n, "sv": sv.n})
        scan_ms.append({"sv": sv.n, "pv": pv.n})

    ag_sems = {}
    yc_sem = [None]
    emitted = [0]

    R.w("sync", scs[0])
    R.w("sync", scs[1])

    def emit_ag(g):
        R.w("gpsimd", sd[0])
        R.w("gpsimd", sd[1])
        R.w("gpsimd", sbcst)
        s = next_cc()
        R.do("gpsimd", "collective_compute", "AllGather", ALU.bypass,
             GROUPS, [dtu_in[g].ap().opt()], [dtu_out[g].ap().opt()], inc=s)
        ag_sems[g] = s

    def emit_scan_next():
        tg = emitted[0]
        scan_tg(tg, ag_sems[tg % NAG])
        emitted[0] += 1
        if emitted[0] == NAG:
            R.w("sync", sv, scan_ms[-1]["sv"])
            R.dma("sync", yc_in[:, :], y4095[:, :], syc)
            R.w("gpsimd", syc)
            s = next_cc()
            R.do("gpsimd", "collective_compute", "AllGather", ALU.bypass,
                 GROUPS, [yc_in.ap().opt()], [yc_out.ap().opt()], inc=s)
            yc_sem[0] = s

    for j in range(NCH):
        pre_chunk(j)
        if j % AGC == AGC - 1:
            emit_ag(j // AGC)
            if j // AGC >= 1:
                emit_scan_next()
    while emitted[0] < NTG:
        emit_scan_next()

    rs_sems = {}
    for g in range(NAG):
        R.w("gpsimd", ssy[g])
        R.w("gpsimd", ssy[NAG + g])
        s = next_cc()
        R.do("gpsimd", "collective_compute", "ReduceScatter", ALU.add,
             GROUPS, [y_in[g].ap().opt()], [y_out[g].ap().opt()], inc=s)
        rs_sems[g] = s

    AF32.pop()
    ABF.pop()

    AF32.push()
    ABF.push()
    y2f = AF32.tiles(KD, W)
    tmpe = AF32.tiles(2, W)
    r12e = AF32.tiles(2, W)
    rv3 = AF32.tiles(2, W)
    outf = AF32.tiles(2, W)
    r12col = AF32.alloc(KD)
    rhl = AF32.alloc(KD)
    yca = AF32.tiles(2, KI)
    blockE = ABF.tiles(KI, W)
    ytl = ABF.tiles(2, W)
    gte = ABF.tiles(2, W)
    y2q = ABF.tiles(2, W)
    xn3b = [ABF.alloc(1 + W) for _ in range(KD)]
    xr3 = ABF.tiles(KD, W)
    xk3 = ABF.tiles(KD, W)
    mt3 = ABF.tiles(2, W)
    gcol = ABF.alloc(KI)
    ygcol = ABF.alloc(KI)
    hxb3 = ABF.alloc(KD)
    hxq3 = ABF.alloc(KD)
    ygb = blockE
    y2b = blockE[0:KD]
    rb3 = blockE[KD:KI]
    ka3 = xr3

    R.w("sync", yc_sem[0])
    R.dma("sync", yca[0][:, :], yc_out[0], sL)
    R.dma("sync", yca[1][:, :], yc_out[1], sL)
    R.w("vector", sL)
    R.do("vector", "tensor_add", y4095[:, :], yca[0][:, :], yca[1][:, :],
         inc=sv)
    yca_done = sv.n
    R.w("sync", sv, yca_done)
    R.w("sync", sg[0])
    R.w("sync", sg[1])
    R.w("sync", sr12)
    for kk in range(KI):
        R.dma("sync", gcol[:, kk:kk + 1],
              gate_sp[:, kk * TOK + TOK - 1:kk * TOK + TOK], sL, nc_ok=True)
    R.w("vector", sL)
    R.do("vector", "tensor_mul", ygcol[:, :], y4095[:, :], gcol[:, :], inc=sv)
    yg_col_v = sv.n
    s0b, v0b = bank_free[5]
    R.w("tensor", s0b, v0b)
    R.w("tensor", sv, yg_col_v)
    first = True
    for pz in range(4):
        mh, kh = pz // 2, pz % 2
        buf, wld, pidx = load_piece(wout_b, pz)
        R.w("tensor", wld)
        for ml in range(4):
            mo = mh * 4 + ml
            for k in range(KD):
                kk = kh * KD + k
                last = (pz == 3 and ml == 3 and k == KD - 1)
                piece_last = (ml == 3 and k == KD - 1)
                R.do("tensor", "matmul", PS[5][:, mo:mo + 1],
                     buf[:, k * 512 + ml * 128:k * 512 + (ml + 1) * 128],
                     ygcol[:, kk:kk + 1],
                     start=first, stop=last, inc=tt if piece_last else None)
                first = False
        piece_done(pidx)
    R.w("sync", sv, yg_col_v)
    for k in range(KD):
        R.dma("sync", r12col[:, k:k + 1],
              r12_sp[:, k * TOK + TOK - 1:k * TOK + TOK], sL, nc_ok=True)
    R.w("vector", sL)
    R.w("vector", tt, tt.n)
    R.do("vector", "tensor_add", r12col[:, :], r12col[:, :], PS[5][:, 0:KD],
         inc=sv)
    bank_free[5] = (sv, sv.n)
    R.w("scalar", sv, sv.n)
    mini_ln_cols(r12col, hxb3, hxq3, KD)
    R.do("vector", "tensor_single_scalar", r12col[:, :], r12col[:, :],
         PS[5][:, 0:1], ALU.subtract)
    R.do("vector", "tensor_single_scalar", r12col[:, :], r12col[:, :],
         PS[5][:, 1:2], ALU.mult, inc=sv)
    bank_free[5] = (sv, sv.n)
    R.w("sync", sv, sv.n)
    R.dma("sync", rh_in[:, :], r12col[:, :], srh)
    R.w("gpsimd", srh)
    rh_sem = next_cc()
    R.do("gpsimd", "collective_compute", "AllGather", ALU.bypass,
         GROUPS, [rh_in.ap().opt()], [rh_out.ap().opt()], inc=rh_sem)
    R.w("sync", rh_sem)
    R.dma("sync", rhl[:, :], rh_out[0], sL)
    R.w("vector", sL)
    R.do("vector", "tensor_single_scalar", xn3_halo[:, :], rhl[:, :],
         hmask_sb[:, 0:1], ALU.mult, inc=sv)
    halo3_done = sv.n

    epi_ms = []

    def epi_chunk(e):
        c0 = e * W
        g = e // AGC
        prev = epi_ms[-1] if epi_ms else None
        R.w("sync", rs_sems[g])
        if prev is None:
            R.w("sync", sg[0])
            R.w("sync", sg[1])
            R.w("sync", sr12)
        if prev is not None:
            R.w("sync", sv, prev["sv"])
            R.w("sync", st, prev["st"])
        else:
            R.w("sync", sv, halo3_done)
        yg_st = []
        for kk in range(KI):
            if kk >= 2:
                R.w("sync", sv, yg_st[kk - 2])
            R.dma("sync", ytl[kk % 2][:, :],
                  y_out[g][:, kk * GA + (e % AGC) * W:
                           kk * GA + (e % AGC + 1) * W], sey[kk % 2])
            R.dma("sync", gte[kk % 2][:, :],
                  gate_sp[:, kk * TOK + c0:kk * TOK + c0 + W], sey[kk % 2])
            R.w("vector", sey[kk % 2])
            R.do("vector", "tensor_mul", ygb[kk][:, :], ytl[kk % 2][:, :],
                 gte[kk % 2][:, :], inc=sv)
            yg_st.append(sv.n)
        yg_done = sv.n
        for mh in range(2):
            bA = load_piece(wout_b, mh * 2 + 0)
            bX = load_piece(wout_b, mh * 2 + 1)
            for ml in range(4):
                mo = mh * 4 + ml
                bk = (0, 1, 6)[mo % 3]
                s0b, v0b = bank_free[bk]
                R.w("tensor", s0b, v0b)
                R.w("tensor", bA[1])
                R.w("tensor", bX[1])
                R.w("tensor", sv, yg_done)
                pairs = []
                for kk in range(KI):
                    buf = (bA if kk < KD else bX)[0]
                    k = kk % KD
                    pairs.append(
                        (buf[:, k * 512 + ml * 128:k * 512 + (ml + 1) * 128],
                         ygb[kk][:, :]))
                mm_group(PS[bk][:, :W], bk, pairs, guard=False)
                if mo >= 2:
                    R.w("sync", sv, sv.n)
                R.dma("sync", r12e[mo % 2][:, :],
                      r12_sp[:, mo * TOK + c0:mo * TOK + c0 + W], ser[mo % 2])
                R.w("vector", tt, tt.n)
                R.w("vector", ser[mo % 2])
                R.do("vector", "tensor_add", y2f[mo][:, :], PS[bk][:, :W],
                     r12e[mo % 2][:, :], inc=sv)
                bank_free[bk] = (sv, sv.n)
            piece_done(bA[2])
            piece_done(bX[2])
        y2_done = sv.n

        def cast_y2(k):
            R.w("scalar", sv, y2_done)
            R.do("scalar", "activation", y2b[k][:, :], y2f[k][:, :], AF.Copy,
                 inc=st)
            R.do("scalar", "activation", y2q[k % 2][:, :], y2f[k][:, :],
                 AF.Square, inc=st)
            return y2b[k][:, :], y2q[k % 2][:, :]

        ln_stats(cast_y2, W)
        if e == 0:
            R.w("vector", sv, halo3_done)
            for k in range(KD):
                R.do("vector", "tensor_copy", xn3b[k][:, 0:1],
                     xn3_halo[:, k:k + 1])
        else:
            for k in range(KD):
                R.do("vector", "tensor_copy", xn3b[k][:, 0:1],
                     xn3b[k][:, W:W + 1])
        for k in range(KD):
            R.do("vector", "tensor_sub", tmpe[k % 2][:, :], y2f[k][:, :],
                 PS[3][:, :W])
            R.do("vector", "tensor_mul", xn3b[k][:, 1:1 + W], tmpe[k % 2][:, :],
                 PS[4][:, :W], inc=sv if k == KD - 1 else None)
        bank_free[3] = (sv, sv.n)
        bank_free[4] = (sv, sv.n)
        for k in range(KD):
            cur = xn3b[k][:, 1:1 + W]
            prv = xn3b[k][:, 0:W]
            R.do("vector", "tensor_sub", mt3[k % 2][:, :], cur, prv)
            R.do("vector", "scalar_tensor_tensor", xr3[k][:, :], mt3[k % 2][:, :],
                 mur_sb[:, k:k + 1], prv, ALU.mult, ALU.add)
            R.do("vector", "scalar_tensor_tensor", xk3[k][:, :], mt3[k % 2][:, :],
                 muk_sb[:, k:k + 1], prv, ALU.mult, ALU.add,
                 inc=sv if k == KD - 1 else None)
        mixes = sv.n
        for mh in range(2):
            buf, wld, pidx = load_piece(wr_b, mh)
            for ml in range(4):
                mo = mh * 4 + ml
                bk = (0, 1, 6)[mo % 3]
                R.w("tensor", sv, mixes)
                R.w("tensor", wld)
                mm_group(PS[bk][:, :W], bk,
                         [(buf[:, k * 512 + ml * 128:k * 512 + (ml + 1) * 128],
                           xr3[k][:, :]) for k in range(KD)])
                R.w("scalar", tt, tt.n)
                R.do("scalar", "activation", rb3[mo % KD][:, :], PS[bk][:, :W],
                     AF.Sigmoid, inc=st)
                bank_free[bk] = (st, st.n)
            piece_done(pidx)
        for mh in range(2):
            buf, wld, pidx = load_piece(wk_b, mh)
            for ml in range(4):
                mo = mh * 4 + ml
                bk = (0, 1, 6)[mo % 3]
                R.w("tensor", wld)
                mm_group(PS[bk][:, :W], bk,
                         [(buf[:, k * 512 + ml * 128:k * 512 + (ml + 1) * 128],
                           xk3[k][:, :]) for k in range(KD)])
                R.w("scalar", tt, tt.n)
                R.do("scalar", "activation", y2q[0][:, :], PS[bk][:, :W],
                     AF.Relu, inc=st)
                bank_free[bk] = (st, st.n)
                R.do("scalar", "activation", ka3[mo][:, :], y2q[0][:, :],
                     AF.Square, inc=st)
            piece_done(pidx)
        ka_done = st.n
        for mh in range(2):
            buf, wld, pidx = load_piece(wv_b, mh)
            for ml in range(4):
                mo = mh * 4 + ml
                bk = (0, 1, 6)[mo % 3]
                R.w("tensor", st, ka_done)
                R.w("tensor", wld)
                mm_group(PS[bk][:, :W], bk,
                         [(buf[:, k * 512 + ml * 128:k * 512 + (ml + 1) * 128],
                           ka3[k][:, :]) for k in range(KD)])
                R.w("vector", tt, tt.n)
                R.w("vector", sout[mo % 2])
                R.do("vector", "tensor_mul", rv3[mo % 2][:, :],
                     rb3[mo % KD][:, :], PS[bk][:, :W])
                R.do("vector", "scalar_tensor_tensor", outf[mo % 2][:, :],
                     rv3[mo % 2][:, :], 0.5, y2f[mo][:, :], ALU.mult, ALU.add,
                     inc=sv)
                bank_free[bk] = (sv, sv.n)
                R.w("sync", sv, sv.n)
                R.dma("sync", outT[mo * 128:(mo + 1) * 128, c0:c0 + W],
                      outf[mo % 2][:, :], sout[mo % 2])
            piece_done(pidx)
        epi_ms.append({"sv": sv.n, "st": st.n})

    for e in range(NCH):
        epi_chunk(e)

    AF32.pop()
    ABF.pop()

    R.w("gpsimd", sout[0])
    R.w("gpsimd", sout[1])

    with ctx:
        with nc.Block() as block:
            @block.sync
            def _(e):
                for f in R.q["sync"]:
                    f(e)

            @block.gpsimd
            def _(e):
                for f in R.q["gpsimd"]:
                    f(e)

            @block.scalar
            def _(e):
                for f in R.q["scalar"]:
                    f(e)

            @block.vector
            def _(e):
                for f in R.q["vector"]:
                    f(e)

            @block.tensor
            def _(e):
                for f in R.q["tensor"]:
                    f(e)

    return nc


_NC_CACHE = {}


def _get_nc():
    key = (T, TCP)
    if key not in _NC_CACHE:
        _NC_CACHE[key] = build()
    return _NC_CACHE[key]


def kernel(
    L0_attn_x, L0_attn_norm_w, L0_attn_norm_b, L0_attn_W_in, L0_attn_W_gate,
    L0_attn_W_dt, L0_attn_b_dt, L0_attn_W_B, L0_attn_W_C, L0_attn_A_log,
    L0_attn_W_out, L0_ffn_W_r, L0_ffn_W_k, L0_ffn_W_v, L0_ffn_mu_r, L0_ffn_mu_k,
):
    f32 = np.float32
    x = np.asarray(L0_attn_x, f32)
    shared = {
        "wrT": np.ascontiguousarray(np.asarray(L0_ffn_W_r, f32).T),
        "wkT": np.ascontiguousarray(np.asarray(L0_ffn_W_k, f32).T),
        "wvT": np.ascontiguousarray(np.asarray(L0_ffn_W_v, f32).T),
        "win": np.ascontiguousarray(np.asarray(L0_attn_W_in, f32)),
        "wgate": np.ascontiguousarray(np.asarray(L0_attn_W_gate, f32)),
        "wdt": np.ascontiguousarray(np.asarray(L0_attn_W_dt, f32)),
        "wb16": np.ascontiguousarray(np.asarray(L0_attn_W_B, f32)),
        "wc16": np.ascontiguousarray(np.asarray(L0_attn_W_C, f32)),
        "wout": np.ascontiguousarray(np.asarray(L0_attn_W_out, f32)),
        "bdt": np.ascontiguousarray(np.asarray(L0_attn_b_dt, f32).reshape(KI, 128).T),
        "mur": np.ascontiguousarray(np.asarray(L0_ffn_mu_r, f32).reshape(KD, 128).T),
        "muk": np.ascontiguousarray(np.asarray(L0_ffn_mu_k, f32).reshape(KD, 128).T),
        "nw": np.ascontiguousarray(np.asarray(L0_attn_norm_w, f32).reshape(KD, 128).T),
        "nb": np.ascontiguousarray(np.asarray(L0_attn_norm_b, f32).reshape(KD, 128).T),
    }
    alog = np.asarray(L0_attn_A_log, f32)
    in_maps = []
    for c in range(NC):
        b, h = c // 2, c % 2
        xt = np.ascontiguousarray(x[b].T)
        halo = (np.zeros((D, 1), f32) if h == 0
                else np.ascontiguousarray(xt[:, TOK - 1:TOK]))
        m = dict(shared)
        m["xT"] = np.ascontiguousarray(
            np.concatenate([halo, xt[:, h * TOK:(h + 1) * TOK]], axis=1))
        asl = alog[:, h * SH:(h + 1) * SH].reshape(KI, 128, SH)
        m["alog"] = np.ascontiguousarray(
            asl.transpose(1, 0, 2).reshape(128, KI * SH))
        m["hmask"] = np.full((128, 1), float(h), f32)
        in_maps.append(m)

    nc = _get_nc()
    res = run_bass_kernel_spmd(nc, in_maps, core_ids=list(range(NC)))
    out = np.empty((B, T, D), f32)
    for c in range(NC):
        b, h = c // 2, c % 2
        out[b, h * TOK:(h + 1) * TOK, :] = np.asarray(res.results[c]["outT"]).T
    return out


# version 06
# speedup vs baseline: 1.0932x; 1.0037x over previous
import numpy as np
from contextlib import ExitStack

import concourse.bass as bass
import concourse.mybir as mybir
from concourse.bass_utils import run_bass_kernel_spmd

F32 = mybir.dt.float32
BF16 = mybir.dt.bfloat16
AF = mybir.ActivationFunctionType
ALU = mybir.AluOpType
AX = mybir.AxisListType

B, T, D, DI, S = 4, 8192, 1024, 2048, 16
NC = 8
KD = D // 128
KI = DI // 128
SH = S // 2
LNEPS = 1e-5

TCP = 512
AGC = 2

ENG_SCAN = "vector"

ENGS = ("sync", "gpsimd", "scalar", "vector", "tensor")
GROUPS = [[0, 1], [2, 3], [4, 5], [6, 7]]


def _derive():
    global TOK, NCH, GA, NAG, NTG, DTU_COLS
    TOK = T // 2
    NCH = TOK // TCP
    GA = AGC * TCP
    NAG = NCH // AGC
    NTG = T // GA
    DTU_COLS = 2 * KI * GA + GA


_derive()


class Sem:
    __slots__ = ("h", "n")


class Rec:
    def __init__(self, nc, ctx):
        self.nc = nc
        self.ctx = ctx
        self.q = {e: [] for e in ENGS}
        self.auto = {}

    def sem(self, name):
        s = Sem()
        s.h = self.ctx.enter_context(self.nc.semaphore(name))
        s.n = 0
        return s

    def w(self, eng, sem, val=None):
        v = sem.n if val is None else val
        assert v <= sem.n, f"wait on future semaphore value {v} > {sem.n}"
        if v <= 0:
            return
        self.q[eng].append(lambda e, h=sem.h, v=v: e.wait_ge(h, v))

    def do(self, eng, meth, *args, inc=None, inck=1, lag=1, **kw):
        s = self.auto.get(eng) if meth not in ("dma_start",
                                               "collective_compute") else None
        if s is not None and s.n - (lag - 1) > 0:
            self.w(eng, s, s.n - (lag - 1))
        if s is not None and inc is None:
            inc = s
        if inc is not None:
            inc.n += inck

        def f(e, meth=meth, args=args, kw=kw, inc=inc, inck=inck):
            ins = getattr(e, meth)(*args, **kw)
            if inc is not None:
                ins.then_inc(inc.h, inck)

        self.q[eng].append(f)

    def dma(self, eng, out, in_, inc, nc_ok=False):
        kw = {"allow_slow_non_contiguous": True} if nc_ok else {}
        self.do(eng, "dma_start", inc=inc, inck=16, out=out, in_=in_, **kw)


class Arena:
    def __init__(self, handle, nelem, gran):
        self.h = handle
        self.ne = nelem
        self.off = 0
        self.gran = gran
        self.marks = []

    def alloc(self, n):
        nr = -(-n // self.gran) * self.gran
        assert self.off + nr <= self.ne, f"arena overflow {self.off}+{nr}>{self.ne}"
        ap = self.h[:, self.off: self.off + n]
        self.off += nr
        return ap

    def tiles(self, k, w):
        return [self.alloc(w) for _ in range(k)]

    def push(self):
        self.marks.append(self.off)

    def pop(self):
        self.off = self.marks.pop()


def build():
    nc = bass.Bass("TRN2", target_bir_lowering=False)
    ctx = ExitStack()
    W = TCP

    din = {}
    for name, shape in [
        ("xT", [D, 1 + TOK]),
        ("wrT", [D, D]), ("wkT", [D, D]), ("wvT", [D, D]),
        ("win", [D, DI]), ("wgate", [D, DI]), ("wdt", [DI, DI]),
        ("wb16", [DI, S]), ("wc16", [DI, S]), ("wout", [DI, D]),
        ("bdt", [128, KI]), ("mur", [128, KD]), ("muk", [128, KD]),
        ("nw", [128, KD]), ("nb", [128, KD]),
        ("alog", [128, KI * SH]), ("hmask", [128, 1]),
    ]:
        din[name] = nc.dram_tensor(name, shape, F32, kind="ExternalInput")
    outT = nc.dram_tensor("outT", [D, TOK], F32, kind="ExternalOutput")

    wr_b = nc.dram_tensor("wr_b", [128, 2 * KD * 512], BF16)
    wk_b = nc.dram_tensor("wk_b", [128, 2 * KD * 512], BF16)
    wv_b = nc.dram_tensor("wv_b", [128, 2 * KD * 512], BF16)
    win_b = nc.dram_tensor("win_b", [128, 4 * KD * 512], BF16)
    wgate_b = nc.dram_tensor("wgate_b", [128, 4 * KD * 512], BF16)
    wdt_b = nc.dram_tensor("wdt_b", [128, 8 * KD * 512], BF16)
    wout_b = nc.dram_tensor("wout_b", [128, 4 * KD * 512], BF16)

    r12_sp = nc.dram_tensor("r12_sp", [128, KD * TOK], F32)
    gate_sp = nc.dram_tensor("gate_sp", [128, KI * TOK], BF16)

    dtu_in = [nc.dram_tensor(f"dtu_in{g}", [128, DTU_COLS], BF16) for g in range(NAG)]
    dtu_out = [nc.dram_tensor(f"dtu_out{g}", [2, 128, DTU_COLS], BF16)
               for g in range(NAG)]
    y_in = [nc.dram_tensor(f"y_in{g}", [2, 128, KI * GA], BF16) for g in range(NAG)]
    y_out = [nc.dram_tensor(f"y_out{g}", [128, KI * GA], BF16) for g in range(NAG)]
    yc_in = nc.dram_tensor("yc_in", [128, KI], F32)
    yc_out = nc.dram_tensor("yc_out", [2, 128, KI], F32)
    rh_in = nc.dram_tensor("rh_in", [128, KD], F32)
    bsel_d = nc.dram_tensor("bsel_d", [2 * SH, GA], BF16)
    rh_out = nc.dram_tensor("rh_out", [2, 128, KD], F32)

    sb = lambda nm, shape, dt: ctx.enter_context(nc.sbuf_tensor(nm, shape, dt))
    AF32 = Arena(sb("af32", [128, 8600], F32), 8600, 4)
    ABF = Arena(sb("abf", [128, 26400], BF16), 26400, 16)
    sA = sb("sA", [128, KD * 512], BF16)
    sB = sb("sB", [128, KD * 512], BF16)
    wb_sb = sb("wb_sb", [128, KI * S], BF16)
    wc_sb = sb("wc_sb", [128, KI * S], BF16)
    bdt_sb = sb("bdt_sb", [128, KI], F32)
    mur_sb = sb("mur_sb", [128, KD], F32)
    muk_sb = sb("muk_sb", [128, KD], F32)
    nw_sb = sb("nw_sb", [128, KD], F32)
    nb_sb = sb("nb_sb", [128, KD], F32)
    A_sb = sb("A_sb", [128, KI * SH], F32)
    hmask_sb = sb("hmask_sb", [128, 1], F32)
    ceps = sb("ceps", [128, 1], F32)
    cone = sb("cone", [128, 1], F32)
    ones_cb = sb("ones_cb", [128, 1], BF16)
    ones_r = sb("ones_r", [1, 128], F32)
    carry = sb("carry", [128, KI * SH], F32)
    row_mean = sb("row_mean", [1, 512], F32)
    row_msq = sb("row_msq", [1, 512], F32)
    row_tmp = sb("row_tmp", [1, 512], F32)
    row_rstd = sb("row_rstd", [1, 512], F32)
    row_sd = sb("row_sd", [1, 512], F32)
    xn0_halo = sb("xn0_halo", [128, KD], BF16)
    xn3_halo = sb("xn3_halo", [128, KD], F32)
    y4095 = sb("y4095", [128, KI], F32)
    dts = [sb(f"dts{i}", [128, GA], BF16) for i in range(2)]
    uts = [sb(f"uts{i}", [128, GA], BF16) for i in range(2)]
    Bbc_all = sb("Bbc_all", [128, SH * GA], BF16)
    Cbc_all = sb("Cbc_all", [128, SH * GA], BF16)
    Bbc = [Bbc_all[:, i * GA:(i + 1) * GA] for i in range(SH)]
    Cbc = [Cbc_all[:, i * GA:(i + 1) * GA] for i in range(SH)]
    dAr = [sb(f"dAr{i}", [128, GA], F32) for i in range(2)]
    hbr2 = sb("hbr2", [128, 2 * GA], BF16)
    burb = [sb(f"burb{i}", [128, GA], BF16) for i in range(4)]
    hcp = [sb(f"hcp{i}", [128, 2 * GA], BF16) for i in range(2)]
    pp = [sb(f"pp{i}", [128, GA], BF16) for i in range(3)]
    ysc = [sb(f"ysc{i}", [128, GA], BF16) for i in range(2)]
    zero5 = sb("zero5", [128, 512], BF16)
    hm0_sb = sb("hm0_sb", [128, 1], F32)
    brow = [sb(f"brow{i}", [SH, GA], BF16) for i in range(4)]
    bwork = [sb(f"bwork{i}", [SH, GA], BF16) for i in range(2)]

    PS = [ctx.enter_context(nc.psum_tensor(f"ps{i}", [128, 512], F32))
          for i in range(8)]

    R = Rec(nc, ctx)
    st = R.sem("st")
    sv = R.sem("sv")
    pv = R.sem("pv")
    tt = R.sem("tt")
    sL = R.sem("sL")
    scf = [R.sem(f"scf{i}") for i in range(2)]
    scs = [R.sem(f"scs{i}") for i in range(2)]
    sx = [R.sem(f"sx{i}") for i in range(2)]
    sw = [R.sem(f"sw{i}") for i in range(2)]
    sr12 = R.sem("sr12")
    sg = [R.sem(f"sg{i}") for i in range(2)]
    sd = [R.sem(f"sd{i}") for i in range(2)]
    sbcst = R.sem("sbcst")
    sbc = R.sem("sbc")
    sdu = [R.sem(f"sdu{i}") for i in range(2)]
    ssy = [R.sem(f"ssy{i}") for i in range(NTG)]
    syc = R.sem("syc")
    srh = R.sem("srh")
    sey = [R.sem(f"sey{i}") for i in range(2)]
    ser = [R.sem(f"ser{i}") for i in range(2)]
    sout = [R.sem(f"sout{i}") for i in range(2)]
    ccs = [R.sem(f"cc{i}") for i in range(2 * NAG + 3)]
    cc_n = [0]

    def next_cc():
        s = ccs[cc_n[0]]
        cc_n[0] += 1
        return s

    R.auto = {"vector": sv, "scalar": st, "gpsimd": pv}
    bank_free = [(sv, 0)] * 8

    def mm_group(bank_ap, bank_idx, pairs, guard=True):
        if guard:
            s0, v0 = bank_free[bank_idx]
            R.w("tensor", s0, v0)
        for i, (lh, rh) in enumerate(pairs):
            R.do("tensor", "matmul", bank_ap, lh, rh,
                 start=(i == 0), stop=(i == len(pairs) - 1),
                 inc=tt if i == len(pairs) - 1 else None)

    def ln_stats(cast_fn, w, kd=KD):
        s0, v0 = bank_free[2]
        ttv = []
        for k in range(kd):
            if k >= 2:
                R.w("scalar", tt, ttv[k - 2])
            vb, qb = cast_fn(k)
            if k == 0:
                R.w("tensor", s0, v0)
            R.w("tensor", st, st.n)
            R.do("tensor", "matmul", PS[2][0:1, :w], ones_cb[:, :], vb,
                 start=(k == 0), stop=(k == kd - 1))
            R.do("tensor", "matmul", PS[2][32:33, :w], ones_cb[:, :], qb,
                 start=(k == 0), stop=(k == kd - 1), tile_position=(0, 32),
                 inc=tt)
            ttv.append(tt.n)
        R.w("vector", tt, tt.n)
        inv = 1.0 / (kd * 128)
        R.do("vector", "tensor_scalar_mul", row_mean[:, :w], PS[2][0:1, :w], inv)
        R.do("vector", "tensor_scalar_mul", row_msq[:, :w], PS[2][32:33, :w], inv)
        R.do("vector", "tensor_mul", row_tmp[:, :w], row_mean[:, :w],
             row_mean[:, :w])
        R.do("vector", "tensor_sub", row_tmp[:, :w], row_msq[:, :w],
             row_tmp[:, :w], inc=sv)
        bank_free[2] = (sv, sv.n)
        R.w("scalar", sv, sv.n)
        R.do("scalar", "activation", row_sd[:, :w], row_tmp[:, :w], AF.Sqrt,
             bias=ceps[0:1, 0:1], inc=st)
        R.w("vector", st, st.n)
        R.do("vector", "reciprocal", row_rstd[:, :w], row_sd[:, :w], inc=sv)
        for bank, row in ((3, row_mean), (4, row_rstd)):
            s0, v0 = bank_free[bank]
            R.w("tensor", s0, v0)
            R.w("tensor", sv, sv.n)
            R.do("tensor", "matmul", PS[bank][:, :w], ones_r[:, :], row[:, :w],
                 start=True, stop=True, inc=tt)
        R.w("vector", tt, tt.n)

    def mini_ln_cols(vals_f32, vb, qb, kd):
        R.do("scalar", "activation", vb[:, :], vals_f32[:, :], AF.Copy, inc=st)
        R.do("scalar", "activation", qb[:, :], vals_f32[:, :], AF.Square, inc=st)
        s0b, v0b = bank_free[5]
        R.w("tensor", s0b, v0b)
        R.w("tensor", st, st.n)
        R.do("tensor", "matmul", PS[5][0:1, 0:kd], ones_cb[:, :], vb[:, :],
             start=True, stop=True)
        R.do("tensor", "matmul", PS[5][32:33, 0:kd], ones_cb[:, :], qb[:, :],
             start=True, stop=True, tile_position=(0, 32), inc=tt)
        R.w("vector", tt, tt.n)
        R.do("vector", "tensor_reduce", row_tmp[0:1, 8:9], PS[5][0:1, 0:kd],
             AX.X, ALU.add)
        R.do("vector", "tensor_reduce", row_tmp[0:1, 9:10],
             PS[5][32:33, 0:kd], AX.X, ALU.add)
        R.do("vector", "tensor_scalar_mul", row_tmp[0:1, 8:9],
             row_tmp[0:1, 8:9], 1.0 / D)
        R.do("vector", "tensor_scalar_mul", row_tmp[0:1, 9:10],
             row_tmp[0:1, 9:10], 1.0 / D)
        R.do("vector", "tensor_mul", row_tmp[0:1, 10:11], row_tmp[0:1, 8:9],
             row_tmp[0:1, 8:9])
        R.do("vector", "tensor_sub", row_tmp[0:1, 9:10], row_tmp[0:1, 9:10],
             row_tmp[0:1, 10:11], inc=sv)
        bank_free[5] = (sv, sv.n)
        R.w("scalar", sv, sv.n)
        R.do("scalar", "activation", row_tmp[0:1, 12:13], row_tmp[0:1, 9:10],
             AF.Sqrt, bias=ceps[0:1, 0:1], inc=st)
        R.w("vector", st, st.n)
        R.do("vector", "reciprocal", row_tmp[0:1, 11:12], row_tmp[0:1, 12:13],
             inc=sv)
        s0b, v0b = bank_free[5]
        R.w("tensor", s0b, v0b)
        R.w("tensor", sv, sv.n)
        R.do("tensor", "matmul", PS[5][:, 0:1], ones_r[:, :],
             row_tmp[0:1, 8:9], start=True, stop=True)
        R.do("tensor", "matmul", PS[5][:, 1:2], ones_r[:, :],
             row_tmp[0:1, 11:12], start=True, stop=True, inc=tt)
        R.w("vector", tt, tt.n)

    for name, tile in [
        ("bdt", bdt_sb), ("mur", mur_sb), ("muk", muk_sb), ("nw", nw_sb),
        ("nb", nb_sb), ("hmask", hmask_sb),
    ]:
        R.dma("sync", tile[:, :], din[name][:, :], sL)
    R.do("vector", "memset", ceps[:, :], LNEPS)
    R.do("vector", "memset", cone[:, :], 1.0)
    R.do("vector", "memset", ones_cb[:, :], 1.0)
    R.do("vector", "memset", ones_r[:, :], 1.0)
    R.do("vector", "memset", carry[:, :], 0.0)
    R.do("vector", "memset", zero5[:, :], 0.0)
    R.w("vector", sL)
    R.do("vector", "tensor_scalar", hm0_sb[:, :], hmask_sb[:, :], -1.0, 1.0,
         ALU.mult, ALU.add)
    R.do("vector", "memset", xn3_halo[:, :], 0.0, inc=sv)
    R.w("sync", sv, sv.n)
    for _g in range(NAG):
        for _c2 in range(AGC):
            _bco = 2 * KI * GA + _c2 * TCP
            R.dma("sync", dtu_in[_g][32:128, _bco:_bco + TCP],
                  zero5[0:96, :TCP], sbcst)
    AF32.push()
    atmp = AF32.alloc(KI * SH)
    R.dma("sync", atmp[:, :], din["alog"][:, :], sL)
    R.w("scalar", sL)
    R.do("scalar", "activation", A_sb[:, :], atmp[:, :], AF.Exp, inc=st)
    R.w("vector", st, st.n)
    R.do("vector", "tensor_scalar", A_sb[:, :], A_sb[:, :], 1e-6, 1.0, ALU.max,
         ALU.min)
    R.do("vector", "tensor_scalar_mul", A_sb[:, :], A_sb[:, :], -1.0, inc=sv)
    AF32.pop()

    R.w("sync", st, st.n)
    AF32.push()
    ABF.push()
    cast_f = AF32.tiles(2, 512)
    cast_b = ABF.tiles(2, 512)
    cast_sv = []
    cast_n = [0]

    def cast_tile(src_ap, dst_ap):
        i = cast_n[0] % 2
        if cast_n[0] >= 2:
            R.w("sync", sv, cast_sv[cast_n[0] - 2])
        R.dma("sync", cast_f[i][:, :512], src_ap, scf[i])
        R.w("vector", scf[i])
        R.w("vector", scs[i])
        R.do("vector", "tensor_copy", cast_b[i][:, :512], cast_f[i][:, :512],
             inc=sv)
        cast_sv.append(sv.n)
        R.w("sync", sv, sv.n)
        R.dma("sync", dst_ap, cast_b[i][:, :512], scs[i])
        cast_n[0] += 1

    def cast_w(src, dst, n_k, n_pieces):
        for p in range(n_pieces):
            for k in range(n_k):
                cast_tile(src[k * 128:(k + 1) * 128, p * 512:(p + 1) * 512],
                          dst[:, (p * n_k + k) * 512:(p * n_k + k + 1) * 512])

    cast_w(din["wrT"], wr_b, KD, 2)
    cast_w(din["wkT"], wk_b, KD, 2)
    cast_w(din["wvT"], wv_b, KD, 2)
    cast_w(din["win"], win_b, KD, 4)
    cast_w(din["wgate"], wgate_b, KD, 4)
    cast_w(din["wout"], wout_b, KI, 2)
    for kh in range(2):
        for mq in range(4):
            for k in range(KD):
                p = kh * 4 + mq
                cast_tile(
                    din["wdt"][(kh * KD + k) * 128:(kh * KD + k + 1) * 128,
                               mq * 512:(mq + 1) * 512],
                    wdt_b[:, (p * KD + k) * 512:(p * KD + k + 1) * 512])
    for src, dst in [(din["wb16"], wb_sb), (din["wc16"], wc_sb)]:
        for k in range(KI):
            i = cast_n[0] % 2
            R.w("sync", sv, cast_sv[cast_n[0] - 2])
            R.dma("sync", cast_f[i][:, :S], src[k * 128:(k + 1) * 128, :],
                  scf[i])
            R.w("vector", scf[i])
            R.do("vector", "tensor_copy", dst[:, k * S:(k + 1) * S],
                 cast_f[i][:, :S], inc=sv)
            cast_sv.append(sv.n)
            cast_n[0] += 1
    wbc_ready = sv.n
    AF32.pop()
    ABF.pop()
    R.w("sync", sv, wbc_ready)

    AF32.push()
    ABF.push()
    xf = AF32.tiles(2, W)
    tmpf = AF32.tiles(2, W)
    y1f = AF32.tiles(KD, W)
    dtt = AF32.tiles(2, W)
    rvf = AF32.tiles(2, W)
    hx = AF32.alloc(KD)
    xb = ABF.tiles(KD, W)
    xq = ABF.tiles(2, W)
    xn0b = [ABF.alloc(1 + W) for _ in range(KD)]
    blockA = ABF.tiles(KD, W)
    blockB = ABF.tiles(KD, W)
    rbt = ABF.tiles(KD, W)
    dtf = ABF.tiles(2, W)
    ub = ABF.tiles(2, W)
    gsil = ABF.tiles(2, W)
    mtb = ABF.tiles(2, W)
    bcst = ABF.alloc(W)
    hxb = ABF.alloc(KD)
    hxq = ABF.alloc(KD)
    xrb, kab = blockA, blockA
    xkb = blockB
    zb = blockA + blockB
    rb, t1b = rbt, rbt

    pstate = {"i": 0, "cons": []}

    def load_piece(src, p, n_k=KD):
        i = pstate["i"]
        buf = sA if i % 2 == 0 else sB
        if i >= 2:
            s0, v0 = pstate["cons"][i - 2]
            R.w("sync", s0, v0)
        R.dma("sync", buf[:, :n_k * 512],
              src[:, p * n_k * 512:(p + 1) * n_k * 512], sw[i % 2])
        pstate["cons"].append(None)
        pstate["i"] += 1
        return buf, sw[i % 2], i

    def piece_done(i):
        pstate["cons"][i] = (tt, tt.n)

    pre_ms = []

    def pre_chunk(j):
        c0 = j * W
        g = j // AGC
        c2 = j % AGC
        prev = pre_ms[-1] if pre_ms else None
        cast_st = []

        def cast_x(k):
            if k < 2:
                if prev is not None:
                    R.w("sync", sv, prev["sv"])
                    R.w("sync", st, prev["st"])
            else:
                R.w("sync", st, cast_st[k - 2])
            R.dma("sync", xf[k % 2][:, :],
                  din["xT"][k * 128:(k + 1) * 128, 1 + c0:1 + c0 + W],
                  sx[k % 2])
            R.w("scalar", sx[k % 2])
            if j == 0 and k == 0:
                R.w("scalar", sL)
                R.w("vector", sL)
                R.w("tensor", sL)
            R.do("scalar", "activation", xb[k][:, :], xf[k % 2][:, :], AF.Copy,
                 inc=st)
            R.do("scalar", "activation", xq[k % 2][:, :], xf[k % 2][:, :],
                 AF.Square, inc=st)
            cast_st.append(st.n)
            return xb[k][:, :], xq[k % 2][:, :]

        ln_stats(cast_x, W)
        if j == 0:
            for k in range(KD):
                R.dma("sync", hx[:, k:k + 1],
                      din["xT"][k * 128:(k + 1) * 128, 0:1], sL, nc_ok=True)
            R.w("scalar", sL)
            mini_ln_cols(hx, hxb, hxq, KD)
            R.do("vector", "tensor_single_scalar", hx[:, :], hx[:, :],
                 PS[5][:, 0:1], ALU.subtract)
            R.do("vector", "tensor_single_scalar", xn0_halo[:, :], hx[:, :],
                 PS[5][:, 1:2], ALU.mult, inc=sv)
            bank_free[5] = (sv, sv.n)
            for k in range(KD):
                R.do("vector", "tensor_copy", xn0b[k][:, 0:1],
                     xn0_halo[:, k:k + 1])
        else:
            for k in range(KD):
                R.do("vector", "tensor_copy", xn0b[k][:, 0:1],
                     xn0b[k][:, W:W + 1])
        for k in range(KD):
            R.do("vector", "tensor_sub", tmpf[k % 2][:, :], xb[k][:, :],
                 PS[3][:, :W])
            R.do("vector", "tensor_mul", xn0b[k][:, 1:1 + W], tmpf[k % 2][:, :],
                 PS[4][:, :W], inc=sv if k == KD - 1 else None)
        bank_free[3] = (sv, sv.n)
        bank_free[4] = (sv, sv.n)
        for k in range(KD):
            cur = xn0b[k][:, 1:1 + W]
            prv = xn0b[k][:, 0:W]
            R.do("vector", "tensor_sub", mtb[k % 2][:, :], cur, prv)
            R.do("vector", "scalar_tensor_tensor", xrb[k][:, :], mtb[k % 2][:, :],
                 mur_sb[:, k:k + 1], prv, ALU.mult, ALU.add)
            R.do("vector", "scalar_tensor_tensor", xkb[k][:, :], mtb[k % 2][:, :],
                 muk_sb[:, k:k + 1], prv, ALU.mult, ALU.add,
                 inc=sv if k == KD - 1 else None)
        mixes = sv.n
        for mh in range(2):
            buf, wld, pidx = load_piece(wr_b, mh)
            for ml in range(4):
                mo = mh * 4 + ml
                bk = (0, 1, 6)[mo % 3]
                R.w("tensor", sv, mixes)
                R.w("tensor", wld)
                mm_group(PS[bk][:, :W], bk,
                         [(buf[:, k * 512 + ml * 128:k * 512 + (ml + 1) * 128],
                           xrb[k][:, :]) for k in range(KD)])
                R.w("scalar", tt, tt.n)
                R.do("scalar", "activation", rb[mo][:, :], PS[bk][:, :W],
                     AF.Sigmoid, inc=st)
                bank_free[bk] = (st, st.n)
            piece_done(pidx)
        for mh in range(2):
            buf, wld, pidx = load_piece(wk_b, mh)
            for ml in range(4):
                mo = mh * 4 + ml
                bk = (0, 1, 6)[mo % 3]
                R.w("tensor", wld)
                mm_group(PS[bk][:, :W], bk,
                         [(buf[:, k * 512 + ml * 128:k * 512 + (ml + 1) * 128],
                           xkb[k][:, :]) for k in range(KD)])
                R.w("scalar", tt, tt.n)
                R.do("scalar", "activation", xq[0][:, :], PS[bk][:, :W],
                     AF.Relu, inc=st)
                bank_free[bk] = (st, st.n)
                R.do("scalar", "activation", kab[mo][:, :], xq[0][:, :],
                     AF.Square, inc=st)
            piece_done(pidx)
        ka_done = st.n
        xld = []
        for mh in range(2):
            buf, wld, pidx = load_piece(wv_b, mh)
            for ml in range(4):
                mo = mh * 4 + ml
                bk = (0, 1, 6)[mo % 3]
                R.w("tensor", st, ka_done)
                R.w("tensor", wld)
                mm_group(PS[bk][:, :W], bk,
                         [(buf[:, k * 512 + ml * 128:k * 512 + (ml + 1) * 128],
                           kab[k][:, :]) for k in range(KD)])
                if mo >= 2:
                    R.w("sync", sv, xld[mo - 2])
                else:
                    R.w("sync", st, cast_st[-1])
                R.dma("sync", xf[mo % 2][:, :],
                      din["xT"][mo * 128:(mo + 1) * 128, 1 + c0:1 + c0 + W],
                      sx[mo % 2])
                R.w("vector", tt, tt.n)
                R.w("vector", sx[mo % 2])
                if j > 0 and mo == 0:
                    R.w("vector", sr12)
                R.do("vector", "tensor_mul", rvf[mo % 2][:, :], rb[mo][:, :],
                     PS[bk][:, :W])
                R.do("vector", "scalar_tensor_tensor", y1f[mo][:, :],
                     rvf[mo % 2][:, :], 0.5, xf[mo % 2][:, :], ALU.mult,
                     ALU.add, inc=sv)
                bank_free[bk] = (sv, sv.n)
                xld.append(sv.n)
            piece_done(pidx)
        y1_done = sv.n

        def cast_y1(k):
            R.w("scalar", sv, y1_done)
            R.do("scalar", "activation", xb[k][:, :], y1f[k][:, :], AF.Copy,
                 inc=st)
            R.do("scalar", "activation", xq[k % 2][:, :], y1f[k][:, :],
                 AF.Square, inc=st)
            return xb[k][:, :], xq[k % 2][:, :]

        ln_stats(cast_y1, W)
        for k in range(KD):
            R.do("vector", "tensor_sub", tmpf[k % 2][:, :], y1f[k][:, :],
                 PS[3][:, :W])
            R.do("vector", "tensor_mul", t1b[k][:, :], tmpf[k % 2][:, :],
                 PS[4][:, :W])
            R.do("vector", "tensor_add", y1f[k][:, :], y1f[k][:, :],
                 t1b[k][:, :], inc=sv)
        bank_free[3] = (sv, sv.n)
        bank_free[4] = (sv, sv.n)
        r12_done = sv.n
        R.w("sync", sv, r12_done)
        for k in range(KD):
            R.dma("sync", r12_sp[:, k * TOK + c0:k * TOK + c0 + W],
                  y1f[k][:, :], sr12)

        def cast_t1(k):
            R.w("scalar", sv, r12_done)
            R.do("scalar", "activation", xq[k % 2][:, :], t1b[k][:, :],
                 AF.Square, inc=st)
            return t1b[k][:, :], xq[k % 2][:, :]

        ln_stats(cast_t1, W)
        for k in range(KD):
            R.do("vector", "tensor_sub", tmpf[k % 2][:, :], t1b[k][:, :],
                 PS[3][:, :W])
            R.do("vector", "tensor_mul", tmpf[k % 2][:, :], tmpf[k % 2][:, :],
                 PS[4][:, :W])
            R.do("vector", "tensor_scalar", xb[k][:, :], tmpf[k % 2][:, :],
                 nw_sb[:, k:k + 1], nb_sb[:, k:k + 1], ALU.mult, ALU.add,
                 inc=sv if k == KD - 1 else None)
        xn_ready = sv.n
        bank_free[3] = (sv, sv.n)
        bank_free[4] = (sv, sv.n)
        for p in range(4):
            buf, wld, pidx = load_piece(win_b, p)
            for ml in range(4):
                mo = p * 4 + ml
                bk = (0, 1, 6)[mo % 3]
                R.w("tensor", sv, xn_ready)
                R.w("tensor", wld)
                mm_group(PS[bk][:, :W], bk,
                         [(buf[:, k * 512 + ml * 128:k * 512 + (ml + 1) * 128],
                           xb[k][:, :]) for k in range(KD)])
                R.w("scalar", tt, tt.n)
                R.do("scalar", "activation", zb[mo][:, :], PS[bk][:, :W],
                     AF.Copy, inc=st)
                bank_free[bk] = (st, st.n)
            piece_done(pidx)
        for p in range(4):
            buf, wld, pidx = load_piece(wgate_b, p)
            for ml in range(4):
                mo = p * 4 + ml
                bk = (0, 1, 6)[mo % 3]
                R.w("tensor", wld)
                mm_group(PS[bk][:, :W], bk,
                         [(buf[:, k * 512 + ml * 128:k * 512 + (ml + 1) * 128],
                           xb[k][:, :]) for k in range(KD)])
                R.w("scalar", tt, tt.n)
                R.w("scalar", sg[mo % 2])
                R.do("scalar", "activation", gsil[mo % 2][:, :], PS[bk][:, :W],
                     AF.Silu, inc=st)
                bank_free[bk] = (st, st.n)
                R.w("sync", st, st.n)
                R.dma("sync", gate_sp[:, mo * TOK + c0:mo * TOK + c0 + W],
                      gsil[mo % 2][:, :], sg[mo % 2])
            piece_done(pidx)
        for mq in range(4):
            b0 = load_piece(wdt_b, 0 * 4 + mq)
            b1 = load_piece(wdt_b, 1 * 4 + mq)
            mq_tt = []
            for ml in range(4):
                mo = mq * 4 + ml
                bk = (0, 1, 6, 7)[ml]
                s0b, v0b = bank_free[bk]
                R.w("tensor", s0b, v0b)
                R.w("tensor", b0[1])
                R.w("tensor", b1[1])
                pairs = []
                for kk in range(KI):
                    buf = (b0 if kk < KD else b1)[0]
                    k = kk % KD
                    pairs.append(
                        (buf[:, k * 512 + ml * 128:k * 512 + (ml + 1) * 128],
                         zb[kk][:, :]))
                mm_group(PS[bk][:, :W], bk, pairs, guard=False)
                mq_tt.append(tt.n)
            piece_done(b0[2])
            piece_done(b1[2])
            for ml in range(4):
                mo = mq * 4 + ml
                bk = (0, 1, 6, 7)[ml]
                R.w("scalar", tt, mq_tt[ml])
                R.w("scalar", sv, sv.n)
                R.do("scalar", "activation", dtt[ml % 2][:, :], PS[bk][:, :W],
                     AF.Exp, bias=bdt_sb[:, mo:mo + 1], inc=st)
                R.do("scalar", "activation", dtt[ml % 2][:, :],
                     dtt[ml % 2][:, :], AF.Ln, bias=cone[:, 0:1], inc=st)
                bank_free[bk] = (st, st.n)
                R.w("vector", st, st.n)
                R.w("vector", sd[ml % 2])
                R.do("vector", "tensor_scalar", dtf[ml % 2][:, :],
                     dtt[ml % 2][:, :], 1e-3, 1.0, ALU.max, ALU.min)
                R.do("vector", "tensor_mul", ub[ml % 2][:, :], dtf[ml % 2][:, :],
                     zb[mo][:, :], inc=sv)
                R.w("sync", sv, sv.n)
                base = mo * GA + c2 * W
                R.dma("sync", dtu_in[g][:, base:base + W], dtf[ml % 2][:, :],
                      sd[ml % 2])
                R.dma("sync", dtu_in[g][:, KI * GA + base:KI * GA + base + W],
                      ub[ml % 2][:, :], sd[ml % 2])
        s0b, v0b = bank_free[5]
        R.w("tensor", s0b, v0b)
        R.w("tensor", sv, wbc_ready)
        for kk in range(KI):
            R.do("tensor", "matmul", PS[5][0:16, :W],
                 wb_sb[:, kk * S:(kk + 1) * S], zb[kk][:, :],
                 start=(kk == 0), stop=(kk == KI - 1))
        for kk in range(KI):
            R.do("tensor", "matmul", PS[5][32:48, :W],
                 wc_sb[:, kk * S:(kk + 1) * S], zb[kk][:, :],
                 start=(kk == 0), stop=(kk == KI - 1), tile_position=(0, 32),
                 inc=tt if kk == KI - 1 else None)
        R.w("vector", tt, tt.n)
        R.w("vector", sbcst)
        R.do("vector", "tensor_copy", bcst[0:16, :W], PS[5][0:16, :W])
        R.do("vector", "tensor_copy", bcst[32:48, :W], PS[5][32:48, :W], inc=sv)
        bank_free[5] = (sv, sv.n)
        R.w("sync", sv, sv.n)
        bco = 2 * KI * GA + c2 * W
        R.dma("sync", dtu_in[g][0:16, bco:bco + W], bcst[0:16, :W], sbcst)
        R.dma("sync", dtu_in[g][16:32, bco:bco + W], bcst[32:48, :W], sbcst)
        pre_ms.append({"sv": sv.n, "st": st.n, "tt": tt.n})

    scan_ms = []

    scan_state = {"tg": -1, "k": 0, "tms": None, "sts": None}

    def scan_prep(tg):
        half = tg // NAG
        gg = tg % NAG
        src = dtu_out[gg]
        R.w("sync", ag_sems[gg])
        if scan_ms:
            R.w("sync", sv, scan_ms[-1]["sv"])
        bco = 2 * KI * GA
        R.w("sync", sv, sv.n)
        R.dma("sync", brow[0][:, :], src[half, 0:SH, bco:bco + GA], sbc)
        R.dma("sync", brow[1][:, :], src[half, SH:2 * SH, bco:bco + GA], sbc)
        R.dma("sync", brow[2][:, :], src[half, 16:16 + SH, bco:bco + GA], sbc)
        R.dma("sync", brow[3][:, :], src[half, 16 + SH:16 + 2 * SH,
                                         bco:bco + GA], sbc)
        R.w("vector", sbc)
        R.do("vector", "tensor_single_scalar", bwork[0][:, :], brow[0][:, :],
             hm0_sb[0:SH, 0:1], ALU.mult)
        R.do("vector", "tensor_single_scalar", bwork[1][:, :], brow[2][:, :],
             hm0_sb[0:SH, 0:1], ALU.mult)
        R.do("vector", "scalar_tensor_tensor", bwork[0][:, :], brow[1][:, :],
             hmask_sb[0:SH, 0:1], bwork[0][:, :], ALU.mult, ALU.add, lag=2)
        R.do("vector", "scalar_tensor_tensor", bwork[1][:, :], brow[3][:, :],
             hmask_sb[0:SH, 0:1], bwork[1][:, :], ALU.mult, ALU.add, inc=sv,
             lag=2)
        R.w("sync", sv, sv.n)
        R.dma("sync", bsel_d[0:SH, :], bwork[0][:, :], sbc)
        R.dma("sync", bsel_d[SH:2 * SH, :], bwork[1][:, :], sbc)
        R.w("sync", sbc)
        for si in range(SH):
            R.dma("sync", Bbc[si][:, :],
                  bsel_d[si:si + 1, :].to_broadcast((128, GA)), sbc)
            R.dma("sync", Cbc[si][:, :],
                  bsel_d[SH + si:SH + si + 1, :].to_broadcast((128, GA)), sbc)
        scan_state["tg"] = tg
        scan_state["k"] = 0
        scan_state["tms"] = []
        scan_state["sts"] = {}

    def scan_ktile():
        tg = scan_state["tg"]
        k = scan_state["k"]
        half = tg // NAG
        gg = tg % NAG
        src = dtu_out[gg]
        tile_ms = scan_state["tms"]
        st_hist = scan_state["sts"]
        kb = k % 2
        if k >= 2:
            m = tile_ms[k - 2]
            R.w("sync", st, m["st"])
            R.w("sync", sv, m["sv"])
        R.dma("sync", dts[kb][:, :],
              src[half, :, k * GA:(k + 1) * GA], sdu[kb])
        R.dma("sync", uts[kb][:, :],
              src[half, :, KI * GA + k * GA:KI * GA + (k + 1) * GA],
              sdu[kb])
        scan_hist = []

        def emit_exp(si):
            col = k * SH + si
            R.w("scalar", sdu[kb])
            if si >= 2:
                R.w("scalar", sv, scan_hist[si - 2])
            R.do("scalar", "activation", dAr[si % 2][:, :], dts[kb][:, :],
                 AF.Exp, scale=A_sb[:, col:col + 1], inc=st, lag=2)
            st_hist[si] = st.n

        def B(si, lag=4):
            R.w("vector", sdu[kb])
            R.w("vector", sbc)
            R.do("vector", "tensor_mul", burb[si % 4][:, :], uts[kb][:, :],
                 Bbc[si][:, :], inc=sv, lag=lag)

        def Sc(si, lag=2):
            emit_exp(si)
            R.w("vector", st, st_hist[si])
            col = k * SH + si
            o = (si % 2) * GA
            R.do("vector", "tensor_tensor_scan", hbr2[:, o:o + GA],
                 dAr[si % 2][:, :], burb[si % 4][:, :],
                 carry[:, col:col + 1], ALU.mult, ALU.add, inc=sv, lag=lag)
            scan_hist.append(sv.n)

        def Cc(si, lag=4):
            col = k * SH + si
            o = (si % 2) * GA
            R.do("vector", "tensor_copy", carry[:, col:col + 1],
                 hbr2[:, o + GA - 1:o + GA], inc=sv, lag=lag)

        def H(j, lag=2):
            R.do("vector", "tensor_mul", hcp[j % 2][:, :], hbr2[:, :],
                 Cbc_all[:, 2 * j * GA:(2 * j + 2) * GA], inc=sv, lag=lag)

        def P(j, dst, lag=2):
            h = hcp[j % 2]
            R.do("vector", "tensor_add", dst[:, :], h[:, 0:GA],
                 h[:, GA:2 * GA], inc=sv, lag=lag)

        B(0)
        B(1)
        B(2)
        B(3)
        Sc(0, lag=4)
        Sc(1, lag=4)
        B(4, lag=2)
        H(0)
        Cc(0)
        Cc(1)
        B(5, lag=4)
        Sc(2, lag=3)
        Sc(3, lag=3)
        P(0, pp[0], lag=4)
        H(1, lag=2)
        Cc(2)
        Cc(3)
        B(6, lag=6)
        Sc(4, lag=3)
        B(7, lag=6)
        Sc(5, lag=2)
        P(1, pp[1], lag=6)
        H(2, lag=2)
        Cc(4)
        Cc(5)
        R.do("vector", "tensor_add", pp[2][:, :], pp[0][:, :],
             pp[1][:, :], inc=sv, lag=4)
        Sc(6, lag=3)
        Sc(7, lag=2)
        P(2, pp[0], lag=3)
        H(3, lag=2)
        Cc(6)
        Cc(7)
        R.do("vector", "tensor_add", pp[1][:, :], pp[2][:, :],
             pp[0][:, :], inc=sv, lag=4)
        P(3, hbr2[:, 0:GA], lag=2)
        if k >= 2:
            R.w("vector", ssy[tg])
        elif tg >= 1:
            R.w("vector", ssy[tg - 1])
        R.do("vector", "tensor_add", ysc[k % 2][:, :], pp[1][:, :],
             hbr2[:, 0:GA], inc=sv, lag=1)
        if tg == NAG - 1:
            R.do("vector", "tensor_copy", y4095[:, k:k + 1],
                 ysc[k % 2][:, GA - 1:GA], inc=sv)
        R.w("sync", sv, sv.n)
        R.dma("sync", y_in[gg][half, :, k * GA:(k + 1) * GA],
              ysc[k % 2][:, :], ssy[tg])
        tile_ms.append({"st": st.n, "sv": sv.n})
        scan_state["k"] = k + 1
        if scan_state["k"] == KI:
            scan_ms.append({"sv": sv.n})

    ag_sems = {}
    yc_sem = [None]
    done = [0]
    next_tg = [0]

    R.w("sync", scs[0])
    R.w("sync", scs[1])

    def emit_ag(g):
        R.w("gpsimd", sd[0])
        R.w("gpsimd", sd[1])
        R.w("gpsimd", sbcst)
        s = next_cc()
        R.do("gpsimd", "collective_compute", "AllGather", ALU.bypass,
             GROUPS, [dtu_in[g].ap().opt()], [dtu_out[g].ap().opt()], inc=s)
        ag_sems[g] = s

    def pump(n):
        while n > 0 and done[0] < NTG * KI:
            if scan_state["tg"] < 0 or scan_state["k"] == KI:
                tg = next_tg[0]
                if tg >= NTG or (tg % NAG) not in ag_sems:
                    return
                scan_prep(tg)
                next_tg[0] += 1
            scan_ktile()
            done[0] += 1

    for j in range(NCH):
        pre_chunk(j)
        if j % AGC == AGC - 1:
            emit_ag(j // AGC)
        pump(6)
    while done[0] < NAG * KI:
        pump(KI)
    R.w("sync", sv, scan_ms[-1]["sv"])
    R.dma("sync", yc_in[:, :], y4095[:, :], syc)
    R.w("gpsimd", syc)
    _s = next_cc()
    R.do("gpsimd", "collective_compute", "AllGather", ALU.bypass,
         GROUPS, [yc_in.ap().opt()], [yc_out.ap().opt()], inc=_s)
    yc_sem[0] = _s
    while done[0] < (NAG + 1) * KI:
        pump(KI)

    rs_sems = {}

    def emit_rs(g):
        R.w("gpsimd", ssy[g])
        R.w("gpsimd", ssy[NAG + g])
        s = next_cc()
        R.do("gpsimd", "collective_compute", "ReduceScatter", ALU.add,
             GROUPS, [y_in[g].ap().opt()], [y_out[g].ap().opt()], inc=s)
        rs_sems[g] = s

    AF32.pop()
    ABF.pop()

    AF32.push()
    ABF.push()
    y2f = AF32.tiles(KD, W)
    tmpe = AF32.tiles(2, W)
    r12e = AF32.tiles(2, W)
    rv3 = AF32.tiles(2, W)
    outf = AF32.tiles(2, W)
    r12col = AF32.alloc(KD)
    rhl = AF32.alloc(KD)
    yca = AF32.tiles(2, KI)
    blockE = ABF.tiles(KI, W)
    ytl = ABF.tiles(2, W)
    gte = ABF.tiles(2, W)
    y2q = ABF.tiles(2, W)
    xn3b = [ABF.alloc(1 + W) for _ in range(KD)]
    xr3 = ABF.tiles(KD, W)
    xk3 = ABF.tiles(KD, W)
    mt3 = ABF.tiles(2, W)
    gcol = ABF.alloc(KI)
    ygcol = ABF.alloc(KI)
    hxb3 = ABF.alloc(KD)
    hxq3 = ABF.alloc(KD)
    ygb = blockE
    y2b = blockE[0:KD]
    rb3 = blockE[KD:KI]
    ka3 = xr3

    R.w("sync", yc_sem[0])
    R.dma("sync", yca[0][:, :], yc_out[0], sL)
    R.dma("sync", yca[1][:, :], yc_out[1], sL)
    R.w("vector", sL)
    R.do("vector", "tensor_add", y4095[:, :], yca[0][:, :], yca[1][:, :],
         inc=sv)
    yca_done = sv.n
    R.w("sync", sv, yca_done)
    R.w("sync", sg[0])
    R.w("sync", sg[1])
    R.w("sync", sr12)
    for kk in range(KI):
        R.dma("sync", gcol[:, kk:kk + 1],
              gate_sp[:, kk * TOK + TOK - 1:kk * TOK + TOK], sL, nc_ok=True)
    R.w("vector", sL)
    R.do("vector", "tensor_mul", ygcol[:, :], y4095[:, :], gcol[:, :], inc=sv)
    yg_col_v = sv.n
    s0b, v0b = bank_free[5]
    R.w("tensor", s0b, v0b)
    R.w("tensor", sv, yg_col_v)
    first = True
    for pz in range(4):
        mh, kh = pz // 2, pz % 2
        buf, wld, pidx = load_piece(wout_b, pz)
        R.w("tensor", wld)
        for ml in range(4):
            mo = mh * 4 + ml
            for k in range(KD):
                kk = kh * KD + k
                last = (pz == 3 and ml == 3 and k == KD - 1)
                piece_last = (ml == 3 and k == KD - 1)
                R.do("tensor", "matmul", PS[5][:, mo:mo + 1],
                     buf[:, k * 512 + ml * 128:k * 512 + (ml + 1) * 128],
                     ygcol[:, kk:kk + 1],
                     start=first, stop=last, inc=tt if piece_last else None)
                first = False
        piece_done(pidx)
    R.w("sync", sv, yg_col_v)
    for k in range(KD):
        R.dma("sync", r12col[:, k:k + 1],
              r12_sp[:, k * TOK + TOK - 1:k * TOK + TOK], sL, nc_ok=True)
    R.w("vector", sL)
    R.w("vector", tt, tt.n)
    R.do("vector", "tensor_add", r12col[:, :], r12col[:, :], PS[5][:, 0:KD],
         inc=sv)
    bank_free[5] = (sv, sv.n)
    R.w("scalar", sv, sv.n)
    mini_ln_cols(r12col, hxb3, hxq3, KD)
    R.do("vector", "tensor_single_scalar", r12col[:, :], r12col[:, :],
         PS[5][:, 0:1], ALU.subtract)
    R.do("vector", "tensor_single_scalar", r12col[:, :], r12col[:, :],
         PS[5][:, 1:2], ALU.mult, inc=sv)
    bank_free[5] = (sv, sv.n)
    R.w("sync", sv, sv.n)
    R.dma("sync", rh_in[:, :], r12col[:, :], srh)
    R.w("gpsimd", srh)
    rh_sem = next_cc()
    R.do("gpsimd", "collective_compute", "AllGather", ALU.bypass,
         GROUPS, [rh_in.ap().opt()], [rh_out.ap().opt()], inc=rh_sem)
    R.w("sync", rh_sem)
    R.dma("sync", rhl[:, :], rh_out[0], sL)
    R.w("vector", sL)
    R.do("vector", "tensor_single_scalar", xn3_halo[:, :], rhl[:, :],
         hmask_sb[:, 0:1], ALU.mult, inc=sv)
    halo3_done = sv.n

    epi_ms = []

    def epi_chunk(e):
        c0 = e * W
        g = e // AGC
        prev = epi_ms[-1] if epi_ms else None
        R.w("sync", rs_sems[g])
        if prev is None:
            R.w("sync", sg[0])
            R.w("sync", sg[1])
            R.w("sync", sr12)
        if prev is not None:
            R.w("sync", sv, prev["sv"])
            R.w("sync", st, prev["st"])
        else:
            R.w("sync", sv, halo3_done)
        yg_st = []
        for kk in range(KI):
            if kk >= 2:
                R.w("sync", sv, yg_st[kk - 2])
            R.dma("sync", ytl[kk % 2][:, :],
                  y_out[g][:, kk * GA + (e % AGC) * W:
                           kk * GA + (e % AGC + 1) * W], sey[kk % 2])
            R.dma("sync", gte[kk % 2][:, :],
                  gate_sp[:, kk * TOK + c0:kk * TOK + c0 + W], sey[kk % 2])
            R.w("vector", sey[kk % 2])
            R.do("vector", "tensor_mul", ygb[kk][:, :], ytl[kk % 2][:, :],
                 gte[kk % 2][:, :], inc=sv)
            yg_st.append(sv.n)
        yg_done = sv.n
        for mh in range(2):
            bA = load_piece(wout_b, mh * 2 + 0)
            bX = load_piece(wout_b, mh * 2 + 1)
            for ml in range(4):
                mo = mh * 4 + ml
                bk = (0, 1, 6)[mo % 3]
                s0b, v0b = bank_free[bk]
                R.w("tensor", s0b, v0b)
                R.w("tensor", bA[1])
                R.w("tensor", bX[1])
                R.w("tensor", sv, yg_done)
                pairs = []
                for kk in range(KI):
                    buf = (bA if kk < KD else bX)[0]
                    k = kk % KD
                    pairs.append(
                        (buf[:, k * 512 + ml * 128:k * 512 + (ml + 1) * 128],
                         ygb[kk][:, :]))
                mm_group(PS[bk][:, :W], bk, pairs, guard=False)
                if mo >= 2:
                    R.w("sync", sv, sv.n)
                R.dma("sync", r12e[mo % 2][:, :],
                      r12_sp[:, mo * TOK + c0:mo * TOK + c0 + W], ser[mo % 2])
                R.w("vector", tt, tt.n)
                R.w("vector", ser[mo % 2])
                R.do("vector", "tensor_add", y2f[mo][:, :], PS[bk][:, :W],
                     r12e[mo % 2][:, :], inc=sv)
                bank_free[bk] = (sv, sv.n)
            piece_done(bA[2])
            piece_done(bX[2])
        y2_done = sv.n

        def cast_y2(k):
            R.w("scalar", sv, y2_done)
            R.do("scalar", "activation", y2b[k][:, :], y2f[k][:, :], AF.Copy,
                 inc=st)
            R.do("scalar", "activation", y2q[k % 2][:, :], y2f[k][:, :],
                 AF.Square, inc=st)
            return y2b[k][:, :], y2q[k % 2][:, :]

        ln_stats(cast_y2, W)
        if e == 0:
            R.w("vector", sv, halo3_done)
            for k in range(KD):
                R.do("vector", "tensor_copy", xn3b[k][:, 0:1],
                     xn3_halo[:, k:k + 1])
        else:
            for k in range(KD):
                R.do("vector", "tensor_copy", xn3b[k][:, 0:1],
                     xn3b[k][:, W:W + 1])
        for k in range(KD):
            R.do("vector", "tensor_sub", tmpe[k % 2][:, :], y2f[k][:, :],
                 PS[3][:, :W])
            R.do("vector", "tensor_mul", xn3b[k][:, 1:1 + W], tmpe[k % 2][:, :],
                 PS[4][:, :W], inc=sv if k == KD - 1 else None)
        bank_free[3] = (sv, sv.n)
        bank_free[4] = (sv, sv.n)
        for k in range(KD):
            cur = xn3b[k][:, 1:1 + W]
            prv = xn3b[k][:, 0:W]
            R.do("vector", "tensor_sub", mt3[k % 2][:, :], cur, prv)
            R.do("vector", "scalar_tensor_tensor", xr3[k][:, :], mt3[k % 2][:, :],
                 mur_sb[:, k:k + 1], prv, ALU.mult, ALU.add)
            R.do("vector", "scalar_tensor_tensor", xk3[k][:, :], mt3[k % 2][:, :],
                 muk_sb[:, k:k + 1], prv, ALU.mult, ALU.add,
                 inc=sv if k == KD - 1 else None)
        mixes = sv.n
        for mh in range(2):
            buf, wld, pidx = load_piece(wr_b, mh)
            for ml in range(4):
                mo = mh * 4 + ml
                bk = (0, 1, 6)[mo % 3]
                R.w("tensor", sv, mixes)
                R.w("tensor", wld)
                mm_group(PS[bk][:, :W], bk,
                         [(buf[:, k * 512 + ml * 128:k * 512 + (ml + 1) * 128],
                           xr3[k][:, :]) for k in range(KD)])
                R.w("scalar", tt, tt.n)
                R.do("scalar", "activation", rb3[mo % KD][:, :], PS[bk][:, :W],
                     AF.Sigmoid, inc=st)
                bank_free[bk] = (st, st.n)
            piece_done(pidx)
        for mh in range(2):
            buf, wld, pidx = load_piece(wk_b, mh)
            for ml in range(4):
                mo = mh * 4 + ml
                bk = (0, 1, 6)[mo % 3]
                R.w("tensor", wld)
                mm_group(PS[bk][:, :W], bk,
                         [(buf[:, k * 512 + ml * 128:k * 512 + (ml + 1) * 128],
                           xk3[k][:, :]) for k in range(KD)])
                R.w("scalar", tt, tt.n)
                R.do("scalar", "activation", y2q[0][:, :], PS[bk][:, :W],
                     AF.Relu, inc=st)
                bank_free[bk] = (st, st.n)
                R.do("scalar", "activation", ka3[mo][:, :], y2q[0][:, :],
                     AF.Square, inc=st)
            piece_done(pidx)
        ka_done = st.n
        for mh in range(2):
            buf, wld, pidx = load_piece(wv_b, mh)
            for ml in range(4):
                mo = mh * 4 + ml
                bk = (0, 1, 6)[mo % 3]
                R.w("tensor", st, ka_done)
                R.w("tensor", wld)
                mm_group(PS[bk][:, :W], bk,
                         [(buf[:, k * 512 + ml * 128:k * 512 + (ml + 1) * 128],
                           ka3[k][:, :]) for k in range(KD)])
                R.w("vector", tt, tt.n)
                R.w("vector", sout[mo % 2])
                R.do("vector", "tensor_mul", rv3[mo % 2][:, :],
                     rb3[mo % KD][:, :], PS[bk][:, :W])
                R.do("vector", "scalar_tensor_tensor", outf[mo % 2][:, :],
                     rv3[mo % 2][:, :], 0.5, y2f[mo][:, :], ALU.mult, ALU.add,
                     inc=sv)
                bank_free[bk] = (sv, sv.n)
                R.w("sync", sv, sv.n)
                R.dma("sync", outT[mo * 128:(mo + 1) * 128, c0:c0 + W],
                      outf[mo % 2][:, :], sout[mo % 2])
            piece_done(pidx)
        epi_ms.append({"sv": sv.n, "st": st.n})

    for g in range(NAG):
        emit_rs(g)
        epi_chunk(2 * g)
        pump(KI // 2)
        epi_chunk(2 * g + 1)
        pump(KI // 2)
    pump(NTG * KI)

    AF32.pop()
    ABF.pop()

    R.w("gpsimd", sout[0])
    R.w("gpsimd", sout[1])

    with ctx:
        with nc.Block() as block:
            @block.sync
            def _(e):
                for f in R.q["sync"]:
                    f(e)

            @block.gpsimd
            def _(e):
                for f in R.q["gpsimd"]:
                    f(e)

            @block.scalar
            def _(e):
                for f in R.q["scalar"]:
                    f(e)

            @block.vector
            def _(e):
                for f in R.q["vector"]:
                    f(e)

            @block.tensor
            def _(e):
                for f in R.q["tensor"]:
                    f(e)

    return nc


_NC_CACHE = {}


def _get_nc():
    key = (T, TCP)
    if key not in _NC_CACHE:
        _NC_CACHE[key] = build()
    return _NC_CACHE[key]


def kernel(
    L0_attn_x, L0_attn_norm_w, L0_attn_norm_b, L0_attn_W_in, L0_attn_W_gate,
    L0_attn_W_dt, L0_attn_b_dt, L0_attn_W_B, L0_attn_W_C, L0_attn_A_log,
    L0_attn_W_out, L0_ffn_W_r, L0_ffn_W_k, L0_ffn_W_v, L0_ffn_mu_r, L0_ffn_mu_k,
):
    f32 = np.float32
    x = np.asarray(L0_attn_x, f32)
    shared = {
        "wrT": np.ascontiguousarray(np.asarray(L0_ffn_W_r, f32).T),
        "wkT": np.ascontiguousarray(np.asarray(L0_ffn_W_k, f32).T),
        "wvT": np.ascontiguousarray(np.asarray(L0_ffn_W_v, f32).T),
        "win": np.ascontiguousarray(np.asarray(L0_attn_W_in, f32)),
        "wgate": np.ascontiguousarray(np.asarray(L0_attn_W_gate, f32)),
        "wdt": np.ascontiguousarray(np.asarray(L0_attn_W_dt, f32)),
        "wb16": np.ascontiguousarray(np.asarray(L0_attn_W_B, f32)),
        "wc16": np.ascontiguousarray(np.asarray(L0_attn_W_C, f32)),
        "wout": np.ascontiguousarray(np.asarray(L0_attn_W_out, f32)),
        "bdt": np.ascontiguousarray(np.asarray(L0_attn_b_dt, f32).reshape(KI, 128).T),
        "mur": np.ascontiguousarray(np.asarray(L0_ffn_mu_r, f32).reshape(KD, 128).T),
        "muk": np.ascontiguousarray(np.asarray(L0_ffn_mu_k, f32).reshape(KD, 128).T),
        "nw": np.ascontiguousarray(np.asarray(L0_attn_norm_w, f32).reshape(KD, 128).T),
        "nb": np.ascontiguousarray(np.asarray(L0_attn_norm_b, f32).reshape(KD, 128).T),
    }
    alog = np.asarray(L0_attn_A_log, f32)
    in_maps = []
    for c in range(NC):
        b, h = c // 2, c % 2
        xt = np.ascontiguousarray(x[b].T)
        halo = (np.zeros((D, 1), f32) if h == 0
                else np.ascontiguousarray(xt[:, TOK - 1:TOK]))
        m = dict(shared)
        m["xT"] = np.ascontiguousarray(
            np.concatenate([halo, xt[:, h * TOK:(h + 1) * TOK]], axis=1))
        asl = alog[:, h * SH:(h + 1) * SH].reshape(KI, 128, SH)
        m["alog"] = np.ascontiguousarray(
            asl.transpose(1, 0, 2).reshape(128, KI * SH))
        m["hmask"] = np.full((128, 1), float(h), f32)
        in_maps.append(m)

    nc = _get_nc()
    res = run_bass_kernel_spmd(nc, in_maps, core_ids=list(range(NC)))
    out = np.empty((B, T, D), f32)
    for c in range(NC):
        b, h = c // 2, c % 2
        out[b, h * TOK:(h + 1) * TOK, :] = np.asarray(res.results[c]["outT"]).T
    return out


# version 07 (final)
# speedup vs baseline: 1.1178x; 1.0226x over previous
import numpy as np
from contextlib import ExitStack

import concourse.bass as bass
import concourse.mybir as mybir
from concourse.bass_utils import run_bass_kernel_spmd

F32 = mybir.dt.float32
BF16 = mybir.dt.bfloat16
AF = mybir.ActivationFunctionType
ALU = mybir.AluOpType
AX = mybir.AxisListType

B, T, D, DI, S = 4, 8192, 1024, 2048, 16
NC = 8
KD = D // 128
KI = DI // 128
SH = S // 2
LNEPS = 1e-5

TCP = 512
AGC = 2

ENG_SCAN = "vector"

ENGS = ("sync", "gpsimd", "scalar", "vector", "tensor")
GROUPS = [[0, 1], [2, 3], [4, 5], [6, 7]]


def _derive():
    global TOK, NCH, GA, NAG, NTG, DTU_COLS
    TOK = T // 2
    NCH = TOK // TCP
    GA = AGC * TCP
    NAG = NCH // AGC
    NTG = T // GA
    DTU_COLS = 2 * KI * GA + GA


_derive()


class Sem:
    __slots__ = ("h", "n")


class Rec:
    def __init__(self, nc, ctx):
        self.nc = nc
        self.ctx = ctx
        self.q = {e: [] for e in ENGS}
        self.auto = {}

    def sem(self, name):
        s = Sem()
        s.h = self.ctx.enter_context(self.nc.semaphore(name))
        s.n = 0
        return s

    def w(self, eng, sem, val=None):
        v = sem.n if val is None else val
        assert v <= sem.n, f"wait on future semaphore value {v} > {sem.n}"
        if v <= 0:
            return
        self.q[eng].append(lambda e, h=sem.h, v=v: e.wait_ge(h, v))

    def do(self, eng, meth, *args, inc=None, inck=1, lag=1, **kw):
        s = self.auto.get(eng) if meth not in ("dma_start",
                                               "collective_compute") else None
        if s is not None and s.n - (lag - 1) > 0:
            self.w(eng, s, s.n - (lag - 1))
        if s is not None and inc is None:
            inc = s
        if inc is not None:
            inc.n += inck

        def f(e, meth=meth, args=args, kw=kw, inc=inc, inck=inck):
            ins = getattr(e, meth)(*args, **kw)
            if inc is not None:
                ins.then_inc(inc.h, inck)

        self.q[eng].append(f)

    def dma(self, eng, out, in_, inc, nc_ok=False):
        kw = {"allow_slow_non_contiguous": True} if nc_ok else {}
        self.do(eng, "dma_start", inc=inc, inck=16, out=out, in_=in_, **kw)


class Arena:
    def __init__(self, handle, nelem, gran):
        self.h = handle
        self.ne = nelem
        self.off = 0
        self.gran = gran
        self.marks = []

    def alloc(self, n):
        nr = -(-n // self.gran) * self.gran
        assert self.off + nr <= self.ne, f"arena overflow {self.off}+{nr}>{self.ne}"
        ap = self.h[:, self.off: self.off + n]
        self.off += nr
        return ap

    def tiles(self, k, w):
        return [self.alloc(w) for _ in range(k)]

    def push(self):
        self.marks.append(self.off)

    def pop(self):
        self.off = self.marks.pop()


def build():
    nc = bass.Bass("TRN2", target_bir_lowering=False)
    ctx = ExitStack()
    W = TCP

    din = {}
    for name, shape in [
        ("xT", [D, 1 + TOK]),
        ("wrT", [D, D]), ("wkT", [D, D]), ("wvT", [D, D]),
        ("win", [D, DI]), ("wgate", [D, DI]), ("wdt", [DI, DI]),
        ("wb16", [DI, S]), ("wc16", [DI, S]), ("wout", [DI, D]),
        ("bdt", [128, KI]), ("mur", [128, KD]), ("muk", [128, KD]),
        ("nw", [128, KD]), ("nb", [128, KD]),
        ("alog", [128, KI * SH]), ("hmask", [128, 1]),
    ]:
        din[name] = nc.dram_tensor(name, shape, F32, kind="ExternalInput")
    outT = nc.dram_tensor("outT", [D, TOK], F32, kind="ExternalOutput")

    wr_b = nc.dram_tensor("wr_b", [128, 2 * KD * 512], BF16)
    wk_b = nc.dram_tensor("wk_b", [128, 2 * KD * 512], BF16)
    wv_b = nc.dram_tensor("wv_b", [128, 2 * KD * 512], BF16)
    win_b = nc.dram_tensor("win_b", [128, 4 * KD * 512], BF16)
    wgate_b = nc.dram_tensor("wgate_b", [128, 4 * KD * 512], BF16)
    wdt_b = nc.dram_tensor("wdt_b", [128, 8 * KD * 512], BF16)
    wout_b = nc.dram_tensor("wout_b", [128, 4 * KD * 512], BF16)

    r12_sp = nc.dram_tensor("r12_sp", [128, KD * TOK], F32)
    gate_sp = nc.dram_tensor("gate_sp", [128, KI * TOK], BF16)

    dtu_in = [nc.dram_tensor(f"dtu_in{g}", [128, DTU_COLS], BF16) for g in range(NAG)]
    dtu_out = [nc.dram_tensor(f"dtu_out{g}", [2, 128, DTU_COLS], BF16)
               for g in range(NAG)]
    y_in = [nc.dram_tensor(f"y_in{g}", [2, 128, KI * GA], BF16) for g in range(NAG)]
    y_out = [nc.dram_tensor(f"y_out{g}", [128, KI * GA], BF16) for g in range(NAG)]
    yc_in = nc.dram_tensor("yc_in", [128, KI], F32)
    yc_out = nc.dram_tensor("yc_out", [2, 128, KI], F32)
    rh_in = nc.dram_tensor("rh_in", [128, KD], F32)
    bsel_d = nc.dram_tensor("bsel_d", [2 * SH, GA], BF16)
    rh_out = nc.dram_tensor("rh_out", [2, 128, KD], F32)

    sb = lambda nm, shape, dt: ctx.enter_context(nc.sbuf_tensor(nm, shape, dt))
    AF32 = Arena(sb("af32", [128, 8600], F32), 8600, 4)
    ABF = Arena(sb("abf", [128, 26400], BF16), 26400, 16)
    sA = sb("sA", [128, KD * 512], BF16)
    sB = sb("sB", [128, KD * 512], BF16)
    wb_sb = sb("wb_sb", [128, KI * S], BF16)
    wc_sb = sb("wc_sb", [128, KI * S], BF16)
    bdt_sb = sb("bdt_sb", [128, KI], F32)
    mur_sb = sb("mur_sb", [128, KD], F32)
    muk_sb = sb("muk_sb", [128, KD], F32)
    nw_sb = sb("nw_sb", [128, KD], F32)
    nb_sb = sb("nb_sb", [128, KD], F32)
    A_sb = sb("A_sb", [128, KI * SH], F32)
    hmask_sb = sb("hmask_sb", [128, 1], F32)
    ceps = sb("ceps", [128, 1], F32)
    cone = sb("cone", [128, 1], F32)
    ones_cb = sb("ones_cb", [128, 1], BF16)
    ones_r = sb("ones_r", [1, 128], F32)
    carry = sb("carry", [128, KI * SH], F32)
    row_mean = sb("row_mean", [1, 512], F32)
    row_msq = sb("row_msq", [1, 512], F32)
    row_tmp = sb("row_tmp", [1, 512], F32)
    row_rstd = sb("row_rstd", [1, 512], F32)
    row_sd = sb("row_sd", [1, 512], F32)
    xn0_halo = sb("xn0_halo", [128, KD], BF16)
    xn3_halo = sb("xn3_halo", [128, KD], F32)
    y4095 = sb("y4095", [128, KI], F32)
    dts = [sb(f"dts{i}", [128, GA], BF16) for i in range(2)]
    uts = [sb(f"uts{i}", [128, GA], BF16) for i in range(2)]
    Bbc_all = sb("Bbc_all", [128, SH * GA], BF16)
    Cbc_all = sb("Cbc_all", [128, SH * GA], BF16)
    Bbc = [Bbc_all[:, i * GA:(i + 1) * GA] for i in range(SH)]
    Cbc = [Cbc_all[:, i * GA:(i + 1) * GA] for i in range(SH)]
    dAr = [sb(f"dAr{i}", [128, GA], F32) for i in range(2)]
    hbr2 = sb("hbr2", [128, 2 * GA], BF16)
    burb = [sb(f"burb{i}", [128, GA], BF16) for i in range(4)]
    hcp = [sb(f"hcp{i}", [128, 2 * GA], BF16) for i in range(2)]
    pp = [sb(f"pp{i}", [128, GA], BF16) for i in range(3)]
    ysc = [sb(f"ysc{i}", [128, GA], BF16) for i in range(2)]
    zero5 = sb("zero5", [128, 512], BF16)
    hm0_sb = sb("hm0_sb", [128, 1], F32)
    brow = [sb(f"brow{i}", [SH, GA], BF16) for i in range(4)]

    PS = [ctx.enter_context(nc.psum_tensor(f"ps{i}", [128, 512], F32))
          for i in range(8)]

    R = Rec(nc, ctx)
    st = R.sem("st")
    sv = R.sem("sv")
    pv = R.sem("pv")
    tt = R.sem("tt")
    sL = R.sem("sL")
    scf = [R.sem(f"scf{i}") for i in range(2)]
    scs = [R.sem(f"scs{i}") for i in range(2)]
    scd = [R.sem(f"scd{i}") for i in range(2)]
    sx = [R.sem(f"sx{i}") for i in range(2)]
    sw = [R.sem(f"sw{i}") for i in range(2)]
    sr12 = R.sem("sr12")
    sg = [R.sem(f"sg{i}") for i in range(2)]
    sd = [R.sem(f"sd{i}") for i in range(2)]
    sbcst = R.sem("sbcst")
    sbc = R.sem("sbc")
    sdu = [R.sem(f"sdu{i}") for i in range(2)]
    ssy = [R.sem(f"ssy{i}") for i in range(NTG)]
    syc = R.sem("syc")
    srh = R.sem("srh")
    sey = [R.sem(f"sey{i}") for i in range(2)]
    ser = [R.sem(f"ser{i}") for i in range(2)]
    sout = [R.sem(f"sout{i}") for i in range(2)]
    ccs = [R.sem(f"cc{i}") for i in range(2 * NAG + 3)]
    cc_n = [0]

    def next_cc():
        s = ccs[cc_n[0]]
        cc_n[0] += 1
        return s

    R.auto = {"vector": sv, "scalar": st, "gpsimd": pv}
    bank_free = [(sv, 0)] * 8

    def mm_group(bank_ap, bank_idx, pairs, guard=True):
        if guard:
            s0, v0 = bank_free[bank_idx]
            R.w("tensor", s0, v0)
        for i, (lh, rh) in enumerate(pairs):
            R.do("tensor", "matmul", bank_ap, lh, rh,
                 start=(i == 0), stop=(i == len(pairs) - 1),
                 inc=tt if i == len(pairs) - 1 else None)

    def ln_stats(cast_fn, w, kd=KD):
        s0, v0 = bank_free[2]
        ttv = []
        for k in range(kd):
            if k >= 2:
                R.w("scalar", tt, ttv[k - 2])
            vb, qb = cast_fn(k)
            if k == 0:
                R.w("tensor", s0, v0)
            R.w("tensor", st, st.n)
            R.do("tensor", "matmul", PS[2][0:1, :w], ones_cb[:, :], vb,
                 start=(k == 0), stop=(k == kd - 1))
            R.do("tensor", "matmul", PS[2][32:33, :w], ones_cb[:, :], qb,
                 start=(k == 0), stop=(k == kd - 1), tile_position=(0, 32),
                 inc=tt)
            ttv.append(tt.n)
        R.w("vector", tt, tt.n)
        inv = 1.0 / (kd * 128)
        R.do("vector", "tensor_scalar_mul", row_mean[:, :w], PS[2][0:1, :w], inv)
        R.do("vector", "tensor_scalar_mul", row_msq[:, :w], PS[2][32:33, :w], inv)
        R.do("vector", "tensor_mul", row_tmp[:, :w], row_mean[:, :w],
             row_mean[:, :w])
        R.do("vector", "tensor_sub", row_tmp[:, :w], row_msq[:, :w],
             row_tmp[:, :w], inc=sv)
        bank_free[2] = (sv, sv.n)
        R.w("scalar", sv, sv.n)
        R.do("scalar", "activation", row_sd[:, :w], row_tmp[:, :w], AF.Sqrt,
             bias=ceps[0:1, 0:1], inc=st)
        R.w("vector", st, st.n)
        R.do("vector", "reciprocal", row_rstd[:, :w], row_sd[:, :w], inc=sv)
        for bank, row in ((3, row_mean), (4, row_rstd)):
            s0, v0 = bank_free[bank]
            R.w("tensor", s0, v0)
            R.w("tensor", sv, sv.n)
            R.do("tensor", "matmul", PS[bank][:, :w], ones_r[:, :], row[:, :w],
                 start=True, stop=True, inc=tt)
        R.w("vector", tt, tt.n)

    def mini_ln_cols(vals_f32, vb, qb, kd):
        R.do("scalar", "activation", vb[:, :], vals_f32[:, :], AF.Copy, inc=st)
        R.do("scalar", "activation", qb[:, :], vals_f32[:, :], AF.Square, inc=st)
        s0b, v0b = bank_free[5]
        R.w("tensor", s0b, v0b)
        R.w("tensor", st, st.n)
        R.do("tensor", "matmul", PS[5][0:1, 0:kd], ones_cb[:, :], vb[:, :],
             start=True, stop=True)
        R.do("tensor", "matmul", PS[5][32:33, 0:kd], ones_cb[:, :], qb[:, :],
             start=True, stop=True, tile_position=(0, 32), inc=tt)
        R.w("vector", tt, tt.n)
        R.do("vector", "tensor_reduce", row_tmp[0:1, 8:9], PS[5][0:1, 0:kd],
             AX.X, ALU.add)
        R.do("vector", "tensor_reduce", row_tmp[0:1, 9:10],
             PS[5][32:33, 0:kd], AX.X, ALU.add)
        R.do("vector", "tensor_scalar_mul", row_tmp[0:1, 8:9],
             row_tmp[0:1, 8:9], 1.0 / D)
        R.do("vector", "tensor_scalar_mul", row_tmp[0:1, 9:10],
             row_tmp[0:1, 9:10], 1.0 / D)
        R.do("vector", "tensor_mul", row_tmp[0:1, 10:11], row_tmp[0:1, 8:9],
             row_tmp[0:1, 8:9])
        R.do("vector", "tensor_sub", row_tmp[0:1, 9:10], row_tmp[0:1, 9:10],
             row_tmp[0:1, 10:11], inc=sv)
        bank_free[5] = (sv, sv.n)
        R.w("scalar", sv, sv.n)
        R.do("scalar", "activation", row_tmp[0:1, 12:13], row_tmp[0:1, 9:10],
             AF.Sqrt, bias=ceps[0:1, 0:1], inc=st)
        R.w("vector", st, st.n)
        R.do("vector", "reciprocal", row_tmp[0:1, 11:12], row_tmp[0:1, 12:13],
             inc=sv)
        s0b, v0b = bank_free[5]
        R.w("tensor", s0b, v0b)
        R.w("tensor", sv, sv.n)
        R.do("tensor", "matmul", PS[5][:, 0:1], ones_r[:, :],
             row_tmp[0:1, 8:9], start=True, stop=True)
        R.do("tensor", "matmul", PS[5][:, 1:2], ones_r[:, :],
             row_tmp[0:1, 11:12], start=True, stop=True, inc=tt)
        R.w("vector", tt, tt.n)

    for name, tile in [
        ("bdt", bdt_sb), ("mur", mur_sb), ("muk", muk_sb), ("nw", nw_sb),
        ("nb", nb_sb), ("hmask", hmask_sb),
    ]:
        R.dma("sync", tile[:, :], din[name][:, :], sL)
    R.do("vector", "memset", ceps[:, :], LNEPS)
    R.do("vector", "memset", cone[:, :], 1.0)
    R.do("vector", "memset", ones_cb[:, :], 1.0)
    R.do("vector", "memset", ones_r[:, :], 1.0)
    R.do("vector", "memset", carry[:, :], 0.0)
    R.do("vector", "memset", zero5[:, :], 0.0)
    R.w("vector", sL)
    R.do("vector", "tensor_scalar", hm0_sb[:, :], hmask_sb[:, :], -1.0, 1.0,
         ALU.mult, ALU.add)
    R.do("vector", "memset", xn3_halo[:, :], 0.0, inc=sv)
    R.w("sync", sv, sv.n)
    for _g in range(NAG):
        for _c2 in range(AGC):
            _bco = 2 * KI * GA + _c2 * TCP
            R.dma("sync", dtu_in[_g][32:128, _bco:_bco + TCP],
                  zero5[0:96, :TCP], sbcst)
    AF32.push()
    atmp = AF32.alloc(KI * SH)
    R.dma("sync", atmp[:, :], din["alog"][:, :], sL)
    R.w("scalar", sL)
    R.do("scalar", "activation", A_sb[:, :], atmp[:, :], AF.Exp, inc=st)
    R.w("vector", st, st.n)
    R.do("vector", "tensor_scalar", A_sb[:, :], A_sb[:, :], 1e-6, 1.0, ALU.max,
         ALU.min)
    R.do("vector", "tensor_scalar_mul", A_sb[:, :], A_sb[:, :], -1.0, inc=sv)
    AF32.pop()
    R.w("sync", st, st.n)

    cast_fd = sb("cast_fd", [128, 1024], F32)
    cast_bd = sb("cast_bd", [128, 1024], BF16)
    cast_f = [cast_fd[:, 0:512], cast_fd[:, 512:1024]]
    cast_b = [cast_bd[:, 0:512], cast_bd[:, 512:1024]]
    cast_sv = []
    cast_n = [0]

    def cast_tile(src_ap, dst_ap, ssem):
        i = cast_n[0] % 2
        if cast_n[0] >= 2:
            R.w("sync", sv, cast_sv[cast_n[0] - 2])
        R.dma("sync", cast_f[i][:, :512], src_ap, scf[i])
        R.w("vector", scf[i])
        R.w("vector", scs[i])
        R.w("vector", scd[i])
        R.do("vector", "tensor_copy", cast_b[i][:, :512], cast_f[i][:, :512],
             inc=sv, lag=2)
        cast_sv.append(sv.n)
        R.w("sync", sv, sv.n)
        R.dma("sync", dst_ap, cast_b[i][:, :512], ssem[i])
        cast_n[0] += 1

    def cast_w(src, dst, n_k, n_pieces):
        for p in range(n_pieces):
            for k in range(n_k):
                cast_tile(src[k * 128:(k + 1) * 128, p * 512:(p + 1) * 512],
                          dst[:, (p * n_k + k) * 512:(p * n_k + k + 1) * 512],
                          scs)

    cast_w(din["wrT"], wr_b, KD, 2)
    cast_w(din["wkT"], wk_b, KD, 2)
    cast_w(din["wvT"], wv_b, KD, 2)
    cast_w(din["win"], win_b, KD, 4)
    cast_w(din["wgate"], wgate_b, KD, 4)
    cast_w(din["wout"], wout_b, KI, 2)
    for kh in range(2):
        for mq in range(4):
            for k in range(KD):
                p = kh * 4 + mq
                cast_tile(
                    din["wdt"][(kh * KD + k) * 128:(kh * KD + k + 1) * 128,
                               mq * 512:(mq + 1) * 512],
                    wdt_b[:, (p * KD + k) * 512:(p * KD + k + 1) * 512],
                    scd)
    for src, dst in [(din["wb16"], wb_sb), (din["wc16"], wc_sb)]:
        for k in range(KI):
            i = cast_n[0] % 2
            R.w("sync", sv, cast_sv[cast_n[0] - 2])
            R.dma("sync", cast_f[i][:, :S], src[k * 128:(k + 1) * 128, :],
                  scf[i])
            R.w("vector", scf[i])
            R.do("vector", "tensor_copy", dst[:, k * S:(k + 1) * S],
                 cast_f[i][:, :S], inc=sv)
            cast_sv.append(sv.n)
            cast_n[0] += 1
    wbc_ready = sv.n

    AF32.push()
    ABF.push()
    xf = AF32.tiles(2, W)
    tmpf = AF32.tiles(2, W)
    y1f = AF32.tiles(KD, W)
    dtt = AF32.tiles(2, W)
    rvf = AF32.tiles(2, W)
    hx = AF32.alloc(KD)
    xb = ABF.tiles(KD, W)
    xq = ABF.tiles(2, W)
    xn0b = [ABF.alloc(1 + W) for _ in range(KD)]
    blockA = ABF.tiles(KD, W)
    blockB = ABF.tiles(KD, W)
    rbt = ABF.tiles(KD, W)
    dtf = ABF.tiles(2, W)
    ub = ABF.tiles(2, W)
    gsil = ABF.tiles(2, W)
    mtb = ABF.tiles(2, W)
    bcst = ABF.alloc(W)
    hxb = ABF.alloc(KD)
    hxq = ABF.alloc(KD)
    xrb, kab = blockA, blockA
    xkb = blockB
    zb = blockA + blockB
    rb, t1b = rbt, rbt

    pstate = {"i": 0, "cons": []}

    def load_piece(src, p, n_k=KD):
        i = pstate["i"]
        buf = sA if i % 2 == 0 else sB
        if i >= 2:
            s0, v0 = pstate["cons"][i - 2]
            R.w("sync", s0, v0)
        R.dma("sync", buf[:, :n_k * 512],
              src[:, p * n_k * 512:(p + 1) * n_k * 512], sw[i % 2])
        pstate["cons"].append(None)
        pstate["i"] += 1
        return buf, sw[i % 2], i

    def piece_done(i):
        pstate["cons"][i] = (tt, tt.n)

    pre_ms = []
    PUMP = [lambda n: None]

    def pre_chunk(j):
        c0 = j * W
        g = j // AGC
        c2 = j % AGC
        prev = pre_ms[-1] if pre_ms else None
        cast_st = []

        def cast_x(k):
            if k < 2:
                if prev is not None:
                    R.w("sync", sv, prev["sv"])
                    R.w("sync", st, prev["st"])
            else:
                R.w("sync", st, cast_st[k - 2])
            R.dma("sync", xf[k % 2][:, :],
                  din["xT"][k * 128:(k + 1) * 128, 1 + c0:1 + c0 + W],
                  sx[k % 2])
            R.w("scalar", sx[k % 2])
            if j == 0 and k == 0:
                R.w("scalar", sL)
                R.w("vector", sL)
                R.w("tensor", sL)
            R.do("scalar", "activation", xb[k][:, :], xf[k % 2][:, :], AF.Copy,
                 inc=st)
            R.do("scalar", "activation", xq[k % 2][:, :], xf[k % 2][:, :],
                 AF.Square, inc=st)
            cast_st.append(st.n)
            return xb[k][:, :], xq[k % 2][:, :]

        ln_stats(cast_x, W)
        if j == 0:
            for k in range(KD):
                R.dma("sync", hx[:, k:k + 1],
                      din["xT"][k * 128:(k + 1) * 128, 0:1], sL, nc_ok=True)
            R.w("scalar", sL)
            mini_ln_cols(hx, hxb, hxq, KD)
            R.do("vector", "tensor_single_scalar", hx[:, :], hx[:, :],
                 PS[5][:, 0:1], ALU.subtract)
            R.do("vector", "tensor_single_scalar", xn0_halo[:, :], hx[:, :],
                 PS[5][:, 1:2], ALU.mult, inc=sv)
            bank_free[5] = (sv, sv.n)
            for k in range(KD):
                R.do("vector", "tensor_copy", xn0b[k][:, 0:1],
                     xn0_halo[:, k:k + 1])
        else:
            for k in range(KD):
                R.do("vector", "tensor_copy", xn0b[k][:, 0:1],
                     xn0b[k][:, W:W + 1])
        for k in range(KD):
            R.do("vector", "tensor_sub", tmpf[k % 2][:, :], xb[k][:, :],
                 PS[3][:, :W])
            R.do("vector", "tensor_mul", xn0b[k][:, 1:1 + W], tmpf[k % 2][:, :],
                 PS[4][:, :W], inc=sv if k == KD - 1 else None)
        bank_free[3] = (sv, sv.n)
        bank_free[4] = (sv, sv.n)
        for k in range(KD):
            cur = xn0b[k][:, 1:1 + W]
            prv = xn0b[k][:, 0:W]
            R.do("vector", "tensor_sub", mtb[k % 2][:, :], cur, prv)
            R.do("vector", "scalar_tensor_tensor", xrb[k][:, :], mtb[k % 2][:, :],
                 mur_sb[:, k:k + 1], prv, ALU.mult, ALU.add)
            R.do("vector", "scalar_tensor_tensor", xkb[k][:, :], mtb[k % 2][:, :],
                 muk_sb[:, k:k + 1], prv, ALU.mult, ALU.add,
                 inc=sv if k == KD - 1 else None)
        mixes = sv.n
        PUMP[0](2)
        for mh in range(2):
            buf, wld, pidx = load_piece(wr_b, mh)
            for ml in range(4):
                mo = mh * 4 + ml
                bk = (0, 1, 6)[mo % 3]
                R.w("tensor", sv, mixes)
                R.w("tensor", wld)
                mm_group(PS[bk][:, :W], bk,
                         [(buf[:, k * 512 + ml * 128:k * 512 + (ml + 1) * 128],
                           xrb[k][:, :]) for k in range(KD)])
                R.w("scalar", tt, tt.n)
                R.do("scalar", "activation", rb[mo][:, :], PS[bk][:, :W],
                     AF.Sigmoid, inc=st)
                bank_free[bk] = (st, st.n)
            piece_done(pidx)
        for mh in range(2):
            buf, wld, pidx = load_piece(wk_b, mh)
            for ml in range(4):
                mo = mh * 4 + ml
                bk = (0, 1, 6)[mo % 3]
                R.w("tensor", wld)
                mm_group(PS[bk][:, :W], bk,
                         [(buf[:, k * 512 + ml * 128:k * 512 + (ml + 1) * 128],
                           xkb[k][:, :]) for k in range(KD)])
                R.w("scalar", tt, tt.n)
                R.do("scalar", "activation", xq[0][:, :], PS[bk][:, :W],
                     AF.Relu, inc=st)
                bank_free[bk] = (st, st.n)
                R.do("scalar", "activation", kab[mo][:, :], xq[0][:, :],
                     AF.Square, inc=st)
            piece_done(pidx)
        ka_done = st.n
        xld = []
        for mh in range(2):
            buf, wld, pidx = load_piece(wv_b, mh)
            for ml in range(4):
                mo = mh * 4 + ml
                bk = (0, 1, 6)[mo % 3]
                R.w("tensor", st, ka_done)
                R.w("tensor", wld)
                mm_group(PS[bk][:, :W], bk,
                         [(buf[:, k * 512 + ml * 128:k * 512 + (ml + 1) * 128],
                           kab[k][:, :]) for k in range(KD)])
                if mo >= 2:
                    R.w("sync", sv, xld[mo - 2])
                else:
                    R.w("sync", st, cast_st[-1])
                R.dma("sync", xf[mo % 2][:, :],
                      din["xT"][mo * 128:(mo + 1) * 128, 1 + c0:1 + c0 + W],
                      sx[mo % 2])
                R.w("vector", tt, tt.n)
                R.w("vector", sx[mo % 2])
                if j > 0 and mo == 0:
                    R.w("vector", sr12)
                R.do("vector", "tensor_mul", rvf[mo % 2][:, :], rb[mo][:, :],
                     PS[bk][:, :W])
                R.do("vector", "scalar_tensor_tensor", y1f[mo][:, :],
                     rvf[mo % 2][:, :], 0.5, xf[mo % 2][:, :], ALU.mult,
                     ALU.add, inc=sv)
                bank_free[bk] = (sv, sv.n)
                xld.append(sv.n)
            piece_done(pidx)
        y1_done = sv.n
        PUMP[0](2)

        def cast_y1(k):
            R.w("scalar", sv, y1_done)
            R.do("scalar", "activation", xb[k][:, :], y1f[k][:, :], AF.Copy,
                 inc=st)
            R.do("scalar", "activation", xq[k % 2][:, :], y1f[k][:, :],
                 AF.Square, inc=st)
            return xb[k][:, :], xq[k % 2][:, :]

        ln_stats(cast_y1, W)
        for k in range(KD):
            R.do("vector", "tensor_sub", tmpf[k % 2][:, :], y1f[k][:, :],
                 PS[3][:, :W])
            R.do("vector", "tensor_mul", t1b[k][:, :], tmpf[k % 2][:, :],
                 PS[4][:, :W])
            R.do("vector", "tensor_add", y1f[k][:, :], y1f[k][:, :],
                 t1b[k][:, :], inc=sv)
        bank_free[3] = (sv, sv.n)
        bank_free[4] = (sv, sv.n)
        r12_done = sv.n
        R.w("sync", sv, r12_done)
        for k in range(KD):
            R.dma("sync", r12_sp[:, k * TOK + c0:k * TOK + c0 + W],
                  y1f[k][:, :], sr12)

        def cast_t1(k):
            R.w("scalar", sv, r12_done)
            R.do("scalar", "activation", xq[k % 2][:, :], t1b[k][:, :],
                 AF.Square, inc=st)
            return t1b[k][:, :], xq[k % 2][:, :]

        ln_stats(cast_t1, W)
        for k in range(KD):
            R.do("vector", "tensor_sub", tmpf[k % 2][:, :], t1b[k][:, :],
                 PS[3][:, :W])
            R.do("vector", "tensor_mul", tmpf[k % 2][:, :], tmpf[k % 2][:, :],
                 PS[4][:, :W])
            R.do("vector", "tensor_scalar", xb[k][:, :], tmpf[k % 2][:, :],
                 nw_sb[:, k:k + 1], nb_sb[:, k:k + 1], ALU.mult, ALU.add,
                 inc=sv if k == KD - 1 else None)
        xn_ready = sv.n
        bank_free[3] = (sv, sv.n)
        bank_free[4] = (sv, sv.n)
        PUMP[0](2)
        for p in range(4):
            buf, wld, pidx = load_piece(win_b, p)
            for ml in range(4):
                mo = p * 4 + ml
                bk = (0, 1, 6)[mo % 3]
                R.w("tensor", sv, xn_ready)
                R.w("tensor", wld)
                mm_group(PS[bk][:, :W], bk,
                         [(buf[:, k * 512 + ml * 128:k * 512 + (ml + 1) * 128],
                           xb[k][:, :]) for k in range(KD)])
                R.w("scalar", tt, tt.n)
                R.do("scalar", "activation", zb[mo][:, :], PS[bk][:, :W],
                     AF.Copy, inc=st)
                bank_free[bk] = (st, st.n)
            piece_done(pidx)
        for p in range(4):
            buf, wld, pidx = load_piece(wgate_b, p)
            for ml in range(4):
                mo = p * 4 + ml
                bk = (0, 1, 6)[mo % 3]
                R.w("tensor", wld)
                mm_group(PS[bk][:, :W], bk,
                         [(buf[:, k * 512 + ml * 128:k * 512 + (ml + 1) * 128],
                           xb[k][:, :]) for k in range(KD)])
                R.w("scalar", tt, tt.n)
                R.w("scalar", sg[mo % 2])
                R.do("scalar", "activation", gsil[mo % 2][:, :], PS[bk][:, :W],
                     AF.Silu, inc=st)
                bank_free[bk] = (st, st.n)
                R.w("sync", st, st.n)
                R.dma("sync", gate_sp[:, mo * TOK + c0:mo * TOK + c0 + W],
                      gsil[mo % 2][:, :], sg[mo % 2])
            piece_done(pidx)
        PUMP[0](3)
        if j == 0:
            R.w("sync", scd[0])
            R.w("sync", scd[1])
        for mq in range(4):
            b0 = load_piece(wdt_b, 0 * 4 + mq)
            b1 = load_piece(wdt_b, 1 * 4 + mq)
            mq_tt = []
            for ml in range(4):
                mo = mq * 4 + ml
                bk = (0, 1, 6, 7)[ml]
                s0b, v0b = bank_free[bk]
                R.w("tensor", s0b, v0b)
                R.w("tensor", b0[1])
                R.w("tensor", b1[1])
                pairs = []
                for kk in range(KI):
                    buf = (b0 if kk < KD else b1)[0]
                    k = kk % KD
                    pairs.append(
                        (buf[:, k * 512 + ml * 128:k * 512 + (ml + 1) * 128],
                         zb[kk][:, :]))
                mm_group(PS[bk][:, :W], bk, pairs, guard=False)
                mq_tt.append(tt.n)
            piece_done(b0[2])
            piece_done(b1[2])
            for ml in range(4):
                mo = mq * 4 + ml
                bk = (0, 1, 6, 7)[ml]
                R.w("scalar", tt, mq_tt[ml])
                R.w("scalar", sv, sv.n)
                R.do("scalar", "activation", dtt[ml % 2][:, :], PS[bk][:, :W],
                     AF.Exp, bias=bdt_sb[:, mo:mo + 1], inc=st)
                R.do("scalar", "activation", dtt[ml % 2][:, :],
                     dtt[ml % 2][:, :], AF.Ln, bias=cone[:, 0:1], inc=st)
                bank_free[bk] = (st, st.n)
                R.w("vector", st, st.n)
                R.w("vector", sd[ml % 2])
                R.do("vector", "tensor_scalar", dtf[ml % 2][:, :],
                     dtt[ml % 2][:, :], 1e-3, 1.0, ALU.max, ALU.min)
                R.do("vector", "tensor_mul", ub[ml % 2][:, :], dtf[ml % 2][:, :],
                     zb[mo][:, :], inc=sv)
                R.w("sync", sv, sv.n)
                base = mo * GA + c2 * W
                R.dma("sync", dtu_in[g][:, base:base + W], dtf[ml % 2][:, :],
                      sd[ml % 2])
                R.dma("sync", dtu_in[g][:, KI * GA + base:KI * GA + base + W],
                      ub[ml % 2][:, :], sd[ml % 2])
        s0b, v0b = bank_free[5]
        R.w("tensor", s0b, v0b)
        R.w("tensor", sv, wbc_ready)
        for kk in range(KI):
            R.do("tensor", "matmul", PS[5][0:16, :W],
                 wb_sb[:, kk * S:(kk + 1) * S], zb[kk][:, :],
                 start=(kk == 0), stop=(kk == KI - 1))
        for kk in range(KI):
            R.do("tensor", "matmul", PS[5][32:48, :W],
                 wc_sb[:, kk * S:(kk + 1) * S], zb[kk][:, :],
                 start=(kk == 0), stop=(kk == KI - 1), tile_position=(0, 32),
                 inc=tt if kk == KI - 1 else None)
        R.w("vector", tt, tt.n)
        R.w("vector", sbcst)
        R.do("vector", "tensor_copy", bcst[0:16, :W], PS[5][0:16, :W])
        R.do("vector", "tensor_copy", bcst[32:48, :W], PS[5][32:48, :W], inc=sv)
        bank_free[5] = (sv, sv.n)
        R.w("sync", sv, sv.n)
        bco = 2 * KI * GA + c2 * W
        R.dma("sync", dtu_in[g][0:16, bco:bco + W], bcst[0:16, :W], sbcst)
        R.dma("sync", dtu_in[g][16:32, bco:bco + W], bcst[32:48, :W], sbcst)
        PUMP[0](2)
        pre_ms.append({"sv": sv.n, "st": st.n, "tt": tt.n})

    scan_ms = []

    scan_state = {"tg": -1, "k": 0, "tms": None, "sts": None}

    def scan_prep(tg):
        half = tg // NAG
        gg = tg % NAG
        src = dtu_out[gg]
        R.w("sync", ag_sems[gg])
        if scan_ms:
            R.w("sync", sv, scan_ms[-1]["sv"])
        bco = 2 * KI * GA
        R.w("sync", sv, sv.n)
        R.dma("sync", brow[0][:, :], src[half, 0:SH, bco:bco + GA], sbc)
        R.dma("sync", brow[1][:, :], src[half, SH:2 * SH, bco:bco + GA], sbc)
        R.dma("sync", brow[2][:, :], src[half, 16:16 + SH, bco:bco + GA], sbc)
        R.dma("sync", brow[3][:, :], src[half, 16 + SH:16 + 2 * SH,
                                         bco:bco + GA], sbc)
        R.w("vector", sbc)
        R.do("vector", "tensor_single_scalar", brow[0][:, :], brow[0][:, :],
             hm0_sb[0:SH, 0:1], ALU.mult)
        R.do("vector", "tensor_single_scalar", brow[2][:, :], brow[2][:, :],
             hm0_sb[0:SH, 0:1], ALU.mult)
        R.do("vector", "scalar_tensor_tensor", brow[0][:, :], brow[1][:, :],
             hmask_sb[0:SH, 0:1], brow[0][:, :], ALU.mult, ALU.add, lag=2)
        R.do("vector", "scalar_tensor_tensor", brow[2][:, :], brow[3][:, :],
             hmask_sb[0:SH, 0:1], brow[2][:, :], ALU.mult, ALU.add, inc=sv,
             lag=2)
        R.w("sync", sv, sv.n)
        R.dma("sync", bsel_d[0:SH, :], brow[0][:, :], sbc)
        R.dma("sync", bsel_d[SH:2 * SH, :], brow[2][:, :], sbc)
        R.w("sync", sbc)
        for si in range(SH):
            R.dma("sync", Bbc[si][:, :],
                  bsel_d[si:si + 1, :].to_broadcast((128, GA)), sbc)
            R.dma("sync", Cbc[si][:, :],
                  bsel_d[SH + si:SH + si + 1, :].to_broadcast((128, GA)), sbc)
        scan_state["tg"] = tg
        scan_state["k"] = 0
        scan_state["tms"] = []
        scan_state["sts"] = {}

    def scan_ktile():
        tg = scan_state["tg"]
        k = scan_state["k"]
        half = tg // NAG
        gg = tg % NAG
        src = dtu_out[gg]
        tile_ms = scan_state["tms"]
        st_hist = scan_state["sts"]
        kb = k % 2
        if k >= 2:
            m = tile_ms[k - 2]
            R.w("sync", st, m["st"])
            R.w("sync", sv, m["sv"])
        R.dma("sync", dts[kb][:, :],
              src[half, :, k * GA:(k + 1) * GA], sdu[kb])
        R.dma("sync", uts[kb][:, :],
              src[half, :, KI * GA + k * GA:KI * GA + (k + 1) * GA],
              sdu[kb])
        scan_hist = []

        def emit_exp(si):
            col = k * SH + si
            R.w("scalar", sdu[kb])
            if si >= 2:
                R.w("scalar", sv, scan_hist[si - 2])
            R.do("scalar", "activation", dAr[si % 2][:, :], dts[kb][:, :],
                 AF.Exp, scale=A_sb[:, col:col + 1], inc=st, lag=2)
            st_hist[si] = st.n

        def B(si, lag=4):
            R.w("vector", sdu[kb])
            R.w("vector", sbc)
            R.do("vector", "tensor_mul", burb[si % 4][:, :], uts[kb][:, :],
                 Bbc[si][:, :], inc=sv, lag=lag)

        def Sc(si, lag=2):
            emit_exp(si)
            R.w("vector", st, st_hist[si])
            col = k * SH + si
            o = (si % 2) * GA
            R.do("vector", "tensor_tensor_scan", hbr2[:, o:o + GA],
                 dAr[si % 2][:, :], burb[si % 4][:, :],
                 carry[:, col:col + 1], ALU.mult, ALU.add, inc=sv, lag=lag)
            scan_hist.append(sv.n)

        def Cc(si, lag=4):
            col = k * SH + si
            o = (si % 2) * GA
            R.do("vector", "tensor_copy", carry[:, col:col + 1],
                 hbr2[:, o + GA - 1:o + GA], inc=sv, lag=lag)

        def H(j, lag=2):
            R.do("vector", "tensor_mul", hcp[j % 2][:, :], hbr2[:, :],
                 Cbc_all[:, 2 * j * GA:(2 * j + 2) * GA], inc=sv, lag=lag)

        def P(j, dst, lag=2):
            h = hcp[j % 2]
            R.do("vector", "tensor_add", dst[:, :], h[:, 0:GA],
                 h[:, GA:2 * GA], inc=sv, lag=lag)

        B(0)
        B(1)
        B(2)
        B(3)
        Sc(0, lag=4)
        Sc(1, lag=4)
        B(4, lag=2)
        H(0)
        Cc(0)
        Cc(1)
        B(5, lag=4)
        Sc(2, lag=3)
        Sc(3, lag=3)
        P(0, pp[0], lag=4)
        H(1, lag=2)
        Cc(2)
        Cc(3)
        B(6, lag=6)
        Sc(4, lag=3)
        B(7, lag=6)
        Sc(5, lag=2)
        P(1, pp[1], lag=6)
        H(2, lag=2)
        Cc(4)
        Cc(5)
        R.do("vector", "tensor_add", pp[2][:, :], pp[0][:, :],
             pp[1][:, :], inc=sv, lag=4)
        Sc(6, lag=3)
        Sc(7, lag=2)
        P(2, pp[0], lag=3)
        H(3, lag=2)
        Cc(6)
        Cc(7)
        R.do("vector", "tensor_add", pp[1][:, :], pp[2][:, :],
             pp[0][:, :], inc=sv, lag=4)
        P(3, hbr2[:, 0:GA], lag=2)
        if k >= 2:
            R.w("vector", ssy[tg])
        elif tg >= 1:
            R.w("vector", ssy[tg - 1])
        R.do("vector", "tensor_add", ysc[k % 2][:, :], pp[1][:, :],
             hbr2[:, 0:GA], inc=sv, lag=1)
        if tg == NAG - 1:
            R.do("vector", "tensor_copy", y4095[:, k:k + 1],
                 ysc[k % 2][:, GA - 1:GA], inc=sv)
        R.w("sync", sv, sv.n)
        R.dma("sync", y_in[gg][half, :, k * GA:(k + 1) * GA],
              ysc[k % 2][:, :], ssy[tg])
        tile_ms.append({"st": st.n, "sv": sv.n})
        scan_state["k"] = k + 1
        if scan_state["k"] == KI:
            scan_ms.append({"sv": sv.n})

    ag_sems = {}
    yc_sem = [None]
    done = [0]
    next_tg = [0]

    R.w("sync", scs[0])
    R.w("sync", scs[1])

    def emit_ag(g):
        R.w("gpsimd", sd[0])
        R.w("gpsimd", sd[1])
        R.w("gpsimd", sbcst)
        s = next_cc()
        R.do("gpsimd", "collective_compute", "AllGather", ALU.bypass,
             GROUPS, [dtu_in[g].ap().opt()], [dtu_out[g].ap().opt()], inc=s)
        ag_sems[g] = s

    def pump(n):
        while n > 0 and done[0] < NTG * KI:
            if scan_state["tg"] < 0 or scan_state["k"] == KI:
                tg = next_tg[0]
                if tg >= NTG or (tg % NAG) not in ag_sems:
                    return
                scan_prep(tg)
                next_tg[0] += 1
            scan_ktile()
            done[0] += 1

    PUMP[0] = pump
    for j in range(NCH):
        pre_chunk(j)
        if j % AGC == AGC - 1:
            emit_ag(j // AGC)
    while done[0] < NAG * KI:
        pump(KI)
    R.w("sync", sv, scan_ms[-1]["sv"])
    R.dma("sync", yc_in[:, :], y4095[:, :], syc)
    R.w("gpsimd", syc)
    _s = next_cc()
    R.do("gpsimd", "collective_compute", "AllGather", ALU.bypass,
         GROUPS, [yc_in.ap().opt()], [yc_out.ap().opt()], inc=_s)
    yc_sem[0] = _s
    while done[0] < (NAG + 1) * KI:
        pump(KI)

    rs_sems = {}

    def emit_rs(g):
        R.w("gpsimd", ssy[g])
        R.w("gpsimd", ssy[NAG + g])
        s = next_cc()
        R.do("gpsimd", "collective_compute", "ReduceScatter", ALU.add,
             GROUPS, [y_in[g].ap().opt()], [y_out[g].ap().opt()], inc=s)
        rs_sems[g] = s

    AF32.pop()
    ABF.pop()

    AF32.push()
    ABF.push()
    y2f = AF32.tiles(KD, W)
    tmpe = AF32.tiles(2, W)
    r12e = AF32.tiles(2, W)
    rv3 = AF32.tiles(2, W)
    outf = AF32.tiles(2, W)
    r12col = AF32.alloc(KD)
    rhl = AF32.alloc(KD)
    yca = AF32.tiles(2, KI)
    blockE = ABF.tiles(KI, W)
    ytl = ABF.tiles(2, W)
    gte = ABF.tiles(2, W)
    y2q = ABF.tiles(2, W)
    xn3b = [ABF.alloc(1 + W) for _ in range(KD)]
    xr3 = ABF.tiles(KD, W)
    xk3 = ABF.tiles(KD, W)
    mt3 = ABF.tiles(2, W)
    gcol = ABF.alloc(KI)
    ygcol = ABF.alloc(KI)
    hxb3 = ABF.alloc(KD)
    hxq3 = ABF.alloc(KD)
    ygb = blockE
    y2b = blockE[0:KD]
    rb3 = blockE[KD:KI]
    ka3 = xr3

    R.w("sync", yc_sem[0])
    R.dma("sync", yca[0][:, :], yc_out[0], sL)
    R.dma("sync", yca[1][:, :], yc_out[1], sL)
    R.w("vector", sL)
    R.do("vector", "tensor_add", y4095[:, :], yca[0][:, :], yca[1][:, :],
         inc=sv)
    yca_done = sv.n
    R.w("sync", sv, yca_done)
    R.w("sync", sg[0])
    R.w("sync", sg[1])
    R.w("sync", sr12)
    for kk in range(KI):
        R.dma("sync", gcol[:, kk:kk + 1],
              gate_sp[:, kk * TOK + TOK - 1:kk * TOK + TOK], sL, nc_ok=True)
    R.w("vector", sL)
    R.do("vector", "tensor_mul", ygcol[:, :], y4095[:, :], gcol[:, :], inc=sv)
    yg_col_v = sv.n
    s0b, v0b = bank_free[5]
    R.w("tensor", s0b, v0b)
    R.w("tensor", sv, yg_col_v)
    first = True
    for pz in range(4):
        mh, kh = pz // 2, pz % 2
        buf, wld, pidx = load_piece(wout_b, pz)
        R.w("tensor", wld)
        for ml in range(4):
            mo = mh * 4 + ml
            for k in range(KD):
                kk = kh * KD + k
                last = (pz == 3 and ml == 3 and k == KD - 1)
                piece_last = (ml == 3 and k == KD - 1)
                R.do("tensor", "matmul", PS[5][:, mo:mo + 1],
                     buf[:, k * 512 + ml * 128:k * 512 + (ml + 1) * 128],
                     ygcol[:, kk:kk + 1],
                     start=first, stop=last, inc=tt if piece_last else None)
                first = False
        piece_done(pidx)
    R.w("sync", sv, yg_col_v)
    for k in range(KD):
        R.dma("sync", r12col[:, k:k + 1],
              r12_sp[:, k * TOK + TOK - 1:k * TOK + TOK], sL, nc_ok=True)
    R.w("vector", sL)
    R.w("vector", tt, tt.n)
    R.do("vector", "tensor_add", r12col[:, :], r12col[:, :], PS[5][:, 0:KD],
         inc=sv)
    bank_free[5] = (sv, sv.n)
    R.w("scalar", sv, sv.n)
    mini_ln_cols(r12col, hxb3, hxq3, KD)
    R.do("vector", "tensor_single_scalar", r12col[:, :], r12col[:, :],
         PS[5][:, 0:1], ALU.subtract)
    R.do("vector", "tensor_single_scalar", r12col[:, :], r12col[:, :],
         PS[5][:, 1:2], ALU.mult, inc=sv)
    bank_free[5] = (sv, sv.n)
    R.w("sync", sv, sv.n)
    R.dma("sync", rh_in[:, :], r12col[:, :], srh)
    R.w("gpsimd", srh)
    rh_sem = next_cc()
    R.do("gpsimd", "collective_compute", "AllGather", ALU.bypass,
         GROUPS, [rh_in.ap().opt()], [rh_out.ap().opt()], inc=rh_sem)
    R.w("sync", rh_sem)
    R.dma("sync", rhl[:, :], rh_out[0], sL)
    R.w("vector", sL)
    R.do("vector", "tensor_single_scalar", xn3_halo[:, :], rhl[:, :],
         hmask_sb[:, 0:1], ALU.mult, inc=sv)
    halo3_done = sv.n

    epi_ms = []

    def epi_chunk(e):
        c0 = e * W
        g = e // AGC
        prev = epi_ms[-1] if epi_ms else None
        R.w("sync", rs_sems[g])
        if prev is None:
            R.w("sync", sg[0])
            R.w("sync", sg[1])
            R.w("sync", sr12)
        if prev is not None:
            R.w("sync", sv, prev["sv"])
            R.w("sync", st, prev["st"])
        else:
            R.w("sync", sv, halo3_done)
        yg_st = []
        for kk in range(KI):
            if kk >= 2:
                R.w("sync", sv, yg_st[kk - 2])
            R.dma("sync", ytl[kk % 2][:, :],
                  y_out[g][:, kk * GA + (e % AGC) * W:
                           kk * GA + (e % AGC + 1) * W], sey[kk % 2])
            R.dma("sync", gte[kk % 2][:, :],
                  gate_sp[:, kk * TOK + c0:kk * TOK + c0 + W], sey[kk % 2])
            R.w("vector", sey[kk % 2])
            R.do("vector", "tensor_mul", ygb[kk][:, :], ytl[kk % 2][:, :],
                 gte[kk % 2][:, :], inc=sv)
            yg_st.append(sv.n)
        yg_done = sv.n
        for mh in range(2):
            bA = load_piece(wout_b, mh * 2 + 0)
            bX = load_piece(wout_b, mh * 2 + 1)
            for ml in range(4):
                mo = mh * 4 + ml
                bk = (0, 1, 6)[mo % 3]
                s0b, v0b = bank_free[bk]
                R.w("tensor", s0b, v0b)
                R.w("tensor", bA[1])
                R.w("tensor", bX[1])
                R.w("tensor", sv, yg_done)
                pairs = []
                for kk in range(KI):
                    buf = (bA if kk < KD else bX)[0]
                    k = kk % KD
                    pairs.append(
                        (buf[:, k * 512 + ml * 128:k * 512 + (ml + 1) * 128],
                         ygb[kk][:, :]))
                mm_group(PS[bk][:, :W], bk, pairs, guard=False)
                if mo >= 2:
                    R.w("sync", sv, sv.n)
                R.dma("sync", r12e[mo % 2][:, :],
                      r12_sp[:, mo * TOK + c0:mo * TOK + c0 + W], ser[mo % 2])
                R.w("vector", tt, tt.n)
                R.w("vector", ser[mo % 2])
                R.do("vector", "tensor_add", y2f[mo][:, :], PS[bk][:, :W],
                     r12e[mo % 2][:, :], inc=sv)
                bank_free[bk] = (sv, sv.n)
            piece_done(bA[2])
            piece_done(bX[2])
        y2_done = sv.n

        def cast_y2(k):
            R.w("scalar", sv, y2_done)
            R.do("scalar", "activation", y2b[k][:, :], y2f[k][:, :], AF.Copy,
                 inc=st)
            R.do("scalar", "activation", y2q[k % 2][:, :], y2f[k][:, :],
                 AF.Square, inc=st)
            return y2b[k][:, :], y2q[k % 2][:, :]

        ln_stats(cast_y2, W)
        if e == 0:
            R.w("vector", sv, halo3_done)
            for k in range(KD):
                R.do("vector", "tensor_copy", xn3b[k][:, 0:1],
                     xn3_halo[:, k:k + 1])
        else:
            for k in range(KD):
                R.do("vector", "tensor_copy", xn3b[k][:, 0:1],
                     xn3b[k][:, W:W + 1])
        for k in range(KD):
            R.do("vector", "tensor_sub", tmpe[k % 2][:, :], y2f[k][:, :],
                 PS[3][:, :W])
            R.do("vector", "tensor_mul", xn3b[k][:, 1:1 + W], tmpe[k % 2][:, :],
                 PS[4][:, :W], inc=sv if k == KD - 1 else None)
        bank_free[3] = (sv, sv.n)
        bank_free[4] = (sv, sv.n)
        for k in range(KD):
            cur = xn3b[k][:, 1:1 + W]
            prv = xn3b[k][:, 0:W]
            R.do("vector", "tensor_sub", mt3[k % 2][:, :], cur, prv)
            R.do("vector", "scalar_tensor_tensor", xr3[k][:, :], mt3[k % 2][:, :],
                 mur_sb[:, k:k + 1], prv, ALU.mult, ALU.add)
            R.do("vector", "scalar_tensor_tensor", xk3[k][:, :], mt3[k % 2][:, :],
                 muk_sb[:, k:k + 1], prv, ALU.mult, ALU.add,
                 inc=sv if k == KD - 1 else None)
        mixes = sv.n
        for mh in range(2):
            buf, wld, pidx = load_piece(wr_b, mh)
            for ml in range(4):
                mo = mh * 4 + ml
                bk = (0, 1, 6)[mo % 3]
                R.w("tensor", sv, mixes)
                R.w("tensor", wld)
                mm_group(PS[bk][:, :W], bk,
                         [(buf[:, k * 512 + ml * 128:k * 512 + (ml + 1) * 128],
                           xr3[k][:, :]) for k in range(KD)])
                R.w("scalar", tt, tt.n)
                R.do("scalar", "activation", rb3[mo % KD][:, :], PS[bk][:, :W],
                     AF.Sigmoid, inc=st)
                bank_free[bk] = (st, st.n)
            piece_done(pidx)
        for mh in range(2):
            buf, wld, pidx = load_piece(wk_b, mh)
            for ml in range(4):
                mo = mh * 4 + ml
                bk = (0, 1, 6)[mo % 3]
                R.w("tensor", wld)
                mm_group(PS[bk][:, :W], bk,
                         [(buf[:, k * 512 + ml * 128:k * 512 + (ml + 1) * 128],
                           xk3[k][:, :]) for k in range(KD)])
                R.w("scalar", tt, tt.n)
                R.do("scalar", "activation", y2q[0][:, :], PS[bk][:, :W],
                     AF.Relu, inc=st)
                bank_free[bk] = (st, st.n)
                R.do("scalar", "activation", ka3[mo][:, :], y2q[0][:, :],
                     AF.Square, inc=st)
            piece_done(pidx)
        ka_done = st.n
        for mh in range(2):
            buf, wld, pidx = load_piece(wv_b, mh)
            for ml in range(4):
                mo = mh * 4 + ml
                bk = (0, 1, 6)[mo % 3]
                R.w("tensor", st, ka_done)
                R.w("tensor", wld)
                mm_group(PS[bk][:, :W], bk,
                         [(buf[:, k * 512 + ml * 128:k * 512 + (ml + 1) * 128],
                           ka3[k][:, :]) for k in range(KD)])
                R.w("vector", tt, tt.n)
                R.w("vector", sout[mo % 2])
                R.do("vector", "tensor_mul", rv3[mo % 2][:, :],
                     rb3[mo % KD][:, :], PS[bk][:, :W])
                R.do("vector", "scalar_tensor_tensor", outf[mo % 2][:, :],
                     rv3[mo % 2][:, :], 0.5, y2f[mo][:, :], ALU.mult, ALU.add,
                     inc=sv)
                bank_free[bk] = (sv, sv.n)
                R.w("sync", sv, sv.n)
                R.dma("sync", outT[mo * 128:(mo + 1) * 128, c0:c0 + W],
                      outf[mo % 2][:, :], sout[mo % 2])
            piece_done(pidx)
        epi_ms.append({"sv": sv.n, "st": st.n})

    for g in range(NAG):
        emit_rs(g)
        epi_chunk(2 * g)
        pump(KI // 2)
        epi_chunk(2 * g + 1)
        pump(KI // 2)
    pump(NTG * KI)

    AF32.pop()
    ABF.pop()

    R.w("gpsimd", sout[0])
    R.w("gpsimd", sout[1])

    with ctx:
        with nc.Block() as block:
            @block.sync
            def _(e):
                for f in R.q["sync"]:
                    f(e)

            @block.gpsimd
            def _(e):
                for f in R.q["gpsimd"]:
                    f(e)

            @block.scalar
            def _(e):
                for f in R.q["scalar"]:
                    f(e)

            @block.vector
            def _(e):
                for f in R.q["vector"]:
                    f(e)

            @block.tensor
            def _(e):
                for f in R.q["tensor"]:
                    f(e)

    return nc


_NC_CACHE = {}


def _get_nc():
    key = (T, TCP)
    if key not in _NC_CACHE:
        _NC_CACHE[key] = build()
    return _NC_CACHE[key]


def kernel(
    L0_attn_x, L0_attn_norm_w, L0_attn_norm_b, L0_attn_W_in, L0_attn_W_gate,
    L0_attn_W_dt, L0_attn_b_dt, L0_attn_W_B, L0_attn_W_C, L0_attn_A_log,
    L0_attn_W_out, L0_ffn_W_r, L0_ffn_W_k, L0_ffn_W_v, L0_ffn_mu_r, L0_ffn_mu_k,
):
    f32 = np.float32
    x = np.asarray(L0_attn_x, f32)
    shared = {
        "wrT": np.ascontiguousarray(np.asarray(L0_ffn_W_r, f32).T),
        "wkT": np.ascontiguousarray(np.asarray(L0_ffn_W_k, f32).T),
        "wvT": np.ascontiguousarray(np.asarray(L0_ffn_W_v, f32).T),
        "win": np.ascontiguousarray(np.asarray(L0_attn_W_in, f32)),
        "wgate": np.ascontiguousarray(np.asarray(L0_attn_W_gate, f32)),
        "wdt": np.ascontiguousarray(np.asarray(L0_attn_W_dt, f32)),
        "wb16": np.ascontiguousarray(np.asarray(L0_attn_W_B, f32)),
        "wc16": np.ascontiguousarray(np.asarray(L0_attn_W_C, f32)),
        "wout": np.ascontiguousarray(np.asarray(L0_attn_W_out, f32)),
        "bdt": np.ascontiguousarray(np.asarray(L0_attn_b_dt, f32).reshape(KI, 128).T),
        "mur": np.ascontiguousarray(np.asarray(L0_ffn_mu_r, f32).reshape(KD, 128).T),
        "muk": np.ascontiguousarray(np.asarray(L0_ffn_mu_k, f32).reshape(KD, 128).T),
        "nw": np.ascontiguousarray(np.asarray(L0_attn_norm_w, f32).reshape(KD, 128).T),
        "nb": np.ascontiguousarray(np.asarray(L0_attn_norm_b, f32).reshape(KD, 128).T),
    }
    alog = np.asarray(L0_attn_A_log, f32)
    in_maps = []
    for c in range(NC):
        b, h = c // 2, c % 2
        xt = np.ascontiguousarray(x[b].T)
        halo = (np.zeros((D, 1), f32) if h == 0
                else np.ascontiguousarray(xt[:, TOK - 1:TOK]))
        m = dict(shared)
        m["xT"] = np.ascontiguousarray(
            np.concatenate([halo, xt[:, h * TOK:(h + 1) * TOK]], axis=1))
        asl = alog[:, h * SH:(h + 1) * SH].reshape(KI, 128, SH)
        m["alog"] = np.ascontiguousarray(
            asl.transpose(1, 0, 2).reshape(128, KI * SH))
        m["hmask"] = np.full((128, 1), float(h), f32)
        in_maps.append(m)

    nc = _get_nc()
    res = run_bass_kernel_spmd(nc, in_maps, core_ids=list(range(NC)))
    out = np.empty((B, T, D), f32)
    for c in range(NC):
        b, h = c // 2, c % 2
        out[b, h * TOK:(h + 1) * TOK, :] = np.asarray(res.results[c]["outT"]).T
    return out
